# version 01
# baseline (speedup 1.0000x reference)
import numpy as np
from contextlib import ExitStack

import concourse.bass as bass
import concourse.mybir as mybir

F32 = mybir.dt.float32
BF16 = mybir.dt.bfloat16
AF = mybir.ActivationFunctionType
ALU = mybir.AluOpType
AX = mybir.AxisListType

ENGINES = ("pe", "act", "dve", "pool", "sp")
BLOCK_ATTR = {"pe": "tensor", "act": "scalar", "dve": "vector",
              "pool": "gpsimd", "sp": "sync"}

HID = 2048
SEQ = 2048
NB = 2


class _Op:
    __slots__ = ("idx", "eng", "track", "fn", "reads", "writes", "tag",
                 "deps", "need_inc", "inc_val", "tpos", "vclock", "is_dma",
                 "waits", "weak", "weak_read")

    def __init__(self):
        self.deps = []
        self.need_inc = False
        self.inc_val = None
        self.vclock = None
        self.weak = False
        self.weak_read = False


class Tin:
    def __init__(self, nc):
        self.nc = nc
        self.ops = []
        self._writers = {}
        self._readers = {}
        self.stats = {}

    def op(self, eng, fn, reads=(), writes=(), tag="", dma=False, weak=False,
           weak_read=False):
        o = _Op()
        o.idx = len(self.ops)
        o.eng = eng
        o.is_dma = dma
        o.weak = weak
        o.weak_read = weak_read
        o.track = ("q" + eng) if dma else eng
        o.fn = fn
        o.reads = tuple(reads)
        o.writes = tuple(writes)
        o.tag = tag
        deps = set()
        for r in o.reads:
            deps.update(self._writers.get(r, ()))
        for r in o.writes:
            deps.update(self._writers.get(r, ()))
            deps.update(self._readers.get(r, ()))
        deps.discard(o)
        o.deps = [d for d in deps
                  if d.weak or d.is_dma or o.is_dma or o.weak_read
                  or d.track != o.track]
        for r in o.writes:
            self._writers[r] = [o]
            self._readers[r] = []
        for r in o.reads:
            self._readers.setdefault(r, []).append(o)
        self.ops.append(o)
        return o

    def dma(self, eng, fn, reads=(), writes=(), tag=""):
        return self.op(eng, fn, reads, writes, tag, dma=True)

    def barrier(self, tag="bar"):
        last = {}
        for o in self.ops:
            last[o.track] = o
        center = self.op("sp", lambda e: e.nop(), tag=tag + "_c")
        center.deps = [d for d in last.values() if d is not center]
        for eng in ("pe", "act", "dve", "pool"):
            o = self.op(eng, lambda e: e.nop(), tag=tag + "_" + eng)
            o.deps = [center]

    def finalize_on(self, eng_name="sp"):
        last = {}
        tcount = {}
        for o in self.ops:
            if o.is_dma:
                pos = tcount.get(o.track, 0)
                tcount[o.track] = pos + 1
                last[(o.track, pos % self.R_DMA)] = o
            else:
                last[o.track] = o
        deps = [o for k, o in last.items() if k != eng_name]
        o = self.op(eng_name, lambda eng: eng.nop(), tag="final_fence")
        o.deps = deps
        return o

    R_DMA = 12

    def emit(self, verbose=False):
        nc = self.nc
        tracks = {}
        for o in self.ops:
            lst = tracks.setdefault(o.track, [])
            o.tpos = len(lst) + 1
            lst.append(o)
        for o in self.ops:
            if o.is_dma:
                o.need_inc = True
            for d in o.deps:
                d.need_inc = True
        inc_count = {t: 0 for t in tracks}
        for o in self.ops:
            if not o.need_inc:
                continue
            if o.is_dma:
                sid = (o.tpos - 1) % self.R_DMA
                use = (o.tpos - 1) // self.R_DMA + 1
                o.inc_val = (sid, use)
            else:
                inc_count[o.track] += 1
                o.inc_val = inc_count[o.track]
        SEG = 50000
        n_seg = {t: (inc_count[t] - 1) // SEG + 1 for t in tracks
                 if not t.startswith("q")}
        self._SEG = SEG

        def dep_key_val(d):
            if d.weak:
                return ("w", d.idx), 1
            if d.is_dma:
                sid, use = d.inc_val
                return (d.track, sid), use
            return d.track, d.tpos

        last_in_eng = {}
        n_waits = 0
        for o in self.ops:
            prev = last_in_eng.get(o.eng)
            vc = dict(prev.vclock) if prev is not None else {}
            if prev is not None and not prev.is_dma:
                vc[prev.track] = max(vc.get(prev.track, 0), prev.tpos)
            waits = []
            deps = sorted(o.deps, key=lambda d: -d.idx)
            if o.is_dma:
                sid, use = (o.tpos - 1) % self.R_DMA, (o.tpos - 1) // self.R_DMA + 1
                if use > 1 and vc.get((o.track, sid), 0) < use - 1:
                    deps.append(tracks[o.track][o.tpos - 1 - self.R_DMA])
            for d in deps:
                key, val = dep_key_val(d)
                same_track_forced = (o.weak_read and not d.is_dma
                                     and not d.weak and d.track == o.track)
                if not same_track_forced and vc.get(key, 0) >= val:
                    continue
                waits.append(d)
                for k, v in d.vclock.items():
                    if vc.get(k, 0) < v:
                        vc[k] = v
                vc[key] = max(vc.get(key, 0), val)
                if d.weak:
                    vc[d.track] = max(vc.get(d.track, 0), d.tpos)
            o.vclock = vc
            o.waits = waits
            n_waits += len(waits)
            last_in_eng[o.eng] = o
        sems = {}
        for t in tracks:
            if t.startswith("q"):
                n_used = min(self.R_DMA, len(tracks[t]))
                for sid in range(n_used):
                    sems[(t, sid)] = nc.alloc_semaphore(name=f"s_{t}_{sid}")
            else:
                for sg in range(n_seg.get(t, 1)):
                    sems[(t, sg)] = nc.alloc_semaphore(name=f"s_{t}_{sg}")
        self.stats = dict(n_ops=len(self.ops), n_waits=n_waits,
                          n_incs=sum(1 for o in self.ops if o.need_inc),
                          per_track={t: len(v) for t, v in tracks.items()},
                          max_inc={t: inc_count[t] for t in tracks
                                   if not t.startswith("q")})
        if verbose:
            print("tin stats:", self.stats)
        per_engine = {e: [] for e in ENGINES}
        for o in self.ops:
            per_engine[o.eng].append(o)
        with nc.Block() as block:
            for e in ENGINES:
                lst = per_engine[e]
                if not lst:
                    continue

                SEG = self._SEG

                def body(eng, lst=lst):
                    for o in lst:
                        for d in o.waits:
                            if d.is_dma:
                                sid, use = d.inc_val
                                eng.wait_ge(sems[(d.track, sid)], use * 16)
                            else:
                                sg = (d.inc_val - 1) // SEG
                                eng.wait_ge(sems[(d.track, sg)],
                                            d.inc_val - sg * SEG)
                        inst = o.fn(eng)
                        if o.need_inc:
                            if inst is None:
                                raise RuntimeError(f"op {o.tag or o.idx} needs "
                                                   f"inc but returned None")
                            if o.is_dma:
                                sid, use = o.inc_val
                                inst.then_inc(sems[(o.track, sid)], 16)
                            else:
                                sg = (o.inc_val - 1) // SEG
                                inst.then_inc(sems[(o.track, sg)], 1)

                getattr(block, BLOCK_ATTR[e])(body)
        return self


N_CTRL = 130156
N_SAMP = 8192 * 2048
N_QB = N_SAMP // 128
N_QT = N_QB // 128
RC_G = 4
RC_GSAMP = RC_G * 16384


def rc_groups(j0, j1):
    return range(j0 // RC_GSAMP, (j1 + RC_GSAMP - 1) // RC_GSAMP)


def spline_tables():
    n = N_SAMP
    t = np.linspace(0.0, N_CTRL - 1.0, n, dtype=np.float64)
    i = np.clip(np.floor(t).astype(np.int64), 0, N_CTRL - 2)
    f = (t - i).astype(np.float32).astype(np.float64)
    f2 = f * f
    f3 = f2 * f
    c = np.stack([
        0.5 * (-f3 + 2.0 * f2 - f),
        0.5 * (3.0 * f3 - 5.0 * f2 + 2.0),
        0.5 * (-3.0 * f3 + 4.0 * f2 + f),
        0.5 * (f3 - f2),
    ], axis=0)
    q = np.arange(n) // 128
    bq = i[::128].copy()
    off = (i - bq[q])
    m = off[None, :] + np.arange(4)[:, None]
    gamma = np.zeros((5, n), dtype=np.float64)
    np.put_along_axis(
        gamma.reshape(5, n), m, c, axis=0)
    gamma = gamma.astype(np.float32).reshape(5, N_QB, 128).transpose(1, 0, 2)
    return np.ascontiguousarray(gamma), bq


def spline_cpw(cp, bq):
    idx = np.clip(bq[:, None] - 1 + np.arange(5)[None, :], 0, N_CTRL - 1)
    return np.ascontiguousarray(cp[idx])


def emit_recon(nc, t, es, gamma_d, cpw_d, w3_d, eng_split=("dve",)):
    G = RC_G
    NG = N_QT // G
    gbuf = [es.enter_context(nc.sbuf_tensor(f"rc_g{i}", [128, 5 * 128 * G], F32))
            for i in range(2)]
    cbuf = [es.enter_context(nc.sbuf_tensor(f"rc_c{i}", [128, 15 * G], F32))
            for i in range(2)]
    acc = es.enter_context(nc.sbuf_tensor("rc_acc", [128, 128], F32))
    wout = [es.enter_context(nc.sbuf_tensor(f"rc_w{i}", [128, 3 * 128 * G], BF16))
            for i in range(2)]
    for g in range(NG):
        pb = g % 2
        gb, cb, wo = gbuf[pb], cbuf[pb], wout[pb]
        gap = bass.AP(gamma_d.tensor, g * G * 128 * 5 * 128,
                      [[5 * 128, 128], [128 * 5 * 128, G], [1, 5 * 128]])
        cap = bass.AP(cpw_d.tensor, g * G * 128 * 15,
                      [[15, 128], [128 * 15, G], [1, 15]])
        t.dma("pool", (lambda e, gb=gb, gap=gap: e.dma_start(gb[:], gap)),
              writes=[("rc_g", pb)], tag=f"rc_ldg{g}")
        t.dma("pool", (lambda e, cb=cb, cap=cap: e.dma_start(cb[:], cap)),
              writes=[("rc_c", pb)], tag=f"rc_ldc{g}")
        for ti in range(G):
            for w in range(3):
                for m in range(5):
                    gsl = gb[:, ti * 640 + m * 128: ti * 640 + (m + 1) * 128]
                    csl = cb[:, ti * 15 + w * 5 + m: ti * 15 + w * 5 + m + 1]
                    if m == 0:
                        t.op("dve", (lambda e, gsl=gsl, csl=csl:
                                     e.tensor_scalar(acc[:], gsl, csl, None,
                                                     ALU.mult)),
                             reads=[("rc_g", pb), ("rc_c", pb)],
                             writes=["rc_acc"], tag=f"rc_m0_{g}_{ti}_{w}")
                    elif m < 4:
                        t.op("dve", (lambda e, gsl=gsl, csl=csl:
                                     e.scalar_tensor_tensor(acc[:], gsl, csl,
                                                            acc[:], ALU.mult,
                                                            ALU.add)),
                             reads=[("rc_g", pb), ("rc_c", pb), "rc_acc"],
                             writes=["rc_acc"], tag=f"rc_m{m}_{g}_{ti}_{w}")
                    else:
                        osl = wo[:, (w * G + ti) * 128:(w * G + ti + 1) * 128]
                        t.op("dve", (lambda e, gsl=gsl, csl=csl, osl=osl:
                                     e.scalar_tensor_tensor(osl, gsl, csl,
                                                            acc[:], ALU.mult,
                                                            ALU.add)),
                             reads=[("rc_g", pb), ("rc_c", pb), "rc_acc"],
                             writes=[("rc_w", pb)], tag=f"rc_m4_{g}_{ti}_{w}")
        for w in range(3):
            oap = bass.AP(w3_d.tensor, w * N_SAMP + g * G * 16384,
                          [[128, 128], [16384, G], [1, 128]])
            t.dma("pool", (lambda e, wo=wo, w=w, oap=oap:
                           e.dma_start(oap, wo[:, w * G * 128:(w + 1) * G * 128])),
                  reads=[("rc_w", pb)], writes=[("rc_w3", w, g)],
                  tag=f"rc_st{g}_{w}")


def emit_glu(nc, t, es, tag, b, hT, hT_key, wsrc, inter_d, out_tile_cb, psum,
             extra_gate_mm=None, phase="both"):
    if phase in ("both", "gu"):
        _emit_glu_gu(nc, t, es, tag, b, hT, hT_key, wsrc, inter_d, psum,
                     extra_gate_mm)
    if phase in ("both", "down"):
        _emit_glu_down(nc, t, es, tag, b, wsrc, inter_d, out_tile_cb, psum)


def _emit_glu_gu(nc, t, es, tag, b, hT, hT_key, wsrc, inter_d, psum,
                 extra_gate_mm):
    wg = es.enter_context(nc.sbuf_tensor(f"{tag}_wg{b}", [128, 16 * 512], BF16))
    wu = es.enter_context(nc.sbuf_tensor(f"{tag}_wu{b}", [128, 16 * 512], BF16))
    gsil = [es.enter_context(nc.sbuf_tensor(f"{tag}_gs{b}_{i}", [128, 512], BF16))
            for i in range(2)]
    isb = [es.enter_context(nc.sbuf_tensor(f"{tag}_i{b}_{i}", [128, 512], BF16))
           for i in range(2)]
    for ic in range(16):
        for hc in range(16):
            for w, buf, bkey in ((0, wg, f"{tag}_wg"), (1, wu, f"{tag}_wu")):
                dst = buf[:, hc * 512:(hc + 1) * 512]
                wsrc["emit_load_gu"](w, ic, hc, dst, (bkey, hc))
        for it in range(4):
            for sc in range(4):
                k = (it * 4 + sc) % 2
                (pg, pgk), (pu, puk) = psum[2 * k], psum[2 * k + 1]
                for w, buf, ps, psk in ((0, wg, pg, pgk), (1, wu, pu, puk)):
                    hold = extra_gate_mm is not None and w == 0
                    for hc in range(16):
                        lhsT = buf[:, hc * 512 + it * 128: hc * 512 + (it + 1) * 128]
                        rhs = hT[:, hc * 2048 + sc * 512: hc * 2048 + (sc + 1) * 512]
                        t.op("pe", (lambda e, ps=ps, lhsT=lhsT, rhs=rhs,
                                    st_=(hc == 0),
                                    sp=(hc == 15 and not hold):
                                    e.matmul(ps[:], lhsT, rhs, start=st_, stop=sp)),
                             reads=[(f"{tag}_wg" if w == 0 else f"{tag}_wu", hc),
                                    hT_key],
                             writes=[psk],
                             tag=f"{tag}mm{w}_{ic}_{it}_{sc}_{hc}")
                    if hold:
                        extra_gate_mm(ic, it, sc, pg, pgk)
                g, i_ = gsil[k], isb[k]
                t.op("act", (lambda e, g=g, pg=pg: e.activation(
                    g[:], pg[:], AF.Silu, bias=0.0, scale=1.0)),
                     reads=[pgk], writes=[(f"{tag}_gs", k)],
                     tag=f"{tag}silu{ic}_{it}_{sc}")
                t.op("dve", (lambda e, i_=i_, pu=pu, g=g: e.tensor_tensor(
                    i_[:], pu[:], g[:], ALU.mult)),
                     reads=[puk, (f"{tag}_gs", k)],
                     writes=[(f"{tag}_i", k)], tag=f"{tag}mul{ic}_{it}_{sc}")
                rt = ic * 4 + it
                dst = bass.AP(inter_d.tensor, (rt * 128) * 2048 + sc * 512,
                              [[2048, 128], [1, 512]])
                t.dma("pool", (lambda e, dst=dst, i_=i_: e.dma_start(dst, i_[:])),
                      reads=[(f"{tag}_i", k)], writes=[("int", rt, sc)],
                      tag=f"{tag}ist{ic}_{it}_{sc}")


def _emit_glu_down(nc, t, es, tag, b, wsrc, inter_d, out_tile_cb, psum):
    wd = es.enter_context(nc.sbuf_tensor(f"{tag}_wd{b}", [128, 64 * 256], BF16))
    itb = [es.enter_context(nc.sbuf_tensor(f"{tag}_it{b}_{i}", [128, 64 * 128], BF16))
           for i in range(2)]
    osb = [es.enter_context(nc.sbuf_tensor(f"{tag}_o{b}_{i}", [128, 256], F32))
           for i in range(2)]
    for dc in range(8):
        for it in range(64):
            dst = wd[:, it * 256:(it + 1) * 256]
            wsrc["emit_load_dn"](dc, it, dst, (f"{tag}_wd", it))
        for st in range(16):
            kb = st % 2
            ib = itb[kb]
            lap = bass.AP(inter_d.tensor, st * 128,
                          [[2048, 128], [128 * 2048, 64], [1, 128]])
            t.dma("pool", (lambda e, ib=ib, lap=lap: e.dma_start(ib[:], lap)),
                  reads=[("int", rt, st // 4) for rt in range(64)],
                  writes=[(f"{tag}_it", kb)], tag=f"{tag}itl{dc}_{st}")
            ps, psk = psum[4 + st % 2]
            for it in range(64):
                lhsT = ib[:, it * 128:(it + 1) * 128]
                rhs = wd[:, it * 256:(it + 1) * 256]
                t.op("pe", (lambda e, ps=ps, lhsT=lhsT, rhs=rhs,
                            s0=(it == 0), s1=(it == 63):
                            e.matmul(ps[:, 0:256], lhsT, rhs, start=s0, stop=s1)),
                     reads=[(f"{tag}_it", kb), (f"{tag}_wd", it)],
                     writes=[psk], tag=f"{tag}dmm{dc}_{st}_{it}")
            ob = osb[st % 2]
            t.op("dve", (lambda e, ob=ob, ps=ps: e.tensor_copy(ob[:], ps[:, 0:256])),
                 reads=[psk], writes=[(f"{tag}_o", st % 2)],
                 tag=f"{tag}dev{dc}_{st}")
            out_tile_cb(st, dc, ob, [(f"{tag}_o", st % 2)])


def wsrc_spline(t, w3_d, w3_key):
    def emit_load_gu(w, ic, hc, dst, dkey):
        src = bass.AP(w3_d.tensor, w * N_SAMP + (ic * 512) * 2048 + hc * 128,
                      [[2048, 512], [1, 128]])
        j0 = (ic * 512) * 2048
        rd = [w3_key(w, g) for g in rc_groups(j0, j0 + 512 * 2048)]
        t.dma("sp", (lambda e, dst=dst, src=src:
                     e.dma_start(dst, src, transpose=True)),
              reads=rd, writes=[dkey], tag=f"sW{w}_{ic}_{hc}")

    def emit_load_dn(dc, it, dst, dkey):
        src = bass.AP(w3_d.tensor, 2 * N_SAMP + (dc * 256) * 8192 + it * 128,
                      [[8192, 256], [1, 128]])
        rd = [w3_key(2, g) for g in rc_groups((dc * 256) * 8192,
                                              (dc * 256 + 256) * 8192)]
        t.dma("sp", (lambda e, dst=dst, src=src:
                     e.dma_start(dst, src, transpose=True)),
              reads=rd, writes=[dkey], tag=f"sWd{dc}_{it}")
    return dict(emit_load_gu=emit_load_gu, emit_load_dn=emit_load_dn)


def wsrc_direct(t, win_d, wup_d, wdn_d):
    def emit_load_gu(w, ic, hc, dst, dkey):
        wd = win_d if w == 0 else wup_d
        src = bass.AP(wd.tensor, (hc * 128) * 8192 + ic * 512,
                      [[8192, 128], [1, 512]])
        t.dma("pool", (lambda e, dst=dst, src=src: e.dma_start(dst, src)),
              writes=[dkey], tag=f"dW{w}_{ic}_{hc}")

    def emit_load_dn(dc, it, dst, dkey):
        src = bass.AP(wdn_d.tensor, (it * 128) * 2048 + dc * 256,
                      [[2048, 128], [1, 256]])
        t.dma("pool", (lambda e, dst=dst, src=src: e.dma_start(dst, src)),
              writes=[dkey], tag=f"dWd{dc}_{it}")
    return dict(emit_load_gu=emit_load_gu, emit_load_dn=emit_load_dn)


SQRT_HD_INV = 0.08838834764831845


def emit_attention(nc, t, es, tagp, b, hT, hT_key, n_q, n_kv, use_rope,
                   wq_d, wk_d, wv_d, attnT, attnT_key, C, psc, psc_keys,
                   pmisc, pkey, heads=None, cache=None):
    G = n_q // n_kv
    if heads is None:
        heads = list(range(n_q))
    h0 = heads[0]
    kvs = sorted({h // G for h in heads})
    if cache is not None and cache:
        (wbuf, qraw, qt_sb, kt_sb, v_sb, probs, pdum, pt_sb, tr,
         cols) = cache["bufs"]
    else:
        wbuf = [es.enter_context(nc.sbuf_tensor(f"{tagp}_w{i}", [128, 2048], BF16))
                for i in range(2)]
        qraw = (es.enter_context(nc.sbuf_tensor(f"{tagp}_qraw", [128, 2048], BF16))
                if use_rope else None)
        qt_sb = es.enter_context(nc.sbuf_tensor(f"{tagp}_qt", [128, 2048], BF16))
        kt_sb = es.enter_context(nc.sbuf_tensor(f"{tagp}_kt", [128, 2048], BF16))
        v_sb = es.enter_context(nc.sbuf_tensor(f"{tagp}_v", [128, 2048], BF16))
        probs = es.enter_context(nc.sbuf_tensor(f"{tagp}_pr", [128, 2048], BF16))
        pdum = es.enter_context(nc.sbuf_tensor(f"{tagp}_pd", [128, 128], BF16))
        pt_sb = es.enter_context(nc.sbuf_tensor(f"{tagp}_ptb", [128, 2048], BF16))
        tr = (es.enter_context(nc.sbuf_tensor(f"{tagp}_tr", [128, 1024], F32))
              if use_rope else None)
        cols = es.enter_context(nc.sbuf_tensor(f"{tagp}_cols", [128, 8], F32))
        if cache is not None:
            cache["bufs"] = (wbuf, qraw, qt_sb, kt_sb, v_sb, probs, pdum,
                             pt_sb, tr, cols)
    K_QR, K_QT, K_KT, K_V, K_PR, K_PD, K_PT, K_TR, K_CO = (
        (tagp, x) for x in ("qraw", "qt", "kt", "v", "pr", "pd", "pt", "tr", "co"))
    t.dbg_tiles = dict(qt=qt_sb, kt=kt_sb, v=v_sb, pr=probs, cols=cols)

    def load_w(wd, head, buf, wkey, tag):
        n_tot = wd.ap[0][0]
        ap = bass.AP(wd.tensor, head * 128,
                     [[n_tot, 128], [128 * n_tot, 16], [1, 128]])
        t.dma("pool", (lambda e, buf=buf, ap=ap: e.dma_start(buf[:], ap)),
              writes=[wkey], tag=tag)

    def proj_T(buf, wkey, dst, dkey, scale, do_rope, tg):
        for c in range(4):
            for hc in range(16):
                t.op("pe", (lambda e, buf=buf, hc=hc, c=c,
                            s0=(hc == 0), s1=(hc == 15):
                            e.matmul(pmisc["proj"][:],
                                     buf[:, hc * 128:(hc + 1) * 128],
                                     hT[:, hc * 2048 + c * 512:
                                        hc * 2048 + (c + 1) * 512],
                                     start=s0, stop=s1)),
                     reads=[wkey, hT_key], writes=[pkey("proj")],
                     tag=f"{tg}pj{c}_{hc}")
            if not do_rope:
                t.op("act", (lambda e, c=c, dst=dst, scale=scale:
                             e.activation(dst[:, c * 512:(c + 1) * 512],
                                          pmisc["proj"][:], AF.Copy,
                                          bias=0.0, scale=scale)),
                     reads=[pkey("proj")], writes=[dkey], tag=f"{tg}ev{c}")
            else:
                t.op("act", (lambda e, c=c, scale=scale:
                             e.activation(qraw[:, c * 512:(c + 1) * 512],
                                          pmisc["proj"][:], AF.Copy,
                                          bias=0.0, scale=scale)),
                     reads=[pkey("proj")], writes=[K_QR], tag=f"{tg}ev{c}")
                t.op("pe", (lambda e, c=c: e.matmul(
                    pmisc["rope"][:], C["pneg"][:],
                    qraw[:, c * 512:(c + 1) * 512], start=True, stop=True)),
                     reads=[K_QR, "c_pneg"], writes=[pkey("rope")],
                     tag=f"{tg}rp{c}")
                t.op("dve", (lambda e, c=c: e.tensor_tensor(
                    tr[:, 0:512], pmisc["rope"][:],
                    C["sinT"][:, c * 512:(c + 1) * 512], ALU.mult)),
                     reads=[pkey("rope"), "c_sinT"], writes=[K_TR],
                     tag=f"{tg}rs{c}")
                t.op("dve", (lambda e, c=c: e.tensor_tensor(
                    tr[:, 512:1024], qraw[:, c * 512:(c + 1) * 512],
                    C["cosT"][:, c * 512:(c + 1) * 512], ALU.mult)),
                     reads=[K_QR, "c_cosT"], writes=[K_TR], tag=f"{tg}rc{c}")
                t.op("dve", (lambda e, c=c, dst=dst: e.tensor_tensor(
                    dst[:, c * 512:(c + 1) * 512], tr[:, 0:512],
                    tr[:, 512:1024], ALU.add)),
                     reads=[K_TR], writes=[dkey], tag=f"{tg}ra{c}")

    for kvh in kvs:
        wb = wbuf[0]
        load_w(wk_d, kvh, wb, (tagp, "w", 0), f"{tagp}lwk{kvh}")
        proj_T(wb, (tagp, "w", 0), kt_sb, K_KT, 1.0, use_rope,
               f"{tagp}K{kvh}")
        wbv = wbuf[1]
        load_w(wv_d, kvh, wbv, (tagp, "w", 1), f"{tagp}lwv{kvh}")
        for st in range(16):
            for hc in range(16):
                t.op("pe", (lambda e, st=st, hc=hc, wbv=wbv,
                            s0=(hc == 0), s1=(hc == 15):
                            e.matmul(pmisc["pv"][:, 0:128],
                                     hT[:, hc * 2048 + st * 128:
                                        hc * 2048 + (st + 1) * 128],
                                     wbv[:, hc * 128:(hc + 1) * 128],
                                     start=s0, stop=s1)),
                     reads=[(tagp, "w", 1), hT_key], writes=[pkey("pv")],
                     tag=f"{tagp}v{kvh}_{st}_{hc}")
            t.op("act", (lambda e, st=st: e.activation(
                v_sb[:, st * 128:(st + 1) * 128], pmisc["pv"][:, 0:128],
                AF.Copy, bias=0.0, scale=1.0)),
                 reads=[pkey("pv")], writes=[K_V], tag=f"{tagp}vev{kvh}_{st}")
        for hq in [h for h in heads if h // G == kvh]:
            wbq = wbuf[0]
            load_w(wq_d, hq, wbq, (tagp, "w", 0), f"{tagp}lwq{hq}")
            proj_T(wbq, (tagp, "w", 0), qt_sb, K_QT, SQRT_HD_INV, use_rope,
                   f"{tagp}Q{hq}")
            for qt in range(16):
                kspan = (qt + 1) * 128
                nch = (kspan + 511) // 512
                for kc in range(nch):
                    n = min(512, kspan - kc * 512)
                    t.op("pe", (lambda e, qt=qt, kc=kc, n=n:
                                e.matmul(psc[:, kc * 512:kc * 512 + n],
                                         qt_sb[:, qt * 128:(qt + 1) * 128],
                                         kt_sb[:, kc * 512:kc * 512 + n],
                                         start=True, stop=True)),
                         reads=[K_QT, K_KT], writes=[psc_keys[kc]],
                         tag=f"{tagp}sc{hq}_{qt}_{kc}")
                t.op("dve", (lambda e, kspan=kspan: e.reduce_max(
                    cols[:, 0:1], psc[:, 0:kspan], axis=AX.X, negate=True)),
                     reads=psc_keys[:nch], writes=[K_CO], tag=f"{tagp}mx{hq}_{qt}")
                if qt > 0:
                    t.op("act", (lambda e, qt=qt: e.activation(
                        probs[:, 0:qt * 128], psc[:, 0:qt * 128], AF.Exp,
                        bias=cols[:, 0:1], scale=1.0,
                        accum_out=cols[:, 1:2])),
                         reads=psc_keys[:nch] + [K_CO], writes=[K_PR, K_CO],
                         tag=f"{tagp}e1{hq}_{qt}")

                t.op("act", (lambda e, qt=qt: e.activation(
                    pdum[:], psc[:, qt * 128:(qt + 1) * 128], AF.Exp,
                    bias=cols[:, 0:1], scale=1.0)),
                     reads=[psc_keys[qt // 4], K_CO], writes=[K_PD],
                     tag=f"{tagp}e2{hq}_{qt}")
                t.op("dve", (lambda e, qt=qt: e.tensor_tensor(
                    probs[:, qt * 128:(qt + 1) * 128], pdum[:],
                    C["tril01"][:], ALU.mult)),
                     reads=[K_PD, "c_tril01"], writes=[K_PR],
                     tag=f"{tagp}tm{hq}_{qt}")
                t.op("act", (lambda e, qt=qt: e.activation(
                    pdum[:], probs[:, qt * 128:(qt + 1) * 128], AF.Copy,
                    bias=0.0, scale=1.0, accum_out=cols[:, 5:6])),
                     reads=[K_PR], writes=[K_PD, K_CO],
                     tag=f"{tagp}s2{hq}_{qt}")
                if qt > 0:
                    t.op("act", (lambda e: e.activation(
                        cols[:, 2:3], cols[:, 5:6], AF.Identity,
                        bias=cols[:, 1:2], scale=1.0)),
                         reads=[K_CO], writes=[K_CO], tag=f"{tagp}sa{hq}_{qt}")
                else:
                    t.op("act", (lambda e: e.activation(
                        cols[:, 2:3], cols[:, 5:6], AF.Copy,
                        bias=0.0, scale=1.0)),
                         reads=[K_CO], writes=[K_CO], tag=f"{tagp}sa{hq}_{qt}")
                t.op("dve", lambda e: e.reciprocal(cols[:, 3:4], cols[:, 2:3]),
                     reads=[K_CO], writes=[K_CO], tag=f"{tagp}rc{hq}_{qt}",
                     weak_read=True, weak=True)
                t.op("dve", (lambda e, kspan=kspan: e.tensor_scalar(
                    probs[:, 0:kspan], probs[:, 0:kspan], cols[:, 3:4], None,
                    ALU.mult)),
                     reads=[K_PR, K_CO], writes=[K_PR],
                     tag=f"{tagp}nm{hq}_{qt}")
                for kt4 in range((qt + 1 + 3) // 4):
                    nblk = min(4, qt + 1 - kt4 * 4)
                    for j in range(nblk):
                        kt = kt4 * 4 + j
                        t.op("pe", (lambda e, kt=kt, j=j: e.matmul(
                            pmisc["pt"][:, j * 128:(j + 1) * 128],
                            probs[:, kt * 128:(kt + 1) * 128], C["ident"][:],
                            start=True, stop=True)),
                             reads=[K_PR, "c_ident"], writes=[pkey("pt")],
                             tag=f"{tagp}pt{hq}_{qt}_{kt}")
                    t.op("dve", (lambda e, kt4=kt4, nblk=nblk: e.tensor_copy(
                        pt_sb[:, kt4 * 512:kt4 * 512 + nblk * 128],
                        pmisc["pt"][:, 0:nblk * 128])),
                         reads=[pkey("pt")], writes=[K_PT],
                         tag=f"{tagp}ptev{hq}_{qt}_{kt4}")
                for kt in range(qt + 1):
                    t.op("pe", (lambda e, kt=kt, s0=(kt == 0), s1=(kt == qt):
                                e.matmul(pmisc["pv"][:, 0:128],
                                         v_sb[:, kt * 128:(kt + 1) * 128],
                                         pt_sb[:, kt * 128:(kt + 1) * 128],
                                         start=s0, stop=s1)),
                         reads=[K_V, K_PT], writes=[pkey("pv")],
                         tag=f"{tagp}pv{hq}_{qt}_{kt}")
                slot = hq - h0
                t.op("act", (lambda e, slot=slot, qt=qt: e.activation(
                    attnT[:, slot * 2048 + qt * 128:slot * 2048 + (qt + 1) * 128],
                    pmisc["pv"][:, 0:128], AF.Copy, bias=0.0, scale=1.0)),
                     reads=[pkey("pv")], writes=[attnT_key],
                     tag=f"{tagp}oev{hq}_{qt}")


def emit_outproj(nc, t, es, tagp, attnT, attnT_key, wo_d, out_cb, pmisc, pkey,
                 heads, bufs=None):
    if bufs is None:
        wob = es.enter_context(nc.sbuf_tensor(f"{tagp}_wo", [128, 8 * 512], BF16))
        osb = [es.enter_context(nc.sbuf_tensor(f"{tagp}_os{i}", [128, 512], F32))
               for i in range(2)]
        bufs = (wob, osb)
    wob, osb = bufs
    nh = len(heads)
    first = heads[0] == 0
    for dc in range(4):
        ap = bass.AP(wo_d.tensor, heads[0] * 128 * 2048 + dc * 512,
                     [[2048, 128], [128 * 2048, nh], [1, 512]])
        t.dma("pool", (lambda e, ap=ap: e.dma_start(wob[:, 0:nh * 512], ap)),
              writes=[(tagp, "wo")], tag=f"{tagp}lwo{dc}")
        for st in range(16):
            pk = ("op", st % 2)
            for j in range(nh):
                t.op("pe", (lambda e, j=j, st=st, pk=pk,
                            s0=(j == 0), s1=(j == nh - 1):
                            e.matmul(pmisc[pk][:],
                                     attnT[:, j * 2048 + st * 128:
                                           j * 2048 + (st + 1) * 128],
                                     wob[:, j * 512:(j + 1) * 512],
                                     start=s0, stop=s1)),
                     reads=[attnT_key, (tagp, "wo")],
                     writes=[pkey(pk)], tag=f"{tagp}op{dc}_{j}_{st}")
            ob = osb[st % 2]
            t.op("dve", (lambda e, ob=ob, pk=pk: e.tensor_copy(
                ob[:], pmisc[pk][:])),
                 reads=[pkey(pk)], writes=[(tagp, "os", st % 2)],
                 tag=f"{tagp}opev{dc}_{st}")
            out_cb(st, dc, ob, [(tagp, "os", st % 2)], first)
    return bufs


def emit_rmsnorm_pair(nc, t, es, b, xnT, xnT_key, ln1_col, ln2_col, h1T,
                      h1T_key, psc, psc_keys, pb, pb_key, C, eps=1e-6):
    sq = es.enter_context(nc.sbuf_tensor(f"rn_sq{b}", [128, 2048], BF16))
    rrow = es.enter_context(nc.sbuf_tensor(f"rn_r{b}", [1, 2048], F32))
    rbc = es.enter_context(nc.sbuf_tensor(f"rn_b{b}", [128, 2048], F32))
    K_SQ, K_RR, K_RB = (f"rn_sq{b}", f"rn_r{b}", f"rn_b{b}")
    for hc in range(16):
        t.op("act", (lambda e, hc=hc: e.activation(
            sq[:], xnT[:, hc * 2048:(hc + 1) * 2048], AF.Square, bias=0.0,
            scale=1.0)),
             reads=[xnT_key], writes=[K_SQ], tag=f"rnsq{b}_{hc}")
        for sc in range(4):
            t.op("pe", (lambda e, hc=hc, sc=sc, s0=(hc == 0), s1=(hc == 15):
                        e.matmul(psc[0:1, sc * 512:(sc + 1) * 512],
                                 C["ones_col"][:], sq[:, sc * 512:(sc + 1) * 512],
                                 start=s0, stop=s1)),
                 reads=[K_SQ, "c_ones_col"], writes=[psc_keys[sc]],
                 tag=f"rnmm{b}_{hc}_{sc}")
    t.op("act", (lambda e: e.activation(rrow[:], psc[0:1, 0:2048], AF.Ln,
                                        bias=C["fc"][0:1, 0:1],
                                        scale=1.0 / 2048.0)),
         reads=list(psc_keys) + ["c_fc"], writes=[K_RR], tag=f"rnln{b}")
    t.op("act", (lambda e: e.activation(rrow[:], rrow[:], AF.Exp, bias=0.0,
                                        scale=-0.5)),
         reads=[K_RR], writes=[K_RR], tag=f"rnex{b}")
    for sc in range(4):
        t.op("pe", (lambda e, sc=sc: e.matmul(
            pb[:], C["ones1f"][:], rrow[0:1, sc * 512:(sc + 1) * 512],
            start=True, stop=True)),
             reads=[K_RR, "c_ones1f"], writes=[pb_key], tag=f"rnbc{b}_{sc}")
        t.op("dve", (lambda e, sc=sc: e.tensor_copy(
            rbc[:, sc * 512:(sc + 1) * 512], pb[:])),
             reads=[pb_key], writes=[K_RB], tag=f"rnbe{b}_{sc}")
    for hc in range(16):
        t.op("dve", (lambda e, hc=hc: e.scalar_tensor_tensor(
            h1T[:, hc * 2048:(hc + 1) * 2048],
            xnT[:, hc * 2048:(hc + 1) * 2048], ln1_col[:, hc:hc + 1],
            rbc[:], ALU.mult, ALU.mult)),
             reads=[xnT_key, "c_ln1", K_RB], writes=[h1T_key],
             tag=f"rnh1{b}_{hc}")
    for hc in range(16):
        t.op("dve", (lambda e, hc=hc: e.scalar_tensor_tensor(
            xnT[:, hc * 2048:(hc + 1) * 2048],
            xnT[:, hc * 2048:(hc + 1) * 2048], ln2_col[:, hc:hc + 1],
            rbc[:], ALU.mult, ALU.mult)),
             reads=[xnT_key, "c_ln2", K_RB], writes=[xnT_key],
             tag=f"rnh2{b}_{hc}")


_WOA_SEQ = [0]


def emit_woa(nc, t, es, wo_d, a_d, woa_sb, a_sb, pb, pb_key):
    _WOA_SEQ[0] += 1
    wob = es.enter_context(nc.sbuf_tensor(f"woa_wob{_WOA_SEQ[0]}",
                                          [128, 2048], BF16))
    aap = bass.AP(a_d.tensor, 0, [[16, 128], [128 * 16, 16], [1, 16]])
    t.dma("pool", (lambda e: e.dma_start(a_sb[:], aap)), writes=["c_a_sb"],
          tag="woa_lda")
    for at in range(16):
        ap = bass.AP(wo_d.tensor, at * 128,
                     [[2048, 128], [128 * 2048, 16], [1, 128]])
        t.dma("pool", (lambda e, ap=ap: e.dma_start(wob[:], ap)),
              writes=["woa_wob"], tag=f"woa_ldw{at}")
        for d2t in range(16):
            t.op("pe", (lambda e, d2t=d2t, s0=(d2t == 0), s1=(d2t == 15):
                        e.matmul(pb[:, 0:16], wob[:, d2t * 128:(d2t + 1) * 128],
                                 a_sb[:, d2t * 16:(d2t + 1) * 16],
                                 start=s0, stop=s1)),
                 reads=["woa_wob", "c_a_sb"], writes=[pb_key],
                 tag=f"woamm{at}_{d2t}")
        t.op("dve", (lambda e, at=at: e.tensor_copy(
            woa_sb[:, at * 16:(at + 1) * 16], pb[:, 0:16])),
             reads=[pb_key], writes=["c_woa"], tag=f"woaev{at}")


def emit_attnA_partial(nc, t, es, b, attnT, attnT_key, woa_sb, attA_acc,
                       nh, first, pb, pb_key):
    for st in range(16):
        for j in range(nh):
            t.op("pe", (lambda e, st=st, j=j, s0=(j == 0), s1=(j == nh - 1):
                        e.matmul(pb[:, 0:16],
                                 attnT[:, j * 2048 + st * 128:
                                       j * 2048 + (st + 1) * 128],
                                 woa_sb[:, j * 16:(j + 1) * 16],
                                 start=s0, stop=s1)),
                 reads=[attnT_key, "c_woa"], writes=[pb_key],
                 tag=f"aA{b}_{st}_{j}")
        if first:
            t.op("dve", (lambda e, st=st: e.tensor_copy(
                attA_acc[:, st * 16:(st + 1) * 16], pb[:, 0:16])),
                 reads=[pb_key], writes=["attA_acc"], tag=f"aAe{b}_{st}")
        else:
            t.op("dve", (lambda e, st=st: e.tensor_tensor(
                attA_acc[:, st * 16:(st + 1) * 16], pb[:, 0:16],
                attA_acc[:, st * 16:(st + 1) * 16], ALU.add)),
                 reads=[pb_key, "attA_acc"], writes=["attA_acc"],
                 tag=f"aAa{b}_{st}")


def emit_mod_finish(nc, t, es, b, h2T, h2T_key, attA_acc, a_sb, modT_sb, C,
                    pb, pb_key):
    hA01 = es.enter_context(nc.sbuf_tensor(f"mp_hA{b}", [128, 256], F32))
    tnh = es.enter_context(nc.sbuf_tensor(f"mp_t{b}", [128, 16], F32))
    mha = es.enter_context(nc.sbuf_tensor(f"mp_m{b}", [128, 16], BF16))
    K_HA, K_T, K_M = f"mp_hA{b}", f"mp_t{b}", f"mp_m{b}"
    for st in range(16):
        for hc in range(16):
            t.op("pe", (lambda e, st=st, hc=hc, s0=(hc == 0), s1=(hc == 15):
                        e.matmul(pb[:, 0:16],
                                 h2T[:, hc * 2048 + st * 128:
                                     hc * 2048 + (st + 1) * 128],
                                 a_sb[:, hc * 16:(hc + 1) * 16],
                                 start=s0, stop=s1)),
                 reads=[h2T_key, "c_a_sb"], writes=[pb_key],
                 tag=f"mphA{b}_{st}_{hc}")
        t.op("act", (lambda e, st=st: e.activation(
            hA01[:, st * 16:(st + 1) * 16], pb[:, 0:16], AF.Copy, bias=0.0,
            scale=0.1)),
             reads=[pb_key], writes=[K_HA], tag=f"mphAe{b}_{st}")
    for st in range(16):
        t.op("act", (lambda e, st=st: e.activation(
            tnh[:], attA_acc[:, st * 16:(st + 1) * 16], AF.Tanh, bias=0.0,
            scale=1.0)),
             reads=["attA_acc"], writes=[K_T], tag=f"mptn{b}_{st}")
        t.op("dve", (lambda e, st=st: e.tensor_tensor(
            mha[:], tnh[:], hA01[:, st * 16:(st + 1) * 16], ALU.mult)),
             reads=[K_T, K_HA], writes=[K_M], tag=f"mpm{b}_{st}")
        t.op("pe", (lambda e, st=st: e.matmul(
            pb[0:16, 16:144], mha[:], C["ident"][:], start=True, stop=True)),
             reads=[K_M, "c_ident"], writes=[pb_key], tag=f"mptr{b}_{st}")
        t.op("dve", (lambda e, st=st: e.tensor_copy(
            modT_sb[0:16, st * 128:(st + 1) * 128], pb[0:16, 16:144])),
             reads=[pb_key], writes=["modT"], tag=f"mptre{b}_{st}")


def emit_l1ffn(nc, t, es, b, xnT, xnT_key, attnT, W, C, inter_d,
               out_tile_cb, psum, psc, psc_keys, pmisc, pkey, do_woa):
    h1T_key = f"h1T{b}"
    es1 = ExitStack()
    h1T = es1.enter_context(nc.sbuf_tensor(f"h1T{b}", [128, 16 * 2048], BF16))
    es_rn = ExitStack()
    emit_rmsnorm_pair(nc, t, es_rn, b, xnT, xnT_key, C["ln1"], C["ln2"], h1T,
                      h1T_key, psc, psc_keys, pmisc["rope"], pkey("rope"), C)
    if do_woa:
        emit_woa(nc, t, es_rn, W["Wo"], W["A"], C["woa"], C["a_sb"],
                 pmisc["proj"], pkey("proj"))
    t.barrier(f"l1f_rn{b}")
    es_rn.close()
    attA_acc = es1.enter_context(nc.sbuf_tensor(f"attA{b}", [128, 256], F32))
    at_cache = {}
    for g0 in (0, 8):
        emit_attention(nc, t, es1, f"af{b}", b, h1T, h1T_key, 16, 16, False,
                       W["WqT"], W["WkT"], W["WvT"], attnT, f"afT{b}", C, psc,
                       psc_keys, pmisc, pkey, heads=list(range(g0, g0 + 8)),
                       cache=at_cache)
        emit_attnA_partial(nc, t, es1, b, attnT, f"afT{b}",
                           C["woa"][:, g0 * 16:(g0 + 8) * 16], attA_acc,
                           8, g0 == 0, pmisc["proj"], pkey("proj"))
    modT = C["modT"]
    emit_mod_finish(nc, t, es1, b, xnT, xnT_key, attA_acc, C["a_sb"], modT, C,
                    pmisc["proj"], pkey("proj"))
    t.barrier(f"l1f_at{b}")
    es1.close()

    es2 = ExitStack()
    bsl = es2.enter_context(nc.sbuf_tensor(f"bsl{b}", [16, 512], BF16))

    def extra_gate_mm(ic, it, sc, pg, pgkey):
        if it == 0 and sc == 0:
            ap = bass.AP(W["B"].tensor, ic * 512, [[8192, 16], [1, 512]])
            t.dma("pool", (lambda e, ap=ap: e.dma_start(bsl[:], ap)),
                  writes=["c_bsl"], tag=f"f1bld{ic}")
        t.op("pe", (lambda e, it=it, sc=sc, pg=pg: e.matmul(
            pg[:], bsl[0:16, it * 128:(it + 1) * 128],
            modT[0:16, sc * 512:(sc + 1) * 512], start=False, stop=True)),
             reads=["c_bsl", "modT"], writes=[pgkey],
             tag=f"f1bg{ic}_{it}_{sc}")
    wsrc = wsrc_direct(t, W["W_inT"], W["W_upT"], W["W_downT"])
    emit_glu(nc, t, es2, f"f1_{b}", b, xnT, xnT_key, wsrc, inter_d,
             out_tile_cb, psum, extra_gate_mm=extra_gate_mm)
    t.barrier(f"l1f_glu{b}")
    es2.close()


def emit_mamba(nc, t, es, b, xnT, xnT_key, W, C, S, out_tile_cb, psum8,
               dbg_cb=None, phase="both"):
    CH = 256
    if phase == "m1":
        _emit_mamba_m1(nc, t, es, b, xnT, xnT_key, W, C, S, psum8)
        return
    if phase == "rest":
        _emit_mamba_rest(nc, t, es, b, W, C, S, out_tile_cb, psum8, dbg_cb)
        return
    _emit_mamba_m1(nc, t, es, b, xnT, xnT_key, W, C, S, psum8)
    _emit_mamba_rest(nc, t, es, b, W, C, S, out_tile_cb, psum8, dbg_cb)


def _emit_mamba_m1(nc, t, es, b, xnT, xnT_key, W, C, S, psum8):
    es1 = ExitStack()
    wbuf = [es1.enter_context(nc.sbuf_tensor(f"ma_w{b}_{i}", [128, 2048], BF16))
            for i in range(2)]
    evb = [es1.enter_context(nc.sbuf_tensor(f"ma_ev{b}_{i}", [128, 512], BF16))
           for i in range(2)]
    for ct in range(35):
        ncols = 128 if ct < 34 else 64
        wb = wbuf[ct % 2]
        ap = bass.AP(W["W_inT"].tensor, 4096 + ct * 128,
                     [[8512, 128], [128 * 8512, 16], [1, ncols]])
        t.dma("pool", (lambda e, wb=wb, ap=ap, ncols=ncols:
                       e.dma_start(wb[:, 0:16 * ncols], ap)),
              writes=[("ma_w", b, ct % 2)], tag=f"ma1w{b}_{ct}")
        for sc in range(4):
            ps, psk = psum8[sc % 2]
            for hc in range(16):
                t.op("pe", (lambda e, wb=wb, hc=hc, sc=sc, ps=ps, ncols=ncols,
                            s0=(hc == 0), s1=(hc == 15):
                            e.matmul(ps[0:ncols, :],
                                     wb[:, hc * ncols:(hc + 1) * ncols],
                                     xnT[:, hc * 2048 + sc * 512:
                                         hc * 2048 + (sc + 1) * 512],
                                     start=s0, stop=s1)),
                     reads=[("ma_w", b, ct % 2), xnT_key], writes=[psk],
                     tag=f"ma1m{b}_{ct}_{sc}_{hc}")
            ev = evb[sc % 2]
            t.op("act", (lambda e, ev=ev, ps=ps, ncols=ncols: e.activation(
                ev[0:ncols, :], ps[0:ncols, :], AF.Copy, bias=0.0, scale=1.0)),
                 reads=[psk], writes=[("ma_ev", b, sc % 2)],
                 tag=f"ma1e{b}_{ct}_{sc}")
            dst = bass.AP(S["hbc"].tensor, ct * 128 * 2048 + sc * 512,
                          [[2048, ncols], [1, 512]])
            t.dma("pool", (lambda e, dst=dst, ev=ev, ncols=ncols:
                           e.dma_start(dst, ev[0:ncols, :])),
                  reads=[("ma_ev", b, sc % 2)], writes=[("hbc_raw", ct)],
                  tag=f"ma1s{b}_{ct}_{sc}")
    gwb = [es1.enter_context(nc.sbuf_tensor(f"ma_gw{b}_{i}", [128, 8192], BF16))
           for i in range(2)]
    for gc in range(8):
        wb = gwb[gc % 2]
        ap = bass.AP(W["W_inT"].tensor, gc * 512,
                     [[8512, 128], [128 * 8512, 16], [1, 512]])
        t.dma("pool", (lambda e, wb=wb, ap=ap: e.dma_start(wb[:], ap)),
              writes=[("ma_w", b, gc % 2)], tag=f"ma1gw{b}_{gc}")
        for st in range(16):
            ps, psk = psum8[st % 2]
            for hc in range(16):
                t.op("pe", (lambda e, wb=wb, hc=hc, st=st, ps=ps,
                            s0=(hc == 0), s1=(hc == 15):
                            e.matmul(ps[:],
                                     xnT[:, hc * 2048 + st * 128:
                                         hc * 2048 + (st + 1) * 128],
                                     wb[:, hc * 512:(hc + 1) * 512],
                                     start=s0, stop=s1)),
                     reads=[("ma_w", b, gc % 2), xnT_key], writes=[psk],
                     tag=f"ma1gm{b}_{st}_{gc}_{hc}")
            ev = evb[st % 2]
            t.op("act", (lambda e, ev=ev, ps=ps: e.activation(
                ev[:], ps[:], AF.Silu, bias=0.0, scale=1.0)),
                 reads=[psk], writes=[("ma_ev", b, st % 2)],
                 tag=f"ma1ge{b}_{st}_{gc}")
            dst = bass.AP(S["gate"].tensor, st * 128 * 4096 + gc * 512,
                          [[4096, 128], [1, 512]])
            t.dma("pool", (lambda e, dst=dst, ev=ev: e.dma_start(dst, ev[:])),
                  reads=[("ma_ev", b, st % 2)], writes=[("gate_s", st, gc)],
                  tag=f"ma1gs{b}_{st}_{gc}")
    t.barrier(f"ma1{b}")
    es1.close()


def _emit_mamba_rest(nc, t, es, b, W, C, S, out_tile_cb, psum8, dbg_cb):
    CH = 256
    acol = es.enter_context(nc.sbuf_tensor(f"ma_acol{b}", [64, 1], F32))
    t.op("act", lambda e: e.activation(acol[:], C["alog"][0:64, :], AF.Exp,
                                       bias=0.0, scale=1.0),
         reads=["c_alog"], writes=["ma_acol"], tag=f"ma_a1{b}")
    t.op("dve", lambda e: e.tensor_scalar(acol[:], acol[:], -1.0, None,
                                          ALU.mult),
         reads=["ma_acol"], writes=["ma_acol"], tag=f"ma_a2{b}")
    es2 = ExitStack()
    ub = [es2.enter_context(nc.sbuf_tensor(f"ma_u{b}_{i}", [128, 2048], BF16))
          for i in range(2)]
    ac = [es2.enter_context(nc.sbuf_tensor(f"ma_ac{b}_{i}", [128, 2048], F32))
          for i in range(2)]
    cv = [es2.enter_context(nc.sbuf_tensor(f"ma_cv{b}_{i}", [128, 2048], BF16))
          for i in range(2)]
    for ct in range(34):
        k2 = ct % 2
        u, a, c_ = ub[k2], ac[k2], cv[k2]
        src = bass.AP(S["hbc"].tensor, ct * 128 * 2048, [[2048, 128], [1, 2048]])
        t.dma("pool", (lambda e, u=u, src=src: e.dma_start(u[:], src)),
              reads=[("hbc_raw", ct)], writes=[("ma_u", k2)], tag=f"ma2l{b}_{ct}")
        t.op("dve", (lambda e, u=u, a=a, ct=ct: e.tensor_scalar(
            a[:], u[:], C["wcv"][:, ct * 4 + 3:ct * 4 + 4], None, ALU.mult)),
             reads=[("ma_u", k2), "c_wcv"], writes=[("ma_ac", k2)],
             tag=f"ma2k3{b}_{ct}")
        for k in range(3):
            d = 3 - k
            t.op("dve", (lambda e, u=u, a=a, ct=ct, k=k, d=d:
                         e.scalar_tensor_tensor(
                             a[:, d:2048], u[:, 0:2048 - d],
                             C["wcv"][:, ct * 4 + k:ct * 4 + k + 1],
                             a[:, d:2048], ALU.mult, ALU.add)),
                 reads=[("ma_u", k2), "c_wcv", ("ma_ac", k2)],
                 writes=[("ma_ac", k2)], tag=f"ma2k{k}{b}_{ct}")
        t.op("act", (lambda e, a=a, c_=c_, ct=ct: e.activation(
            c_[:], a[:], AF.Silu, bias=C["cbc"][:, ct:ct + 1], scale=1.0)),
             reads=[("ma_ac", k2), "c_cbc"], writes=[("ma_cv", k2)],
             tag=f"ma2s{b}_{ct}")
        dst = bass.AP(S["hbc"].tensor, ct * 128 * 2048, [[2048, 128], [1, 2048]])
        t.dma("pool", (lambda e, dst=dst, c_=c_: e.dma_start(dst, c_[:])),
              reads=[("ma_cv", k2)], writes=[("hbc_cv", ct), ("hbc_raw", ct)],
              tag=f"ma2st{b}_{ct}")
    t.barrier(f"ma2{b}")
    es2.close()

    es3 = ExitStack()
    sb_ = lambda n, sh, dt: es3.enter_context(nc.sbuf_tensor(n + str(b), sh, dt))
    xs_cm = sb_("ma_xs", [128, 8192], BF16)
    x_tm = sb_("ma_xtm", [128, 8192], BF16)
    xdt = sb_("ma_xdt", [128, 8192], BF16)
    xdte = xs_cm
    ybuf = sb_("ma_y", [128, 8192], BF16)
    gsb = None
    bc_sb = sb_("ma_bc", [128, 512], BF16)
    b_tm = sb_("ma_btm", [128, 256], BF16)
    cbt = sb_("ma_cbt", [128, 512], BF16)
    mbuf = [sb_(f"ma_mT{i}", [128, 512], BF16) for i in range(2)]
    argb = sb_("ma_arg", [128, 256], F32)
    dtraw = sb_("ma_dtr", [64, 256], BF16)
    f64a = sb_("ma_f64a", [64, 256], F32)
    f64b = sb_("ma_f64b", [64, 256], F32)
    dt_cm = sb_("ma_dtc", [64, 256], F32)
    acum = sb_("ma_acu", [64, 256], F32)
    edec_cm = sb_("ma_edc", [64, 256], F32)
    acum_tm = sb_("ma_atm", [128, 128], F32)
    negacum_tm = sb_("ma_natm", [128, 128], F32)
    eacum_tm = sb_("ma_eatm", [128, 128], F32)
    dt_tm = sb_("ma_dttm", [128, 128], F32)
    edec_tm = sb_("ma_edtm", [128, 128], F32)
    cdb = sb_("ma_cdb", [128, 64], F32)
    cdrow = sb_("ma_cdr", [1, 64], F32)
    carry = sb_("ma_carry", [128, 4096], BF16)
    ytmp = sb_("ma_yt1", [128, 512], F32)
    ytmp2 = sb_("ma_yt2", [128, 512], F32)
    trev = sb_("ma_trev", [128, 512], BF16)
    ycol = sb_("ma_ycol", [128, 2], F32)
    KX, KXT, KDT, KDE, KY, KG = [("ma", b, i) for i in range(6)]
    KBC, KBT, KCBT, KAC, KTM, KCD, KCAR = [("ma", b, 6 + i) for i in range(7)]
    KF = ("ma", b, 13)
    pbc, pbc_k = psum8[0]
    pY = [psum8[1], psum8[2], psum8[3], psum8[4]]
    pst, pst_k = psum8[5]
    pcb, pcb_k = psum8[6]
    ptr, ptr_k = psum8[7]
    NCH = SEQ // CH
    for c8 in range(NCH):
        ap = bass.AP(S["hbc"].tensor, c8 * 256,
                     [[2048, 128], [128 * 2048, 32], [1, 256]])
        t.dma("pool", (lambda e, ap=ap: e.dma_start(xs_cm[:], ap)),
              reads=[("hbc_cv", ct) for ct in range(32)], writes=[KX, KDE],
              tag=f"m3lx{b}_{c8}")
        for i, r0 in ((0, 4096), (1, 4224)):
            ap = bass.AP(S["hbc"].tensor, r0 * 2048 + c8 * 256,
                         [[2048, 128], [1, 256]])
            t.dma("pool", (lambda e, ap=ap, i=i: e.dma_start(
                bc_sb[:, i * 256:(i + 1) * 256], ap)),
                  reads=[("hbc_cv", 32 + i)], writes=[KBC], tag=f"m3lbc{b}_{c8}_{i}")
        ap = bass.AP(S["hbc"].tensor, 4352 * 2048 + c8 * 256,
                     [[2048, 64], [1, 256]])
        t.dma("pool", (lambda e, ap=ap: e.dma_start(dtraw[:], ap)),
              reads=[("hbc_raw", 34)], writes=[("ma", b, "dtr")], tag=f"m3ldt{b}_{c8}")
        t.op("dve", lambda e: e.tensor_scalar(f64a[:], dtraw[:], C["dtb"][0:64, :],
                                              None, ALU.add),
             reads=[("ma", b, "dtr"), "c_dtb"], writes=[KF], tag=f"m3z{b}_{c8}")
        t.op("act", lambda e: e.activation(f64b[:], f64a[:], AF.Abs, bias=0.0,
                                           scale=1.0),
             reads=[KF], writes=[("ma", b, 14)], tag=f"m3ab{b}_{c8}")
        t.op("act", lambda e: e.activation(f64b[:], f64b[:], AF.Exp, bias=0.0,
                                           scale=-1.0),
             reads=[("ma", b, 14)], writes=[("ma", b, 14)], tag=f"m3ex{b}_{c8}")
        t.op("act", lambda e: e.activation(f64b[:], f64b[:], AF.Ln,
                                           bias=C["fc"][0:64, 2:3], scale=1.0),
             reads=[("ma", b, 14), "c_fc"], writes=[("ma", b, 14)], tag=f"m3ln{b}_{c8}")
        t.op("act", lambda e: e.activation(f64a[:], f64a[:], AF.Relu, bias=0.0,
                                           scale=1.0),
             reads=[KF], writes=[KF], tag=f"m3rl{b}_{c8}")
        t.op("dve", lambda e: e.tensor_tensor(dt_cm[:], f64a[:], f64b[:], ALU.add),
             reads=[KF, ("ma", b, 14)], writes=[("ma", b, "dtc")], tag=f"m3dt{b}_{c8}")
        t.op("dve", lambda e: e.tensor_scalar(f64a[:], dt_cm[:], acol[0:64, :],
                                              None, ALU.mult),
             reads=[("ma", b, "dtc"), "ma_acol"], writes=[KF], tag=f"m3da{b}_{c8}")
        t.op("dve", lambda e: e.tensor_tensor_scan(acum[:], C["ones256"][0:64, :],
                                                   f64a[:], 0.0, ALU.mult, ALU.add),
             reads=[KF, "c_ones256"], writes=[KAC], tag=f"m3sc{b}_{c8}",
             weak=True, weak_read=True)
        t.op("act", lambda e: e.activation(edec_cm[:], acum[:], AF.Exp,
                                           bias=acum[:, 255:256], scale=-1.0),
             reads=[KAC], writes=[("ma", b, "edc")], tag=f"m3ed{b}_{c8}")
        t.op("act", lambda e: e.activation(f64b[0:64, 0:1], acum[:, 255:256],
                                           AF.Exp, bias=0.0, scale=1.0),
             reads=[KAC], writes=[("ma", b, 14)], tag=f"m3cd{b}_{c8}")
        t.op("pe", lambda e: e.matmul(ptr[0:1, 0:64], f64b[0:64, 0:1],
                                      C["identf"][0:64, 0:64], start=True, stop=True),
             reads=[("ma", b, 14), "c_identf"], writes=[ptr_k], tag=f"m3ct{b}_{c8}")
        t.op("dve", lambda e: e.tensor_copy(cdrow[:], ptr[0:1, 0:64]),
             reads=[ptr_k], writes=[("ma", b, "cdr")], tag=f"m3ce{b}_{c8}")
        t.op("pe", lambda e: e.matmul(ptr[:, 64:128], C["ones1f"][:],
                                      cdrow[:], start=True, stop=True),
             reads=[("ma", b, "cdr"), "c_ones1f"], writes=[ptr_k], tag=f"m3cb{b}_{c8}")
        t.op("dve", lambda e: e.tensor_copy(cdb[:], ptr[:, 64:128]),
             reads=[ptr_k], writes=[KCD], tag=f"m3cbe{b}_{c8}")
        for st in range(2):
            t.op("pe", (lambda e, st=st: e.matmul(
                ptr[:, 128:192], acum[0:64, st * 128:(st + 1) * 128],
                C["identf"][0:64, 0:64], start=True, stop=True)),
                 reads=[KAC, "c_identf"], writes=[ptr_k], tag=f"m3ta{b}_{c8}_{st}")
            t.op("act", (lambda e, st=st: e.activation(
                acum_tm[:, st * 64:(st + 1) * 64], ptr[:, 128:192], AF.Copy,
                bias=0.0, scale=1.0)),
                 reads=[ptr_k], writes=[KTM], tag=f"m3tae{b}_{c8}_{st}")
            t.op("act", (lambda e, st=st: e.activation(
                negacum_tm[:, st * 64:(st + 1) * 64], ptr[:, 128:192], AF.Copy,
                bias=0.0, scale=-1.0)),
                 reads=[ptr_k], writes=[KTM], tag=f"m3tne{b}_{c8}_{st}")
            t.op("act", (lambda e, st=st: e.activation(
                eacum_tm[:, st * 64:(st + 1) * 64], ptr[:, 128:192], AF.Exp,
                bias=0.0, scale=1.0)),
                 reads=[ptr_k], writes=[KTM], tag=f"m3tee{b}_{c8}_{st}")
            t.op("pe", (lambda e, st=st: e.matmul(
                ptr[:, 192:256], dt_cm[0:64, st * 128:(st + 1) * 128],
                C["identf"][0:64, 0:64], start=True, stop=True)),
                 reads=[("ma", b, "dtc"), "c_identf"], writes=[ptr_k],
                 tag=f"m3td{b}_{c8}_{st}")
            t.op("act", (lambda e, st=st: e.activation(
                dt_tm[:, st * 64:(st + 1) * 64], ptr[:, 192:256], AF.Copy,
                bias=0.0, scale=1.0)),
                 reads=[ptr_k], writes=[KTM], tag=f"m3tde{b}_{c8}_{st}")
            t.op("pe", (lambda e, st=st: e.matmul(
                ptr[:, 256:320], edec_cm[0:64, st * 128:(st + 1) * 128],
                C["identf"][0:64, 0:64], start=True, stop=True)),
                 reads=[("ma", b, "edc"), "c_identf"], writes=[ptr_k],
                 tag=f"m3te{b}_{c8}_{st}")
            t.op("act", (lambda e, st=st: e.activation(
                edec_tm[:, st * 64:(st + 1) * 64], ptr[:, 256:320], AF.Copy,
                bias=0.0, scale=1.0)),
                 reads=[ptr_k], writes=[KTM], tag=f"m3tede{b}_{c8}_{st}")
        for st in range(2):
            t.op("pe", (lambda e, st=st: e.matmul(
                pcb[:, 0:256], bc_sb[:, st * 128:(st + 1) * 128],
                bc_sb[:, 256:512], start=True, stop=True)),
                 reads=[KBC], writes=[pcb_k], tag=f"m3cbt{b}_{c8}_{st}")
            t.op("dve", (lambda e, st=st: e.tensor_tensor(
                cbt[:, st * 256:(st + 1) * 256], pcb[:, 0:256],
                C["ssdm"][:, st * 256:(st + 1) * 256], ALU.mult)),
                 reads=[pcb_k, "c_ssdm"], writes=[KCBT], tag=f"m3cbm{b}_{c8}_{st}")
            t.op("pe", (lambda e, st=st: e.matmul(
                pcb[:, 256:384], bc_sb[:, st * 128:(st + 1) * 128],
                C["ident"][:], start=True, stop=True)),
                 reads=[KBC, "c_ident"], writes=[pcb_k], tag=f"m3bt{b}_{c8}_{st}")
            t.op("dve", (lambda e, st=st: e.tensor_copy(
                b_tm[:, st * 128:(st + 1) * 128], pcb[:, 256:384])),
                 reads=[pcb_k], writes=[KBT], tag=f"m3bte{b}_{c8}_{st}")
        for tt4 in range(16):
            for j in range(4):
                tt = tt4 * 4 + j
                ctl, st = tt // 2, tt % 2
                t.op("pe", (lambda e, ctl=ctl, st=st, j=j: e.matmul(
                    ptr[:, 128 * j:128 * (j + 1)],
                    xs_cm[:, ctl * 256 + st * 128: ctl * 256 + st * 128 + 128],
                    C["ident"][:], start=True, stop=True)),
                     reads=[KX, "c_ident"], writes=[ptr_k],
                     tag=f"m3xt{b}_{c8}_{tt}")
            t.op("dve", (lambda e: e.tensor_copy(trev[:], ptr[:, 0:512])),
                 reads=[ptr_k], writes=[("ma", b, "trev")],
                 tag=f"m3xte{b}_{c8}_{tt4}")
            for j in range(4):
                tt = tt4 * 4 + j
                ctl, st = tt // 2, tt % 2
                t.op("dve", (lambda e, st=st, ctl=ctl, j=j: e.tensor_copy(
                    x_tm[:, st * 4096 + ctl * 128: st * 4096 + (ctl + 1) * 128],
                    trev[:, j * 128:(j + 1) * 128])),
                     reads=[("ma", b, "trev")], writes=[KXT],
                     tag=f"m3xtc{b}_{c8}_{tt}")
        for st in range(2):
            dtexp = bass.AP(dt_tm, st * 64, [[128, 128], [1, 64], [0, 64]])
            t.op("dve", (lambda e, st=st, dtexp=dtexp: e.scalar_tensor_tensor(
                xdt[:, st * 4096:(st + 1) * 4096],
                x_tm[:, st * 4096:(st + 1) * 4096], 1.0, dtexp,
                ALU.mult, ALU.mult)),
                 reads=[KXT, KTM], writes=[KDT], tag=f"m3xd{b}_{c8}_{st}")
            edexp = bass.AP(edec_tm, st * 64, [[128, 128], [1, 64], [0, 64]])
            t.op("dve", (lambda e, st=st, edexp=edexp: e.scalar_tensor_tensor(
                xdte[:, st * 4096:(st + 1) * 4096],
                xdt[:, st * 4096:(st + 1) * 4096], 1.0, edexp,
                ALU.mult, ALU.mult)),
                 reads=[KDT, KTM], writes=[KDE, KX], tag=f"m3xe{b}_{c8}_{st}")
        for g in range(8):
            for hl in range(8):
                h = g * 8 + hl
                mb = mbuf[hl % 2]
                t.op("pe", (lambda e, h=h: e.matmul(
                    pbc[:, 0:256], C["selb"][0:64, h * 128:(h + 1) * 128],
                    acum[:], start=True, stop=True)),
                     reads=["c_selb", KAC], writes=[pbc_k], tag=f"m3bc{b}_{c8}_{h}")
                for st in range(2):
                    t.op("dve", (lambda e, st=st, h=h: e.tensor_scalar(
                        argb[:], pbc[:, 0:256],
                        negacum_tm[:, st * 64 + h:st * 64 + h + 1], 0.0,
                        ALU.add, ALU.min)),
                         reads=[pbc_k, KTM], writes=[("ma", b, "arg")],
                         tag=f"m3ma{b}_{c8}_{h}_{st}")
                    t.op("act", (lambda e, st=st, mb=mb: e.activation(
                        mb[:, st * 256:(st + 1) * 256], argb[:], AF.Exp,
                        bias=0.0, scale=1.0)),
                         reads=[("ma", b, "arg")], writes=[("ma", b, "mT", hl % 2)],
                         tag=f"m3me{b}_{c8}_{h}_{st}")
                    t.op("dve", (lambda e, st=st, mb=mb: e.tensor_tensor(
                        mb[:, st * 256:(st + 1) * 256],
                        mb[:, st * 256:(st + 1) * 256],
                        cbt[:, st * 256:(st + 1) * 256], ALU.mult)),
                         reads=[("ma", b, "mT", hl % 2), KCBT],
                         writes=[("ma", b, "mT", hl % 2)],
                         tag=f"m3mm{b}_{c8}_{h}_{st}")
                for st in range(2):
                    pY1, pY1k = pY[st * 2]
                    for s2 in range(st + 1):
                        t.op("pe", (lambda e, st=st, s2=s2, hl=hl, mb=mb, h=h,
                                    pY1=pY1, s0=(s2 == 0), s1=(s2 == st):
                                    e.matmul(
                            pY1[:, hl * 64:(hl + 1) * 64],
                            mb[:, s2 * 256 + st * 128: s2 * 256 + st * 128 + 128],
                            xdt[:, s2 * 4096 + h * 64: s2 * 4096 + (h + 1) * 64],
                            start=s0, stop=s1)),
                             reads=[("ma", b, "mT", hl % 2), KDT],
                             writes=[pY1k], tag=f"m3y1{b}_{c8}_{h}_{st}_{s2}")
                    if c8 > 0:
                        pY2, pY2k = pY[st * 2 + 1]
                        t.op("pe", (lambda e, st=st, hl=hl, h=h, pY2=pY2:
                                    e.matmul(
                            pY2[:, hl * 64:(hl + 1) * 64],
                            bc_sb[:, 256 + st * 128:256 + st * 128 + 128],
                            carry[:, h * 64:(h + 1) * 64],
                            start=True, stop=True)),
                             reads=[KBC, ("carry", g)], writes=[pY2k],
                             tag=f"m3y2{b}_{c8}_{h}_{st}")
                for st in range(2):
                    t.op("pe", (lambda e, st=st, hl=hl, h=h,
                                s0=(st == 0), s1=(st == 1): e.matmul(
                        pst[:, hl * 64:(hl + 1) * 64],
                        b_tm[:, st * 128:(st + 1) * 128],
                        xdte[:, st * 4096 + h * 64: st * 4096 + (h + 1) * 64],
                        start=s0, stop=s1)),
                         reads=[KBT, KDE], writes=[pst_k],
                         tag=f"m3st{b}_{c8}_{h}_{st}")
            for st in range(2):
                pY1, pY1k = pY[st * 2]
                pY2, pY2k = pY[st * 2 + 1]
                if c8 > 0:
                    eaexp = bass.AP(eacum_tm, st * 64 + g * 8,
                                    [[128, 128], [1, 8], [0, 64]])
                    t.op("dve", (lambda e, st=st, eaexp=eaexp, pY2=pY2:
                                 e.scalar_tensor_tensor(
                                     ytmp[:], pY2[:, 0:512], 1.0, eaexp,
                                     ALU.mult, ALU.mult)),
                         reads=[pY2k, KTM], writes=[("ma", b, "yt1")],
                         tag=f"m3e1{b}_{c8}_{g}_{st}")
                    t.op("dve", (lambda e, st=st, pY1=pY1: e.tensor_tensor(
                        ytmp[:], ytmp[:], pY1[:, 0:512], ALU.add)),
                         reads=[("ma", b, "yt1"), pY1k], writes=[("ma", b, "yt1")],
                         tag=f"m3e2{b}_{c8}_{g}_{st}")
                else:
                    t.op("dve", (lambda e, st=st, pY1=pY1: e.tensor_copy(
                        ytmp[:], pY1[:, 0:512])),
                         reads=[pY1k], writes=[("ma", b, "yt1")],
                         tag=f"m3e2z{b}_{c8}_{g}_{st}")
                dexp = bass.AP(C["Db"], g * 8, [[64, 128], [1, 8], [0, 64]])
                t.op("dve", (lambda e, st=st, g=g, dexp=dexp:
                             e.scalar_tensor_tensor(
                                 ytmp2[:], x_tm[:, st * 4096 + g * 512:
                                                st * 4096 + (g + 1) * 512],
                                 1.0, dexp, ALU.mult, ALU.mult)),
                     reads=[KXT, "c_Db"], writes=[("ma", b, "yt2")],
                     tag=f"m3e3{b}_{c8}_{g}_{st}")
                t.op("dve", (lambda e, st=st, g=g: e.tensor_tensor(
                    ybuf[:, st * 4096 + g * 512: st * 4096 + (g + 1) * 512],
                    ytmp[:], ytmp2[:], ALU.add)),
                     reads=[("ma", b, "yt1"), ("ma", b, "yt2")], writes=[KY],
                     tag=f"m3e4{b}_{c8}_{g}_{st}")
            if c8 == 0:
                t.op("dve", (lambda e, g=g: e.tensor_copy(
                    carry[:, g * 512:(g + 1) * 512], pst[:, 0:512])),
                     reads=[pst_k], writes=[("carry", g)], tag=f"m3cu0{b}_{g}")
            else:
                cdexp = bass.AP(cdb, g * 8, [[64, 128], [1, 8], [0, 64]])
                t.op("dve", (lambda e, g=g, cdexp=cdexp:
                             e.scalar_tensor_tensor(
                                 ytmp[:], carry[:, g * 512:(g + 1) * 512],
                                 1.0, cdexp, ALU.mult, ALU.mult)),
                     reads=[("carry", g), KCD], writes=[("ma", b, "yt1")],
                     tag=f"m3cu1{b}_{c8}_{g}")
                t.op("dve", (lambda e, g=g: e.tensor_tensor(
                    carry[:, g * 512:(g + 1) * 512], ytmp[:], pst[:, 0:512],
                    ALU.add)),
                     reads=[("ma", b, "yt1"), pst_k], writes=[("carry", g)],
                     tag=f"m3cu2{b}_{c8}_{g}")
        if dbg_cb is not None:
            dbg_cb(c8, dict(acum=acum, dt_cm=dt_cm, cbt=cbt, x_tm=x_tm,
                            xdt=xdt, ybuf=ybuf, carry=carry, b_tm=b_tm,
                            eacum_tm=eacum_tm, negacum_tm=negacum_tm,
                            edec_tm=edec_tm, dt_tm=dt_tm, cdb=cdb, mbuf=mbuf,
                            keys=dict(KY=KY, KAC=KAC, KTM=KTM, KCBT=KCBT,
                                      KXT=KXT, KDT=KDT, KCD=KCD)))
        gsb = xdt
        ap = bass.AP(S["gate"].tensor, c8 * 256 * 4096,
                     [[4096, 128], [128 * 4096, 2], [1, 4096]])
        t.dma("pool", (lambda e, ap=ap: e.dma_start(gsb[:], ap)),
              reads=[("gate_s", st2, gc) for st2 in (c8 * 2, c8 * 2 + 1)
                     for gc in range(8)],
              writes=[KG, KDT], tag=f"m3lg{b}_{c8}")
        for st in range(2):
            t.op("dve", (lambda e, st=st: e.tensor_tensor(
                ybuf[:, st * 4096:(st + 1) * 4096],
                ybuf[:, st * 4096:(st + 1) * 4096],
                gsb[:, st * 4096:(st + 1) * 4096], ALU.mult)),
                 reads=[KY, KG], writes=[KY], tag=f"m3gt{b}_{c8}_{st}")
            t.op("act", (lambda e, st=st: e.activation(
                gsb[:, st * 4096:(st + 1) * 4096],
                ybuf[:, st * 4096:(st + 1) * 4096], AF.Square,
                bias=0.0, scale=1.0, accum_out=ycol[:, 0:1])),
                 reads=[KY], writes=[KG, ("ma", b, "yc")],
                 tag=f"m3sq{b}_{c8}_{st}")
            t.op("act", (lambda e: e.activation(
                ycol[:, 1:2], ycol[:, 0:1], AF.Ln, bias=C["fc"][:, 1:2],
                scale=1.0 / 4096.0)),
                 reads=[("ma", b, "yc"), "c_fc"], writes=[("ma", b, "yc")],
                 tag=f"m3l{b}_{c8}_{st}")
            t.op("act", (lambda e: e.activation(
                ycol[:, 1:2], ycol[:, 1:2], AF.Exp, bias=0.0, scale=-0.5)),
                 reads=[("ma", b, "yc")], writes=[("ma", b, "yc")],
                 tag=f"m3r{b}_{c8}_{st}")
            t.op("dve", (lambda e, st=st: e.scalar_tensor_tensor(
                ybuf[:, st * 4096:(st + 1) * 4096],
                ybuf[:, st * 4096:(st + 1) * 4096], ycol[:, 1:2],
                C["nrmw"][:], ALU.mult, ALU.mult)),
                 reads=[KY, ("ma", b, "yc"), "c_nrmw"], writes=[KY],
                 tag=f"m3nw{b}_{c8}_{st}")
            for ct4 in range(8):
                for j in range(4):
                    ct = ct4 * 4 + j
                    t.op("pe", (lambda e, st=st, ct=ct, j=j: e.matmul(
                        ptr[:, 128 * j:128 * (j + 1)],
                        ybuf[:, st * 4096 + ct * 128: st * 4096 + (ct + 1) * 128],
                        C["ident"][:], start=True, stop=True)),
                         reads=[KY, "c_ident"], writes=[ptr_k],
                         tag=f"m3yt{b}_{c8}_{st}_{ct}")
                t.op("dve", (lambda e: e.tensor_copy(trev[:], ptr[:, 0:512])),
                     reads=[ptr_k], writes=[("ma", b, "trev")],
                     tag=f"m3yte{b}_{c8}_{st}_{ct4}")
                dst = bass.AP(S["yT"].tensor,
                              (ct4 * 4) * 128 * 2048 + c8 * 256 + st * 128,
                              [[2048, 128], [128 * 2048, 4], [1, 128]])
                t.dma("pool", (lambda e, dst=dst: e.dma_start(dst, trev[:])),
                      reads=[("ma", b, "trev")], writes=[("yT", c8, st, ct4)],
                      tag=f"m3ys{b}_{c8}_{st}_{ct4}")
    t.barrier(f"ma3{b}")
    es3.close()

    es4 = ExitStack()
    wod = es4.enter_context(nc.sbuf_tensor(f"ma_wod{b}", [128, 32 * 512], BF16))
    itb = [es4.enter_context(nc.sbuf_tensor(f"ma_itb{b}_{i}", [128, 32 * 128], BF16))
           for i in range(2)]
    osb = [es4.enter_context(nc.sbuf_tensor(f"ma_osb{b}_{i}", [128, 512], F32))
           for i in range(2)]
    for dc in range(4):
        for it in range(32):
            ap = bass.AP(W["W_outT"].tensor, it * 128 * 2048 + dc * 512,
                         [[2048, 128], [1, 512]])
            t.dma("pool", (lambda e, it=it, ap=ap: e.dma_start(
                wod[:, it * 512:(it + 1) * 512], ap)),
                  writes=[("ma_wod", it)], tag=f"m4w{b}_{dc}_{it}")
        for st in range(16):
            kb = st % 2
            ib = itb[kb]
            lap = bass.AP(S["yT"].tensor, st * 128,
                          [[2048, 128], [128 * 2048, 32], [1, 128]])
            t.dma("pool", (lambda e, ib=ib, lap=lap: e.dma_start(ib[:], lap)),
                  reads=[("yT", st // 2, st % 2, c4) for c4 in range(8)],
                  writes=[("ma_itb", kb)], tag=f"m4il{b}_{dc}_{st}")
            ps, psk = psum8[kb]
            for it in range(32):
                t.op("pe", (lambda e, ib=ib, it=it, ps=ps,
                            s0=(it == 0), s1=(it == 31): e.matmul(
                    ps[:], ib[:, it * 128:(it + 1) * 128],
                    wod[:, it * 512:(it + 1) * 512], start=s0, stop=s1)),
                     reads=[("ma_itb", kb), ("ma_wod", it)], writes=[psk],
                     tag=f"m4m{b}_{dc}_{st}_{it}")
            ob = osb[kb]
            t.op("dve", (lambda e, ob=ob, ps=ps: e.tensor_copy(ob[:], ps[:])),
                 reads=[psk], writes=[("ma_osb", kb)], tag=f"m4e{b}_{dc}_{st}")
            out_tile_cb(st, dc, ob, [("ma_osb", kb)])
    t.barrier(f"ma4{b}")
    es4.close()


def act_fence(nc, t, col_ap, key, tag):
    t.op("act", (lambda e: e.activation(col_ap, col_ap, AF.Copy, bias=0.0,
                                        scale=1.0)),
         reads=[key], writes=[key], tag=tag)


def emit_ln(nc, t, es, b, layer, x_d, x_rkey, store_d, acc_key, acc_xn,
            xnT, xnT_key, C, ptrp):
    ptr, ptr_k = ptrp
    xt = es.enter_context(nc.sbuf_tensor(f"ln_x{layer}{b}", [128, 2048], F32))
    xc = es.enter_context(nc.sbuf_tensor(f"ln_c{layer}{b}", [128, 2048], F32))
    xnb = es.enter_context(nc.sbuf_tensor(f"ln_n{layer}{b}", [128, 2048], BF16))
    sj = es.enter_context(nc.sbuf_tensor(f"ln_j{layer}{b}", [128, 2048], BF16))
    cols = es.enter_context(nc.sbuf_tensor(f"ln_co{layer}{b}", [128, 4], F32))
    KXT_, KC, KN, KJ, KCO = [(f"ln{layer}{b}", i) for i in range(5)]
    for st in range(16):
        src = bass.AP(x_d.tensor, (b * 2048 + st * 128) * 2048,
                      [[2048, 128], [1, 2048]])
        t.dma("pool", (lambda e, src=src: e.dma_start(xt[:], src)),
              reads=[x_rkey(st)], writes=[KXT_], tag=f"lnl{layer}{b}_{st}")
        dstx = bass.AP(store_d.tensor, (b * 2048 + st * 128) * 2048,
                       [[2048, 128], [1, 2048]])
        t.dma("pool", (lambda e, dstx=dstx: e.dma_start(dstx, xt[:])),
              reads=[KXT_], writes=[acc_key(st)], tag=f"lnsx{layer}{b}_{st}")
        t.op("act", (lambda e: e.activation(sj[:], xt[:], AF.Copy, bias=0.0,
                                            scale=1.0, accum_out=cols[:, 0:1])),
             reads=[KXT_], writes=[KJ, KCO], tag=f"lnsm{layer}{b}_{st}")
        t.op("act", (lambda e: e.activation(cols[:, 1:2], cols[:, 0:1],
                                            AF.Copy, bias=0.0,
                                            scale=-1.0 / 2048.0)),
             reads=[KCO], writes=[KCO], tag=f"lnmc{layer}{b}_{st}")
        t.op("dve", (lambda e: e.tensor_scalar(xc[:], xt[:], cols[:, 1:2],
                                               None, ALU.add)),
             reads=[KXT_, KCO], writes=[KC], tag=f"lncc{layer}{b}_{st}")
        t.op("act", (lambda e: e.activation(sj[:], xc[:], AF.Square, bias=0.0,
                                            scale=1.0, accum_out=cols[:, 2:3])),
             reads=[KC], writes=[KJ, KCO], tag=f"lnsq{layer}{b}_{st}")
        t.op("act", (lambda e: e.activation(cols[:, 3:4], cols[:, 2:3], AF.Ln,
                                            bias=C["fc"][:, 1:2],
                                            scale=1.0 / 2048.0)),
             reads=[KCO, "c_fc"], writes=[KCO], tag=f"lnln{layer}{b}_{st}")
        t.op("act", (lambda e: e.activation(cols[:, 3:4], cols[:, 3:4], AF.Exp,
                                            bias=0.0, scale=-0.5)),
             reads=[KCO], writes=[KCO], tag=f"lnex{layer}{b}_{st}")
        t.op("dve", (lambda e: e.tensor_scalar(xnb[:], xc[:], cols[:, 3:4],
                                               None, ALU.mult)),
             reads=[KC, KCO], writes=[KN], tag=f"lnxn{layer}{b}_{st}")
        if acc_xn:
            t.op("dve", (lambda e: e.tensor_scalar(xc[:], xc[:], cols[:, 3:4],
                                                   None, ALU.mult)),
                 reads=[KC, KCO], writes=[KC], tag=f"lnxf{layer}{b}_{st}")
            t.dma("pool", (lambda e, dstx=dstx: e.dma_start(
                dstx, xc[:], accum_op=ALU.add)),
                  reads=[KC, acc_key(st)], writes=[acc_key(st)],
                  tag=f"lnax{layer}{b}_{st}")
        for hc4 in range(4):
            for j in range(4):
                hc = hc4 * 4 + j
                t.op("pe", (lambda e, hc=hc, j=j: e.matmul(
                    ptr[:, j * 128:(j + 1) * 128],
                    xnb[:, hc * 128:(hc + 1) * 128], C["ident"][:],
                    start=True, stop=True)),
                     reads=[KN, "c_ident"], writes=[ptr_k],
                     tag=f"lnt{layer}{b}_{st}_{hc}")
            for j in range(4):
                hc = hc4 * 4 + j
                t.op("dve", (lambda e, hc=hc, st=st, j=j: e.tensor_copy(
                    xnT[:, hc * 2048 + st * 128:hc * 2048 + (st + 1) * 128],
                    ptr[:, j * 128:(j + 1) * 128])),
                     reads=[ptr_k], writes=[xnT_key],
                     tag=f"lnte{layer}{b}_{st}_{hc}")


CSPEC_G = [("ident", [128, 128], BF16), ("tril01", [128, 128], BF16),
           ("ones_col", [128, 1], BF16), ("ones1f", [1, 128], F32),
           ("fc", [128, 4], F32), ("ln1", [128, 16], F32),
           ("ln2", [128, 16], F32)]
CSPEC_A1 = [("pneg", [128, 128], BF16), ("cosT", [128, 2048], F32),
            ("sinT", [128, 2048], F32)]
CSPEC_MA = [("identf", [128, 128], F32), ("selb", [64, 8192], F32),
            ("ssdm", [128, 512], BF16), ("Db", [128, 64], F32),
            ("dtb", [64, 1], F32), ("alog", [64, 1], F32),
            ("wcv", [128, 136], F32), ("cbc", [128, 34], F32),
            ("ones256", [128, 256], F32), ("nrmw", [128, 4096], BF16)]

WSPEC = [("L0_W_inT", [2048, 8512]), ("L0_W_outT", [4096, 2048]),
         ("A1_WqT", [2048, 2048]), ("A1_WkT", [2048, 512]),
         ("A1_WvT", [2048, 512]), ("A1_WoT", [2048, 2048]),
         ("F1_WqT", [2048, 2048]), ("F1_WkT", [2048, 2048]),
         ("F1_WvT", [2048, 2048]), ("F1_Wo", [2048, 2048]),
         ("F1_A", [2048, 16]), ("F1_B", [16, 8192]),
         ("F1_W_inT", [2048, 8192]), ("F1_W_upT", [2048, 8192]),
         ("F1_W_downT", [8192, 2048])]


_LC_SEQ = [0]


def _load_consts(nc, t, es, cd, spec):
    C = {}
    for n, sh, dt in spec:
        _LC_SEQ[0] += 1
        C[n] = es.enter_context(
            nc.sbuf_tensor(f"c_{n}_{_LC_SEQ[0]}", sh, dt))
        t.dma("pool", (lambda e, n=n, C=C: e.dma_start(C[n][:], cd[n])),
              writes=["c_" + n], tag="ldc_" + n)
    return C


def build_full(n_b=NB, dump_acc0=False):
    nc = bass.Bass("TRN2", target_bir_lowering=False)
    t = Tin(nc)
    x_d = nc.dram_tensor("x", [n_b * 2048 * 2048], F32, kind="ExternalInput").ap()
    out_d = nc.dram_tensor("out", [n_b * 2048 * 2048], F32, kind="ExternalOutput").ap()
    gamma_d = nc.dram_tensor("gamma", [N_QT * 128 * 5 * 128], F32, kind="ExternalInput").ap()
    cpw_d = nc.dram_tensor("cpw", [N_QT * 128 * 15], F32, kind="ExternalInput").ap()
    cd = {}
    for n, sh, dt in CSPEC_G + CSPEC_A1 + CSPEC_MA:
        cd[n] = nc.dram_tensor("i_" + n, sh, dt, kind="ExternalInput").ap()
    Wd = {}
    for n, sh in WSPEC:
        Wd[n] = nc.dram_tensor(n, sh, F32, kind="ExternalInput").ap()
    S = {"hbc": nc.dram_tensor("s_hbc", [4416 * 2048], BF16, kind="Internal").ap(),
         "gate": nc.dram_tensor("s_gate", [2048 * 4096], BF16, kind="Internal").ap(),
         "yT": nc.dram_tensor("s_yT", [4096 * 2048], BF16, kind="Internal").ap(),
         "w3": nc.dram_tensor("s_w3", [3 * N_SAMP], BF16, kind="Internal").ap(),
         "int": nc.dram_tensor("s_int", [8192 * 2048], BF16, kind="Internal").ap(),
         "acc0": nc.dram_tensor("s_acc0", [n_b * 2048 * 2048], F32,
                                kind="ExternalOutput" if dump_acc0 else "Internal").ap()}

    es = ExitStack()
    psc = es.enter_context(nc.psum_tensor("psc", [128, 2048], F32))
    pm = [es.enter_context(nc.psum_tensor(f"pm{i}", [128, 512], F32)) for i in range(4)]
    psc_keys = [("psb", i) for i in range(4)]
    pmisc = {"proj": pm[0], "rope": pm[1], "pt": pm[2], "pv": pm[3],
             ("op", 0): pm[0], ("op", 1): pm[1]}
    pknames = {"proj": 0, "rope": 1, "pt": 2, "pv": 3, ("op", 0): 0, ("op", 1): 1}
    pkey = lambda n: ("ps", pknames[n])
    psum6 = [(psc[:, i * 512:(i + 1) * 512], ("psb", i)) for i in range(4)] +             [(pm[2], ("ps", 2)), (pm[3], ("ps", 3))]
    psum8 = [(psc[:, i * 512:(i + 1) * 512], ("psb", i)) for i in range(4)] +             [(pm[i], ("ps", i)) for i in range(4)]
    C = _load_consts(nc, t, es, cd, CSPEC_G)

    es_rc = ExitStack()
    mk_w3key = lambda w, g: ("rc_w3", w, g)
    emit_recon(nc, t, es_rc, gamma_d, cpw_d, S["w3"])
    wsrc = wsrc_spline(t, S["w3"], mk_w3key)
    W0 = {"W_inT": Wd["L0_W_inT"], "W_outT": Wd["L0_W_outT"]}

    for b in range(n_b):
        es_b = ExitStack()
        xnT = es_b.enter_context(nc.sbuf_tensor(f"xnT0_{b}", [128, 16 * 2048], BF16))
        es_ln = ExitStack()
        emit_ln(nc, t, es_ln, b, 0, x_d, lambda st: "x_in", S["acc0"],
                (lambda st, b=b: ("acc0", b, st)), False, xnT, "xnT", C,
                (pm[3], ("ps", 3)))
        t.barrier(f"ln0_{b}")
        es_ln.close()
        emit_mamba(nc, t, es_b, b, xnT, "xnT", W0, None, S, None, psum8,
                   phase="m1")
        emit_glu(nc, t, es_b, "f0", b, xnT, "xnT", wsrc, S["int"], None,
                 psum6, phase="gu")
        t.barrier(f"l0gu_{b}")
        es_b.close()
        es_r = ExitStack()
        Cm = _load_consts(nc, t, es_r, cd, CSPEC_MA)
        Cm["fc"] = C["fc"]
        Cm["ident"] = C["ident"]
        Cm["ones1f"] = C["ones1f"]

        def cb_mamba(st, dc, ob, reads, b=b):
            dst = bass.AP(S["acc0"].tensor, (b * 2048 + st * 128) * 2048 + dc * 512,
                          [[2048, 128], [1, 512]])
            t.dma("pool", (lambda e, dst=dst, ob=ob: e.dma_start(
                dst, ob[:], accum_op=ALU.add)),
                  reads=reads + [("acc0", b, st)], writes=[("acc0", b, st)],
                  tag=f"accm{b}_{st}_{dc}")
        emit_mamba(nc, t, es_r, b, None, None, W0, Cm, S, cb_mamba, psum8,
                   phase="rest")

        def cb_ffn0(st, dc, ob, reads, b=b):
            dst = bass.AP(S["acc0"].tensor, (b * 2048 + st * 128) * 2048 + dc * 256,
                          [[2048, 128], [1, 256]])
            t.dma("pool", (lambda e, dst=dst, ob=ob: e.dma_start(
                dst, ob[:], accum_op=ALU.add)),
                  reads=reads + [("acc0", b, st)], writes=[("acc0", b, st)],
                  tag=f"accf0{b}_{st}_{dc}")
        emit_glu(nc, t, es_r, "f0", b, None, None, wsrc, S["int"], cb_ffn0,
                 psum6, phase="down")
        t.barrier(f"l0_{b}")
        es_r.close()
    es_rc.close()

    for b in range(n_b):
        es_b = ExitStack()
        xnT = es_b.enter_context(nc.sbuf_tensor(f"xnT1_{b}", [128, 16 * 2048], BF16))
        attnT = es_b.enter_context(nc.sbuf_tensor(f"attnT_{b}", [128, 8 * 2048], BF16))
        es_ln = ExitStack()
        emit_ln(nc, t, es_ln, b, 1, S["acc0"], (lambda st, b=b: ("acc0", b, st)),
                out_d, (lambda st, b=b: ("out", b, st)), True, xnT, "xnT", C,
                (pm[3], ("ps", 3)))
        t.barrier(f"ln1_{b}")
        es_ln.close()
        es_a = ExitStack()
        Ca = _load_consts(nc, t, es_a, cd, CSPEC_A1)
        Ca.update(C)
        at_cache = {}
        opb = None
        for g0 in (0, 8):
            emit_attention(nc, t, es_a, f"a1_{b}", b, xnT, "xnT", 16, 4, True,
                           Wd["A1_WqT"], Wd["A1_WkT"], Wd["A1_WvT"], attnT,
                           "attnT", Ca, psc, psc_keys, pmisc, pkey,
                           heads=list(range(g0, g0 + 8)), cache=at_cache)

            def cb_at(st, dc, ob, reads, first, b=b):
                dst = bass.AP(out_d.tensor, (b * 2048 + st * 128) * 2048 + dc * 512,
                              [[2048, 128], [1, 512]])
                t.dma("pool", (lambda e, dst=dst, ob=ob: e.dma_start(
                    dst, ob[:], accum_op=ALU.add)),
                      reads=reads + [("out", b, st)], writes=[("out", b, st)],
                      tag=f"acca1{b}_{st}_{dc}_{first}")
            opb = emit_outproj(nc, t, es_a, f"a1_{b}", attnT, "attnT",
                               Wd["A1_WoT"], cb_at, pmisc, pkey,
                               heads=list(range(g0, g0 + 8)), bufs=opb)
        t.barrier(f"l1a_{b}")
        es_a.close()

        def cb_ffn1(st, dc, ob, reads, b=b):
            dst = bass.AP(out_d.tensor, (b * 2048 + st * 128) * 2048 + dc * 256,
                          [[2048, 128], [1, 256]])
            t.dma("pool", (lambda e, dst=dst, ob=ob: e.dma_start(
                dst, ob[:], accum_op=ALU.add)),
                  reads=reads + [("out", b, st)], writes=[("out", b, st)],
                  tag=f"accf1{b}_{st}_{dc}")
        W1 = {"WqT": Wd["F1_WqT"], "WkT": Wd["F1_WkT"], "WvT": Wd["F1_WvT"],
              "Wo": Wd["F1_Wo"], "A": Wd["F1_A"], "B": Wd["F1_B"],
              "W_inT": Wd["F1_W_inT"], "W_upT": Wd["F1_W_upT"],
              "W_downT": Wd["F1_W_downT"]}
        es_f = ExitStack()
        Cf = dict(C)
        Cf["woa"] = es_b.enter_context(nc.sbuf_tensor(f"c_woa{b}", [128, 256], BF16))
        Cf["a_sb"] = es_b.enter_context(nc.sbuf_tensor(f"c_a_sb{b}", [128, 256], BF16))
        Cf["modT"] = es_b.enter_context(nc.sbuf_tensor(f"c_modT{b}", [16, 2048], BF16))
        emit_l1ffn(nc, t, es_f, b, xnT, "xnT", attnT, W1, Cf, S["int"], cb_ffn1,
                   psum6, psc, psc_keys, pmisc, pkey, do_woa=True)
        t.barrier(f"l1f_{b}")
        es_f.close()
        es_b.close()

    t.finalize_on("sp")
    t.emit(verbose=True)
    es.close()
    return nc


_SPLINE_CACHE = {}


def host_prep(inputs):
    import ml_dtypes
    np_ = np
    if "gamma" not in _SPLINE_CACHE:
        g, bq = spline_tables()
        _SPLINE_CACHE["gamma"] = np_.ascontiguousarray(g.reshape(-1))
        _SPLINE_CACHE["bq"] = bq
    bq = _SPLINE_CACHE["bq"]
    cpw = np_.stack([spline_cpw(np_.asarray(inputs[k]), bq) for k in
                     ("L0_ffn_gate_cp", "L0_ffn_up_cp", "L0_ffn_down_cp")],
                    axis=-1).transpose(0, 2, 1).reshape(-1)
    TT = lambda a: np_.ascontiguousarray(np_.asarray(a).T)
    ins = {"gamma": _SPLINE_CACHE["gamma"], "cpw": np_.ascontiguousarray(cpw)}
    ins["L0_W_inT"] = TT(inputs["L0_attn_W_in"])
    ins["L0_W_outT"] = TT(inputs["L0_attn_W_out"])
    ins["A1_WqT"] = TT(inputs["L1_attn_Wq"])
    ins["A1_WkT"] = TT(inputs["L1_attn_Wk"])
    ins["A1_WvT"] = TT(inputs["L1_attn_Wv"])
    ins["A1_WoT"] = TT(inputs["L1_attn_Wo"])
    ins["F1_WqT"] = TT(inputs["L1_ffn_Wq"])
    ins["F1_WkT"] = TT(inputs["L1_ffn_Wk"])
    ins["F1_WvT"] = TT(inputs["L1_ffn_Wv"])
    ins["F1_Wo"] = np_.ascontiguousarray(np_.asarray(inputs["L1_ffn_Wo"]))
    ins["F1_A"] = np_.ascontiguousarray(np_.asarray(inputs["L1_ffn_A"]))
    ins["F1_B"] = np_.ascontiguousarray(np_.asarray(inputs["L1_ffn_B"]))
    ins["F1_W_inT"] = TT(inputs["L1_ffn_W_in_base"])
    ins["F1_W_upT"] = TT(inputs["L1_ffn_W_up_base"])
    ins["F1_W_downT"] = TT(inputs["L1_ffn_W_down"])
    bf = ml_dtypes.bfloat16
    ins["i_ident"] = np_.eye(128).astype(bf)
    ins["i_identf"] = np_.eye(128).astype(np_.float32)
    ins["i_tril01"] = np_.tril(np_.ones((128, 128))).astype(bf)
    pneg = np_.zeros((128, 128), np_.float32)
    for dp in range(128):
        if dp % 2 == 0:
            pneg[dp + 1, dp] = -1.0
        else:
            pneg[dp - 1, dp] = 1.0
    ins["i_pneg"] = pneg.astype(bf)
    freqs = 1.0 / (10000.0 ** (np_.arange(0, 128, 2, dtype=np_.float64) / 128))
    ang = np_.arange(2048, dtype=np_.float64)[:, None] * freqs[None, :]
    ins["i_cosT"] = np_.cos(ang).T.repeat(2, axis=0).astype(np_.float32)
    ins["i_sinT"] = np_.sin(ang).T.repeat(2, axis=0).astype(np_.float32)
    ins["i_ones_col"] = np_.ones((128, 1), bf)
    ins["i_ones1f"] = np_.ones((1, 128), np_.float32)
    ins["i_fc"] = np_.broadcast_to(
        np_.array([1e-6, 1e-5, 1.0, 0.0], np_.float32), (128, 4)).copy()
    selb = np_.zeros((64, 64, 128), np_.float32)
    for h in range(64):
        selb[h, h, :] = 1.0
    ins["i_selb"] = np_.ascontiguousarray(selb.transpose(1, 0, 2).reshape(64, 8192))
    m0 = np_.zeros((128, 256), np_.float32)
    m1 = np_.zeros((128, 256), np_.float32)
    for s in range(128):
        m0[s, s:] = 1.0
        m1[s, 128 + s:] = 1.0
    ins["i_ssdm"] = np_.concatenate([m0, m1], 1).astype(bf)
    ins["i_Db"] = np_.broadcast_to(np_.asarray(inputs["L0_attn_D"]), (128, 64)).copy()
    ins["i_dtb"] = np_.asarray(inputs["L0_attn_dt_bias"]).reshape(64, 1).copy()
    ins["i_alog"] = np_.asarray(inputs["L0_attn_A_log"]).reshape(64, 1).copy()
    cw = np_.asarray(inputs["L0_attn_conv_w"])
    ins["i_wcv"] = np_.ascontiguousarray(
        cw.reshape(34, 128, 4).transpose(1, 0, 2).reshape(128, 136))
    ins["i_cbc"] = np_.ascontiguousarray(
        np_.asarray(inputs["L0_attn_conv_b"]).reshape(34, 128).T)
    ins["i_ones256"] = np_.ones((128, 256), np_.float32)
    ins["i_nrmw"] = np_.broadcast_to(np_.asarray(inputs["L0_attn_norm_w"]),
                                     (128, 4096)).astype(bf)
    ins["i_ln1"] = np_.ascontiguousarray(
        np_.asarray(inputs["L1_ffn_ln1_w"]).reshape(16, 128).T)
    ins["i_ln2"] = np_.ascontiguousarray(
        np_.asarray(inputs["L1_ffn_ln2_w"]).reshape(16, 128).T)
    return ins


from concourse.bass_utils import run_bass_kernel_spmd

_NC_CACHE = {}


def kernel(**inputs):
    inputs = {k: np.asarray(v) for k, v in inputs.items()}
    common = host_prep(inputs)
    x = np.ascontiguousarray(np.asarray(inputs["L0_attn_x"], np.float32))
    n_cores = 8
    per = x.shape[0] // n_cores
    if "nc" not in _NC_CACHE:
        _NC_CACHE["nc"] = build_full(per)
    nc = _NC_CACHE["nc"]
    in_maps = []
    for c in range(n_cores):
        m = dict(common)
        m["x"] = np.ascontiguousarray(x[c * per:(c + 1) * per].reshape(-1))
        in_maps.append(m)
    res = run_bass_kernel_spmd(nc, in_maps, core_ids=list(range(n_cores)))
    out = np.concatenate(
        [np.asarray(r["out"], np.float32).reshape(per, 2048, 2048)
         for r in res.results], axis=0)
    return out


# version 05 (final)
# speedup vs baseline: 1.0124x; 1.0124x over previous
import numpy as np
from contextlib import ExitStack

import concourse.bass as bass
import concourse.mybir as mybir

F32 = mybir.dt.float32
BF16 = mybir.dt.bfloat16
AF = mybir.ActivationFunctionType
ALU = mybir.AluOpType
AX = mybir.AxisListType

ENGINES = ("pe", "act", "dve", "pool", "sp")
BLOCK_ATTR = {"pe": "tensor", "act": "scalar", "dve": "vector",
              "pool": "gpsimd", "sp": "sync"}

HID = 2048
SEQ = 2048
NB = 2


class _Op:
    __slots__ = ("idx", "eng", "track", "fn", "reads", "writes", "tag",
                 "deps", "need_inc", "inc_val", "tpos", "vclock", "is_dma",
                 "waits", "weak", "weak_read")

    def __init__(self):
        self.deps = []
        self.need_inc = False
        self.inc_val = None
        self.vclock = None
        self.weak = False
        self.weak_read = False


class Tin:
    def __init__(self, nc):
        self.nc = nc
        self.ops = []
        self._writers = {}
        self._readers = {}
        self.stats = {}

    def op(self, eng, fn, reads=(), writes=(), tag="", dma=False, weak=False,
           weak_read=False):
        o = _Op()
        o.idx = len(self.ops)
        o.eng = eng
        o.is_dma = dma
        o.weak = weak
        o.weak_read = weak_read
        o.track = ("q" + eng) if dma else eng
        o.fn = fn
        o.reads = tuple(reads)
        o.writes = tuple(writes)
        o.tag = tag
        deps = set()
        for r in o.reads:
            deps.update(self._writers.get(r, ()))
        for r in o.writes:
            deps.update(self._writers.get(r, ()))
            deps.update(self._readers.get(r, ()))
        deps.discard(o)
        o.deps = [d for d in deps
                  if d.weak or d.is_dma or o.is_dma or o.weak_read
                  or d.track != o.track]
        for r in o.writes:
            self._writers[r] = [o]
            self._readers[r] = []
        for r in o.reads:
            self._readers.setdefault(r, []).append(o)
        self.ops.append(o)
        return o

    def dma(self, eng, fn, reads=(), writes=(), tag=""):
        return self.op(eng, fn, reads, writes, tag, dma=True)

    def barrier(self, tag="bar"):
        last = {}
        for o in self.ops:
            last[o.track] = o
        center = self.op("sp", lambda e: e.nop(), tag=tag + "_c")
        center.deps = [d for d in last.values() if d is not center]
        for eng in ("pe", "act", "dve", "pool"):
            o = self.op(eng, lambda e: e.nop(), tag=tag + "_" + eng)
            o.deps = [center]

    def finalize_on(self, eng_name="sp"):
        last = {}
        tcount = {}
        for o in self.ops:
            if o.is_dma:
                pos = tcount.get(o.track, 0)
                tcount[o.track] = pos + 1
                last[(o.track, pos % self.R_DMA)] = o
            else:
                last[o.track] = o
        deps = [o for k, o in last.items() if k != eng_name]
        o = self.op(eng_name, lambda eng: eng.nop(), tag="final_fence")
        o.deps = deps
        return o

    R_DMA = 12

    def emit(self, verbose=False):
        nc = self.nc
        tracks = {}
        for o in self.ops:
            lst = tracks.setdefault(o.track, [])
            o.tpos = len(lst) + 1
            lst.append(o)
        for o in self.ops:
            if o.is_dma:
                o.need_inc = True
            for d in o.deps:
                d.need_inc = True
        inc_count = {t: 0 for t in tracks}
        for o in self.ops:
            if not o.need_inc:
                continue
            if o.is_dma:
                sid = (o.tpos - 1) % self.R_DMA
                use = (o.tpos - 1) // self.R_DMA + 1
                o.inc_val = (sid, use)
            else:
                inc_count[o.track] += 1
                o.inc_val = inc_count[o.track]
        SEG = 50000
        n_seg = {t: (inc_count[t] - 1) // SEG + 1 for t in tracks
                 if not t.startswith("q")}
        self._SEG = SEG

        def dep_key_val(d):
            if d.weak:
                return ("w", d.idx), 1
            if d.is_dma:
                sid, use = d.inc_val
                return (d.track, sid), use
            return d.track, d.tpos

        last_in_eng = {}
        n_waits = 0
        for o in self.ops:
            prev = last_in_eng.get(o.eng)
            vc = dict(prev.vclock) if prev is not None else {}
            if prev is not None and not prev.is_dma:
                vc[prev.track] = max(vc.get(prev.track, 0), prev.tpos)
            waits = []
            deps = sorted(o.deps, key=lambda d: -d.idx)
            if o.is_dma:
                sid, use = (o.tpos - 1) % self.R_DMA, (o.tpos - 1) // self.R_DMA + 1
                if use > 1 and vc.get((o.track, sid), 0) < use - 1:
                    deps.append(tracks[o.track][o.tpos - 1 - self.R_DMA])
            for d in deps:
                key, val = dep_key_val(d)
                same_track_forced = (o.weak_read and not d.is_dma
                                     and not d.weak and d.track == o.track)
                if not same_track_forced and vc.get(key, 0) >= val:
                    continue
                waits.append(d)
                for k, v in d.vclock.items():
                    if vc.get(k, 0) < v:
                        vc[k] = v
                vc[key] = max(vc.get(key, 0), val)
                if d.weak:
                    vc[d.track] = max(vc.get(d.track, 0), d.tpos)
            o.vclock = vc
            o.waits = waits
            n_waits += len(waits)
            last_in_eng[o.eng] = o
        sems = {}
        for t in tracks:
            if t.startswith("q"):
                n_used = min(self.R_DMA, len(tracks[t]))
                for sid in range(n_used):
                    sems[(t, sid)] = nc.alloc_semaphore(name=f"s_{t}_{sid}")
            else:
                for sg in range(n_seg.get(t, 1)):
                    sems[(t, sg)] = nc.alloc_semaphore(name=f"s_{t}_{sg}")
        self.stats = dict(n_ops=len(self.ops), n_waits=n_waits,
                          n_incs=sum(1 for o in self.ops if o.need_inc),
                          per_track={t: len(v) for t, v in tracks.items()},
                          max_inc={t: inc_count[t] for t in tracks
                                   if not t.startswith("q")})
        if verbose:
            print("tin stats:", self.stats)
        per_engine = {e: [] for e in ENGINES}
        for o in self.ops:
            per_engine[o.eng].append(o)
        with nc.Block() as block:
            for e in ENGINES:
                lst = per_engine[e]
                if not lst:
                    continue

                SEG = self._SEG

                def body(eng, lst=lst):
                    for o in lst:
                        for d in o.waits:
                            if d.is_dma:
                                sid, use = d.inc_val
                                eng.wait_ge(sems[(d.track, sid)], use * 16)
                            else:
                                sg = (d.inc_val - 1) // SEG
                                eng.wait_ge(sems[(d.track, sg)],
                                            d.inc_val - sg * SEG)
                        inst = o.fn(eng)
                        if o.need_inc:
                            if inst is None:
                                raise RuntimeError(f"op {o.tag or o.idx} needs "
                                                   f"inc but returned None")
                            if o.is_dma:
                                sid, use = o.inc_val
                                inst.then_inc(sems[(o.track, sid)], 16)
                            else:
                                sg = (o.inc_val - 1) // SEG
                                inst.then_inc(sems[(o.track, sg)], 1)

                getattr(block, BLOCK_ATTR[e])(body)
        return self


N_CTRL = 130156
N_SAMP = 8192 * 2048
N_QB = N_SAMP // 128
N_QT = N_QB // 128
RC_G = 4
RC_GSAMP = RC_G * 16384


def rc_groups(j0, j1):
    return range(j0 // RC_GSAMP, (j1 + RC_GSAMP - 1) // RC_GSAMP)


def spline_tables():
    n = N_SAMP
    t = np.linspace(0.0, N_CTRL - 1.0, n, dtype=np.float64)
    i = np.clip(np.floor(t).astype(np.int64), 0, N_CTRL - 2)
    f = (t - i).astype(np.float32).astype(np.float64)
    f2 = f * f
    f3 = f2 * f
    c = np.stack([
        0.5 * (-f3 + 2.0 * f2 - f),
        0.5 * (3.0 * f3 - 5.0 * f2 + 2.0),
        0.5 * (-3.0 * f3 + 4.0 * f2 + f),
        0.5 * (f3 - f2),
    ], axis=0)
    q = np.arange(n) // 128
    bq = i[::128].copy()
    off = (i - bq[q])
    m = off[None, :] + np.arange(4)[:, None]
    gamma = np.zeros((5, n), dtype=np.float64)
    np.put_along_axis(
        gamma.reshape(5, n), m, c, axis=0)
    gamma = gamma.astype(np.float32).reshape(5, N_QB, 128).transpose(1, 0, 2)
    return np.ascontiguousarray(gamma), bq


def spline_cpw(cp, bq):
    idx = np.clip(bq[:, None] - 1 + np.arange(5)[None, :], 0, N_CTRL - 1)
    return np.ascontiguousarray(cp[idx])


def emit_recon(nc, t, es, gamma_d, cpw_d, w3_d, eng_split=("dve",)):
    G = RC_G
    NG = N_QT // G
    gbuf = [es.enter_context(nc.sbuf_tensor(f"rc_g{i}", [128, 5 * 128 * G], F32))
            for i in range(2)]
    cbuf = [es.enter_context(nc.sbuf_tensor(f"rc_c{i}", [128, 15 * G], F32))
            for i in range(2)]
    acc = es.enter_context(nc.sbuf_tensor("rc_acc", [128, 128], F32))
    wout = [es.enter_context(nc.sbuf_tensor(f"rc_w{i}", [128, 3 * 128 * G], BF16))
            for i in range(2)]
    for g in range(NG):
        pb = g % 2
        gb, cb, wo = gbuf[pb], cbuf[pb], wout[pb]
        gap = bass.AP(gamma_d.tensor, g * G * 128 * 5 * 128,
                      [[5 * 128, 128], [128 * 5 * 128, G], [1, 5 * 128]])
        cap = bass.AP(cpw_d.tensor, g * G * 128 * 15,
                      [[15, 128], [128 * 15, G], [1, 15]])
        t.dma("pool", (lambda e, gb=gb, gap=gap: e.dma_start(gb[:], gap)),
              writes=[("rc_g", pb)], tag=f"rc_ldg{g}")
        t.dma("pool", (lambda e, cb=cb, cap=cap: e.dma_start(cb[:], cap)),
              writes=[("rc_c", pb)], tag=f"rc_ldc{g}")
        for ti in range(G):
            for w in range(3):
                for m in range(5):
                    gsl = gb[:, ti * 640 + m * 128: ti * 640 + (m + 1) * 128]
                    csl = cb[:, ti * 15 + w * 5 + m: ti * 15 + w * 5 + m + 1]
                    if m == 0:
                        t.op("dve", (lambda e, gsl=gsl, csl=csl:
                                     e.tensor_scalar(acc[:], gsl, csl, None,
                                                     ALU.mult)),
                             reads=[("rc_g", pb), ("rc_c", pb)],
                             writes=["rc_acc"], tag=f"rc_m0_{g}_{ti}_{w}")
                    elif m < 4:
                        t.op("dve", (lambda e, gsl=gsl, csl=csl:
                                     e.scalar_tensor_tensor(acc[:], gsl, csl,
                                                            acc[:], ALU.mult,
                                                            ALU.add)),
                             reads=[("rc_g", pb), ("rc_c", pb), "rc_acc"],
                             writes=["rc_acc"], tag=f"rc_m{m}_{g}_{ti}_{w}")
                    else:
                        osl = wo[:, (w * G + ti) * 128:(w * G + ti + 1) * 128]
                        t.op("dve", (lambda e, gsl=gsl, csl=csl, osl=osl:
                                     e.scalar_tensor_tensor(osl, gsl, csl,
                                                            acc[:], ALU.mult,
                                                            ALU.add)),
                             reads=[("rc_g", pb), ("rc_c", pb), "rc_acc"],
                             writes=[("rc_w", pb)], tag=f"rc_m4_{g}_{ti}_{w}")
        for w in range(3):
            oap = bass.AP(w3_d.tensor, w * N_SAMP + g * G * 16384,
                          [[128, 128], [16384, G], [1, 128]])
            t.dma("pool", (lambda e, wo=wo, w=w, oap=oap:
                           e.dma_start(oap, wo[:, w * G * 128:(w + 1) * G * 128])),
                  reads=[("rc_w", pb)], writes=[("rc_w3", w, g)],
                  tag=f"rc_st{g}_{w}")


def emit_glu(nc, t, es, tag, b, hT, hT_key, wsrc, inter_d, out_tile_cb, psum,
             extra_gate_mm=None, phase="both", db_w=True):
    if phase in ("both", "gu"):
        _emit_glu_gu(nc, t, es, tag, b, hT, hT_key, wsrc, inter_d, psum,
                     extra_gate_mm, db_w)
    if phase in ("both", "down"):
        _emit_glu_down(nc, t, es, tag, b, wsrc, inter_d, out_tile_cb, psum)


def _emit_glu_gu(nc, t, es, tag, b, hT, hT_key, wsrc, inter_d, psum,
                 extra_gate_mm, db_w=True):
    nwb = 2 if db_w else 1
    wg2 = [es.enter_context(nc.sbuf_tensor(f"{tag}_wg{b}_{i}", [128, 16 * 512], BF16))
           for i in range(nwb)]
    wu2 = [es.enter_context(nc.sbuf_tensor(f"{tag}_wu{b}_{i}", [128, 16 * 512], BF16))
           for i in range(nwb)]
    gsil = [es.enter_context(nc.sbuf_tensor(f"{tag}_gs{b}_{i}", [128, 512], BF16))
            for i in range(2)]
    isb = [es.enter_context(nc.sbuf_tensor(f"{tag}_i{b}_{i}", [128, 512], BF16))
           for i in range(2)]
    for ic in range(16):
        kw = ic % nwb
        wg, wu = wg2[kw], wu2[kw]
        for hc in range(16):
            for w, buf, bkey in ((0, wg, f"{tag}_wg{kw}"), (1, wu, f"{tag}_wu{kw}")):
                dst = buf[:, hc * 512:(hc + 1) * 512]
                wsrc["emit_load_gu"](w, ic, hc, dst, (bkey, hc))
        for it in range(4):
            for sc in range(4):
                k = (it * 4 + sc) % 2
                (pg, pgk), (pu, puk) = psum[2 * k], psum[2 * k + 1]
                for w, buf, ps, psk in ((0, wg, pg, pgk), (1, wu, pu, puk)):
                    hold = extra_gate_mm is not None and w == 0
                    for hc in range(16):
                        lhsT = buf[:, hc * 512 + it * 128: hc * 512 + (it + 1) * 128]
                        rhs = hT[:, hc * 2048 + sc * 512: hc * 2048 + (sc + 1) * 512]
                        t.op("pe", (lambda e, ps=ps, lhsT=lhsT, rhs=rhs,
                                    st_=(hc == 0),
                                    sp=(hc == 15 and not hold):
                                    e.matmul(ps[:], lhsT, rhs, start=st_, stop=sp)),
                             reads=[(f"{tag}_wg{kw}" if w == 0 else f"{tag}_wu{kw}", hc),
                                    hT_key],
                             writes=[psk],
                             tag=f"{tag}mm{w}_{ic}_{it}_{sc}_{hc}")
                    if hold:
                        extra_gate_mm(ic, it, sc, pg, pgk)
                g, i_ = gsil[k], isb[k]
                t.op("act", (lambda e, g=g, pg=pg: e.activation(
                    g[:], pg[:], AF.Silu, bias=0.0, scale=1.0)),
                     reads=[pgk], writes=[(f"{tag}_gs", k)],
                     tag=f"{tag}silu{ic}_{it}_{sc}")
                t.op("dve", (lambda e, i_=i_, pu=pu, g=g: e.tensor_tensor(
                    i_[:], pu[:], g[:], ALU.mult)),
                     reads=[puk, (f"{tag}_gs", k)],
                     writes=[(f"{tag}_i", k)], tag=f"{tag}mul{ic}_{it}_{sc}")
                rt = ic * 4 + it
                dst = bass.AP(inter_d.tensor, (rt * 128) * 2048 + sc * 512,
                              [[2048, 128], [1, 512]])
                t.dma("pool", (lambda e, dst=dst, i_=i_: e.dma_start(dst, i_[:])),
                      reads=[(f"{tag}_i", k)], writes=[("int", rt, sc)],
                      tag=f"{tag}ist{ic}_{it}_{sc}")


def _emit_glu_down(nc, t, es, tag, b, wsrc, inter_d, out_tile_cb, psum):
    wd = es.enter_context(nc.sbuf_tensor(f"{tag}_wd{b}", [128, 64 * 256], BF16))
    itb = [es.enter_context(nc.sbuf_tensor(f"{tag}_it{b}_{i}", [128, 64 * 128], BF16))
           for i in range(2)]
    osb = [es.enter_context(nc.sbuf_tensor(f"{tag}_o{b}_{i}", [128, 256], F32))
           for i in range(2)]
    for dc in range(8):
        for it in range(64):
            dst = wd[:, it * 256:(it + 1) * 256]
            wsrc["emit_load_dn"](dc, it, dst, (f"{tag}_wd", it))
        for st in range(16):
            kb = st % 2
            ib = itb[kb]
            lap = bass.AP(inter_d.tensor, st * 128,
                          [[2048, 128], [128 * 2048, 64], [1, 128]])
            t.dma("pool", (lambda e, ib=ib, lap=lap: e.dma_start(ib[:], lap)),
                  reads=[("int", rt, st // 4) for rt in range(64)],
                  writes=[(f"{tag}_it", kb)], tag=f"{tag}itl{dc}_{st}")
            ps, psk = psum[4 + st % 2]
            for it in range(64):
                lhsT = ib[:, it * 128:(it + 1) * 128]
                rhs = wd[:, it * 256:(it + 1) * 256]
                t.op("pe", (lambda e, ps=ps, lhsT=lhsT, rhs=rhs,
                            s0=(it == 0), s1=(it == 63):
                            e.matmul(ps[:, 0:256], lhsT, rhs, start=s0, stop=s1)),
                     reads=[(f"{tag}_it", kb), (f"{tag}_wd", it)],
                     writes=[psk], tag=f"{tag}dmm{dc}_{st}_{it}")
            ob = osb[st % 2]
            t.op("dve", (lambda e, ob=ob, ps=ps: e.tensor_copy(ob[:], ps[:, 0:256])),
                 reads=[psk], writes=[(f"{tag}_o", st % 2)],
                 tag=f"{tag}dev{dc}_{st}")
            out_tile_cb(st, dc, ob, [(f"{tag}_o", st % 2)])


def wsrc_spline(t, w3_d, w3_key):
    def emit_load_gu(w, ic, hc, dst, dkey):
        src = bass.AP(w3_d.tensor, w * N_SAMP + (ic * 512) * 2048 + hc * 128,
                      [[2048, 512], [1, 128]])
        j0 = (ic * 512) * 2048
        rd = [w3_key(w, g) for g in rc_groups(j0, j0 + 512 * 2048)]
        t.dma("sp", (lambda e, dst=dst, src=src:
                     e.dma_start(dst, src, transpose=True)),
              reads=rd, writes=[dkey], tag=f"sW{w}_{ic}_{hc}")

    def emit_load_dn(dc, it, dst, dkey):
        src = bass.AP(w3_d.tensor, 2 * N_SAMP + (dc * 256) * 8192 + it * 128,
                      [[8192, 256], [1, 128]])
        rd = [w3_key(2, g) for g in rc_groups((dc * 256) * 8192,
                                              (dc * 256 + 256) * 8192)]
        t.dma("sp", (lambda e, dst=dst, src=src:
                     e.dma_start(dst, src, transpose=True)),
              reads=rd, writes=[dkey], tag=f"sWd{dc}_{it}")
    return dict(emit_load_gu=emit_load_gu, emit_load_dn=emit_load_dn)


def wsrc_direct(t, win_d, wup_d, wdn_d):
    def emit_load_gu(w, ic, hc, dst, dkey):
        wd = win_d if w == 0 else wup_d
        src = bass.AP(wd.tensor, (hc * 128) * 8192 + ic * 512,
                      [[8192, 128], [1, 512]])
        t.dma("pool", (lambda e, dst=dst, src=src: e.dma_start(dst, src)),
              writes=[dkey], tag=f"dW{w}_{ic}_{hc}")

    def emit_load_dn(dc, it, dst, dkey):
        src = bass.AP(wdn_d.tensor, (it * 128) * 2048 + dc * 256,
                      [[2048, 128], [1, 256]])
        t.dma("pool", (lambda e, dst=dst, src=src: e.dma_start(dst, src)),
              writes=[dkey], tag=f"dWd{dc}_{it}")
    return dict(emit_load_gu=emit_load_gu, emit_load_dn=emit_load_dn)


SQRT_HD_INV = 0.08838834764831845


def emit_attention(nc, t, es, tagp, b, hT, hT_key, n_q, n_kv, use_rope,
                   wq_d, wk_d, wv_d, attnT, attnT_key, C, psc, psc_keys,
                   pmisc, pkey, heads=None, cache=None):
    G = n_q // n_kv
    if heads is None:
        heads = list(range(n_q))
    h0 = heads[0]
    kvs = sorted({h // G for h in heads})
    if cache is not None and cache:
        (wbuf, qraw, qt_sb, kt_sb, v_sb, probs, pdum, pt_sb, tr,
         cols) = cache["bufs"]
    else:
        wbuf = [es.enter_context(nc.sbuf_tensor(f"{tagp}_w{i}", [128, 2048], BF16))
                for i in range(2)]
        qraw = (es.enter_context(nc.sbuf_tensor(f"{tagp}_qraw", [128, 2048], BF16))
                if use_rope else None)
        qt_sb = es.enter_context(nc.sbuf_tensor(f"{tagp}_qt", [128, 2048], BF16))
        kt_sb = es.enter_context(nc.sbuf_tensor(f"{tagp}_kt", [128, 2048], BF16))
        v_sb = es.enter_context(nc.sbuf_tensor(f"{tagp}_v", [128, 2048], BF16))
        probs = es.enter_context(nc.sbuf_tensor(f"{tagp}_pr", [128, 2048], BF16))
        pdum = es.enter_context(nc.sbuf_tensor(f"{tagp}_pd", [128, 128], BF16))
        pt_sb = es.enter_context(nc.sbuf_tensor(f"{tagp}_ptb", [128, 2048], BF16))
        tr = (es.enter_context(nc.sbuf_tensor(f"{tagp}_tr", [128, 1024], F32))
              if use_rope else None)
        cols = es.enter_context(nc.sbuf_tensor(f"{tagp}_cols", [128, 10], F32))
        if cache is not None:
            cache["bufs"] = (wbuf, qraw, qt_sb, kt_sb, v_sb, probs, pdum,
                             pt_sb, tr, cols)
    K_QR, K_QT, K_KT, K_V, K_PR, K_PD, K_PT, K_TR, K_CO = (
        (tagp, x) for x in ("qraw", "qt", "kt", "v", "pr", "pd", "pt", "tr", "co"))
    t.dbg_tiles = dict(qt=qt_sb, kt=kt_sb, v=v_sb, pr=probs, cols=cols)

    def load_w(wd, head, buf, wkey, tag):
        n_tot = wd.ap[0][0]
        ap = bass.AP(wd.tensor, head * 128,
                     [[n_tot, 128], [128 * n_tot, 16], [1, 128]])
        t.dma("pool", (lambda e, buf=buf, ap=ap: e.dma_start(buf[:], ap)),
              writes=[wkey], tag=tag)

    def proj_T(buf, wkey, dst, dkey, scale, do_rope, tg):
        for c in range(4):
            for hc in range(16):
                t.op("pe", (lambda e, buf=buf, hc=hc, c=c,
                            s0=(hc == 0), s1=(hc == 15):
                            e.matmul(pmisc["proj"][:],
                                     buf[:, hc * 128:(hc + 1) * 128],
                                     hT[:, hc * 2048 + c * 512:
                                        hc * 2048 + (c + 1) * 512],
                                     start=s0, stop=s1)),
                     reads=[wkey, hT_key], writes=[pkey("proj")],
                     tag=f"{tg}pj{c}_{hc}")
            if not do_rope:
                t.op("act", (lambda e, c=c, dst=dst, scale=scale:
                             e.activation(dst[:, c * 512:(c + 1) * 512],
                                          pmisc["proj"][:], AF.Copy,
                                          bias=0.0, scale=scale)),
                     reads=[pkey("proj")], writes=[dkey], tag=f"{tg}ev{c}")
            else:
                t.op("act", (lambda e, c=c, scale=scale:
                             e.activation(qraw[:, c * 512:(c + 1) * 512],
                                          pmisc["proj"][:], AF.Copy,
                                          bias=0.0, scale=scale)),
                     reads=[pkey("proj")], writes=[K_QR], tag=f"{tg}ev{c}")
                t.op("pe", (lambda e, c=c: e.matmul(
                    pmisc["rope"][:], C["pneg"][:],
                    qraw[:, c * 512:(c + 1) * 512], start=True, stop=True)),
                     reads=[K_QR, "c_pneg"], writes=[pkey("rope")],
                     tag=f"{tg}rp{c}")
                t.op("dve", (lambda e, c=c: e.tensor_tensor(
                    tr[:, 0:512], pmisc["rope"][:],
                    C["sinT"][:, c * 512:(c + 1) * 512], ALU.mult)),
                     reads=[pkey("rope"), "c_sinT"], writes=[K_TR],
                     tag=f"{tg}rs{c}")
                t.op("dve", (lambda e, c=c: e.tensor_tensor(
                    tr[:, 512:1024], qraw[:, c * 512:(c + 1) * 512],
                    C["cosT"][:, c * 512:(c + 1) * 512], ALU.mult)),
                     reads=[K_QR, "c_cosT"], writes=[K_TR], tag=f"{tg}rc{c}")
                t.op("dve", (lambda e, c=c, dst=dst: e.tensor_tensor(
                    dst[:, c * 512:(c + 1) * 512], tr[:, 0:512],
                    tr[:, 512:1024], ALU.add)),
                     reads=[K_TR], writes=[dkey], tag=f"{tg}ra{c}")

    for kvh in kvs:
        wb = wbuf[0]
        load_w(wk_d, kvh, wb, (tagp, "w", 0), f"{tagp}lwk{kvh}")
        proj_T(wb, (tagp, "w", 0), kt_sb, K_KT, 1.0, use_rope,
               f"{tagp}K{kvh}")
        wbv = wbuf[1]
        load_w(wv_d, kvh, wbv, (tagp, "w", 1), f"{tagp}lwv{kvh}")
        for st in range(16):
            for hc in range(16):
                t.op("pe", (lambda e, st=st, hc=hc, wbv=wbv,
                            s0=(hc == 0), s1=(hc == 15):
                            e.matmul(pmisc["pv"][:, 0:128],
                                     hT[:, hc * 2048 + st * 128:
                                        hc * 2048 + (st + 1) * 128],
                                     wbv[:, hc * 128:(hc + 1) * 128],
                                     start=s0, stop=s1)),
                     reads=[(tagp, "w", 1), hT_key], writes=[pkey("pv")],
                     tag=f"{tagp}v{kvh}_{st}_{hc}")
            t.op("act", (lambda e, st=st: e.activation(
                v_sb[:, st * 128:(st + 1) * 128], pmisc["pv"][:, 0:128],
                AF.Copy, bias=0.0, scale=1.0)),
                 reads=[pkey("pv")], writes=[K_V], tag=f"{tagp}vev{kvh}_{st}")
        for hq in [h for h in heads if h // G == kvh]:
            wbq = wbuf[0]
            load_w(wq_d, hq, wbq, (tagp, "w", 0), f"{tagp}lwq{hq}")
            proj_T(wbq, (tagp, "w", 0), qt_sb, K_QT, SQRT_HD_INV, use_rope,
                   f"{tagp}Q{hq}")
            for qt in range(16):
                kspan = (qt + 1) * 128
                co = (qt % 2) * 5
                KC2 = (K_CO, qt % 2)
                if qt < 8:
                    base = (qt % 2) * 1024
                    pkeys = psc_keys[2 * (qt % 2):2 * (qt % 2) + 2]
                else:
                    base = 0
                    pkeys = psc_keys
                nch = (kspan + 511) // 512
                for kc in range(nch):
                    n = min(512, kspan - kc * 512)
                    t.op("pe", (lambda e, qt=qt, kc=kc, n=n, base=base:
                                e.matmul(psc[:, base + kc * 512:base + kc * 512 + n],
                                         qt_sb[:, qt * 128:(qt + 1) * 128],
                                         kt_sb[:, kc * 512:kc * 512 + n],
                                         start=True, stop=True)),
                         reads=[K_QT, K_KT], writes=[pkeys[kc]],
                         tag=f"{tagp}sc{hq}_{qt}_{kc}")
                t.op("dve", (lambda e, kspan=kspan, base=base, co=co:
                             e.reduce_max(
                    cols[:, co:co + 1], psc[:, base:base + kspan], axis=AX.X,
                    negate=True)),
                     reads=pkeys[:nch], writes=[KC2], tag=f"{tagp}mx{hq}_{qt}")
                if qt > 0:
                    t.op("act", (lambda e, qt=qt, base=base, co=co:
                                 e.activation(
                        probs[:, 0:qt * 128], psc[:, base:base + qt * 128],
                        AF.Exp, bias=cols[:, co:co + 1], scale=1.0,
                        accum_out=cols[:, co + 1:co + 2])),
                         reads=pkeys[:nch] + [KC2], writes=[K_PR, KC2],
                         tag=f"{tagp}e1{hq}_{qt}")
                t.op("act", (lambda e, qt=qt, base=base, co=co: e.activation(
                    pdum[:], psc[:, base + qt * 128:base + (qt + 1) * 128],
                    AF.Exp, bias=cols[:, co:co + 1], scale=1.0)),
                     reads=[pkeys[(qt * 128) // 512 if qt < 8 else qt // 4],
                            KC2],
                     writes=[K_PD],
                     tag=f"{tagp}e2{hq}_{qt}")
                t.op("dve", (lambda e, qt=qt: e.tensor_tensor(
                    probs[:, qt * 128:(qt + 1) * 128], pdum[:],
                    C["tril01"][:], ALU.mult)),
                     reads=[K_PD, "c_tril01"], writes=[K_PR],
                     tag=f"{tagp}tm{hq}_{qt}")
                t.op("act", (lambda e, qt=qt, co=co: e.activation(
                    pdum[:], probs[:, qt * 128:(qt + 1) * 128], AF.Copy,
                    bias=0.0, scale=1.0, accum_out=cols[:, co + 4:co + 5])),
                     reads=[K_PR], writes=[K_PD, KC2],
                     tag=f"{tagp}s2{hq}_{qt}")
                if qt > 0:
                    t.op("act", (lambda e, co=co: e.activation(
                        cols[:, co + 2:co + 3], cols[:, co + 4:co + 5],
                        AF.Identity, bias=cols[:, co + 1:co + 2], scale=1.0)),
                         reads=[KC2], writes=[KC2], tag=f"{tagp}sa{hq}_{qt}")
                else:
                    t.op("act", (lambda e, co=co: e.activation(
                        cols[:, co + 2:co + 3], cols[:, co + 4:co + 5],
                        AF.Copy, bias=0.0, scale=1.0)),
                         reads=[KC2], writes=[KC2], tag=f"{tagp}sa{hq}_{qt}")
                t.op("dve", (lambda e, co=co: e.reciprocal(
                    cols[:, co + 3:co + 4], cols[:, co + 2:co + 3])),
                     reads=[KC2], writes=[KC2], tag=f"{tagp}rc{hq}_{qt}",
                     weak_read=True, weak=True)
                t.op("dve", (lambda e, kspan=kspan, co=co: e.tensor_scalar(
                    probs[:, 0:kspan], probs[:, 0:kspan],
                    cols[:, co + 3:co + 4], None, ALU.mult)),
                     reads=[K_PR, KC2], writes=[K_PR],
                     tag=f"{tagp}nm{hq}_{qt}")
                for kt4 in range((qt + 1 + 3) // 4):
                    nblk = min(4, qt + 1 - kt4 * 4)
                    for j in range(nblk):
                        kt = kt4 * 4 + j
                        t.op("pe", (lambda e, kt=kt, j=j: e.matmul(
                            pmisc["pt"][:, j * 128:(j + 1) * 128],
                            probs[:, kt * 128:(kt + 1) * 128], C["ident"][:],
                            start=True, stop=True)),
                             reads=[K_PR, "c_ident"], writes=[pkey("pt")],
                             tag=f"{tagp}pt{hq}_{qt}_{kt}")
                    t.op("dve", (lambda e, kt4=kt4, nblk=nblk: e.tensor_copy(
                        pt_sb[:, kt4 * 512:kt4 * 512 + nblk * 128],
                        pmisc["pt"][:, 0:nblk * 128])),
                         reads=[pkey("pt")], writes=[K_PT],
                         tag=f"{tagp}ptev{hq}_{qt}_{kt4}")
                for kt in range(qt + 1):
                    t.op("pe", (lambda e, kt=kt, s0=(kt == 0), s1=(kt == qt):
                                e.matmul(pmisc["pv"][:, 0:128],
                                         v_sb[:, kt * 128:(kt + 1) * 128],
                                         pt_sb[:, kt * 128:(kt + 1) * 128],
                                         start=s0, stop=s1)),
                         reads=[K_V, K_PT], writes=[pkey("pv")],
                         tag=f"{tagp}pv{hq}_{qt}_{kt}")
                slot = hq - h0
                t.op("act", (lambda e, slot=slot, qt=qt: e.activation(
                    attnT[:, slot * 2048 + qt * 128:slot * 2048 + (qt + 1) * 128],
                    pmisc["pv"][:, 0:128], AF.Copy, bias=0.0, scale=1.0)),
                     reads=[pkey("pv")], writes=[attnT_key],
                     tag=f"{tagp}oev{hq}_{qt}")


def emit_outproj(nc, t, es, tagp, attnT, attnT_key, wo_d, out_cb, pmisc, pkey,
                 heads, bufs=None):
    if bufs is None:
        wob = es.enter_context(nc.sbuf_tensor(f"{tagp}_wo", [128, 8 * 512], BF16))
        osb = [es.enter_context(nc.sbuf_tensor(f"{tagp}_os{i}", [128, 512], F32))
               for i in range(2)]
        bufs = (wob, osb)
    wob, osb = bufs
    nh = len(heads)
    first = heads[0] == 0
    for dc in range(4):
        ap = bass.AP(wo_d.tensor, heads[0] * 128 * 2048 + dc * 512,
                     [[2048, 128], [128 * 2048, nh], [1, 512]])
        t.dma("pool", (lambda e, ap=ap: e.dma_start(wob[:, 0:nh * 512], ap)),
              writes=[(tagp, "wo")], tag=f"{tagp}lwo{dc}")
        for st in range(16):
            pk = ("op", st % 2)
            for j in range(nh):
                t.op("pe", (lambda e, j=j, st=st, pk=pk,
                            s0=(j == 0), s1=(j == nh - 1):
                            e.matmul(pmisc[pk][:],
                                     attnT[:, j * 2048 + st * 128:
                                           j * 2048 + (st + 1) * 128],
                                     wob[:, j * 512:(j + 1) * 512],
                                     start=s0, stop=s1)),
                     reads=[attnT_key, (tagp, "wo")],
                     writes=[pkey(pk)], tag=f"{tagp}op{dc}_{j}_{st}")
            ob = osb[st % 2]
            t.op("dve", (lambda e, ob=ob, pk=pk: e.tensor_copy(
                ob[:], pmisc[pk][:])),
                 reads=[pkey(pk)], writes=[(tagp, "os", st % 2)],
                 tag=f"{tagp}opev{dc}_{st}")
            out_cb(st, dc, ob, [(tagp, "os", st % 2)], first)
    return bufs


def emit_rmsnorm_pair(nc, t, es, b, xnT, xnT_key, ln1_col, ln2_col, h1T,
                      h1T_key, psc, psc_keys, pb, pb_key, C, eps=1e-6):
    sq = es.enter_context(nc.sbuf_tensor(f"rn_sq{b}", [128, 2048], BF16))
    rrow = es.enter_context(nc.sbuf_tensor(f"rn_r{b}", [1, 2048], F32))
    rbc = es.enter_context(nc.sbuf_tensor(f"rn_b{b}", [128, 2048], F32))
    K_SQ, K_RR, K_RB = (f"rn_sq{b}", f"rn_r{b}", f"rn_b{b}")
    for hc in range(16):
        t.op("act", (lambda e, hc=hc: e.activation(
            sq[:], xnT[:, hc * 2048:(hc + 1) * 2048], AF.Square, bias=0.0,
            scale=1.0)),
             reads=[xnT_key], writes=[K_SQ], tag=f"rnsq{b}_{hc}")
        for sc in range(4):
            t.op("pe", (lambda e, hc=hc, sc=sc, s0=(hc == 0), s1=(hc == 15):
                        e.matmul(psc[0:1, sc * 512:(sc + 1) * 512],
                                 C["ones_col"][:], sq[:, sc * 512:(sc + 1) * 512],
                                 start=s0, stop=s1)),
                 reads=[K_SQ, "c_ones_col"], writes=[psc_keys[sc]],
                 tag=f"rnmm{b}_{hc}_{sc}")
    t.op("act", (lambda e: e.activation(rrow[:], psc[0:1, 0:2048], AF.Ln,
                                        bias=C["fc"][0:1, 0:1],
                                        scale=1.0 / 2048.0)),
         reads=list(psc_keys) + ["c_fc"], writes=[K_RR], tag=f"rnln{b}")
    t.op("act", (lambda e: e.activation(rrow[:], rrow[:], AF.Exp, bias=0.0,
                                        scale=-0.5)),
         reads=[K_RR], writes=[K_RR], tag=f"rnex{b}")
    for sc in range(4):
        t.op("pe", (lambda e, sc=sc: e.matmul(
            pb[:], C["ones1f"][:], rrow[0:1, sc * 512:(sc + 1) * 512],
            start=True, stop=True)),
             reads=[K_RR, "c_ones1f"], writes=[pb_key], tag=f"rnbc{b}_{sc}")
        t.op("dve", (lambda e, sc=sc: e.tensor_copy(
            rbc[:, sc * 512:(sc + 1) * 512], pb[:])),
             reads=[pb_key], writes=[K_RB], tag=f"rnbe{b}_{sc}")
    for hc in range(16):
        t.op("dve", (lambda e, hc=hc: e.scalar_tensor_tensor(
            h1T[:, hc * 2048:(hc + 1) * 2048],
            xnT[:, hc * 2048:(hc + 1) * 2048], ln1_col[:, hc:hc + 1],
            rbc[:], ALU.mult, ALU.mult)),
             reads=[xnT_key, "c_ln1", K_RB], writes=[h1T_key],
             tag=f"rnh1{b}_{hc}")
    for hc in range(16):
        t.op("dve", (lambda e, hc=hc: e.scalar_tensor_tensor(
            xnT[:, hc * 2048:(hc + 1) * 2048],
            xnT[:, hc * 2048:(hc + 1) * 2048], ln2_col[:, hc:hc + 1],
            rbc[:], ALU.mult, ALU.mult)),
             reads=[xnT_key, "c_ln2", K_RB], writes=[xnT_key],
             tag=f"rnh2{b}_{hc}")


_WOA_SEQ = [0]


def emit_woa(nc, t, es, wo_d, a_d, woa_sb, a_sb, pb, pb_key):
    _WOA_SEQ[0] += 1
    wob = es.enter_context(nc.sbuf_tensor(f"woa_wob{_WOA_SEQ[0]}",
                                          [128, 2048], BF16))
    aap = bass.AP(a_d.tensor, 0, [[16, 128], [128 * 16, 16], [1, 16]])
    t.dma("pool", (lambda e: e.dma_start(a_sb[:], aap)), writes=["c_a_sb"],
          tag="woa_lda")
    for at in range(16):
        ap = bass.AP(wo_d.tensor, at * 128,
                     [[2048, 128], [128 * 2048, 16], [1, 128]])
        t.dma("pool", (lambda e, ap=ap: e.dma_start(wob[:], ap)),
              writes=["woa_wob"], tag=f"woa_ldw{at}")
        for d2t in range(16):
            t.op("pe", (lambda e, d2t=d2t, s0=(d2t == 0), s1=(d2t == 15):
                        e.matmul(pb[:, 0:16], wob[:, d2t * 128:(d2t + 1) * 128],
                                 a_sb[:, d2t * 16:(d2t + 1) * 16],
                                 start=s0, stop=s1)),
                 reads=["woa_wob", "c_a_sb"], writes=[pb_key],
                 tag=f"woamm{at}_{d2t}")
        t.op("dve", (lambda e, at=at: e.tensor_copy(
            woa_sb[:, at * 16:(at + 1) * 16], pb[:, 0:16])),
             reads=[pb_key], writes=["c_woa"], tag=f"woaev{at}")


def emit_attnA_partial(nc, t, es, b, attnT, attnT_key, woa_sb, attA_acc,
                       nh, first, pb, pb_key):
    for st in range(16):
        for j in range(nh):
            t.op("pe", (lambda e, st=st, j=j, s0=(j == 0), s1=(j == nh - 1):
                        e.matmul(pb[:, 0:16],
                                 attnT[:, j * 2048 + st * 128:
                                       j * 2048 + (st + 1) * 128],
                                 woa_sb[:, j * 16:(j + 1) * 16],
                                 start=s0, stop=s1)),
                 reads=[attnT_key, "c_woa"], writes=[pb_key],
                 tag=f"aA{b}_{st}_{j}")
        if first:
            t.op("dve", (lambda e, st=st: e.tensor_copy(
                attA_acc[:, st * 16:(st + 1) * 16], pb[:, 0:16])),
                 reads=[pb_key], writes=["attA_acc"], tag=f"aAe{b}_{st}")
        else:
            t.op("dve", (lambda e, st=st: e.tensor_tensor(
                attA_acc[:, st * 16:(st + 1) * 16], pb[:, 0:16],
                attA_acc[:, st * 16:(st + 1) * 16], ALU.add)),
                 reads=[pb_key, "attA_acc"], writes=["attA_acc"],
                 tag=f"aAa{b}_{st}")


def emit_mod_finish(nc, t, es, b, h2T, h2T_key, attA_acc, a_sb, modT_sb, C,
                    pb, pb_key):
    hA01 = es.enter_context(nc.sbuf_tensor(f"mp_hA{b}", [128, 256], F32))
    tnh = es.enter_context(nc.sbuf_tensor(f"mp_t{b}", [128, 16], F32))
    mha = es.enter_context(nc.sbuf_tensor(f"mp_m{b}", [128, 16], BF16))
    K_HA, K_T, K_M = f"mp_hA{b}", f"mp_t{b}", f"mp_m{b}"
    for st in range(16):
        for hc in range(16):
            t.op("pe", (lambda e, st=st, hc=hc, s0=(hc == 0), s1=(hc == 15):
                        e.matmul(pb[:, 0:16],
                                 h2T[:, hc * 2048 + st * 128:
                                     hc * 2048 + (st + 1) * 128],
                                 a_sb[:, hc * 16:(hc + 1) * 16],
                                 start=s0, stop=s1)),
                 reads=[h2T_key, "c_a_sb"], writes=[pb_key],
                 tag=f"mphA{b}_{st}_{hc}")
        t.op("act", (lambda e, st=st: e.activation(
            hA01[:, st * 16:(st + 1) * 16], pb[:, 0:16], AF.Copy, bias=0.0,
            scale=0.1)),
             reads=[pb_key], writes=[K_HA], tag=f"mphAe{b}_{st}")
    for st in range(16):
        t.op("act", (lambda e, st=st: e.activation(
            tnh[:], attA_acc[:, st * 16:(st + 1) * 16], AF.Tanh, bias=0.0,
            scale=1.0)),
             reads=["attA_acc"], writes=[K_T], tag=f"mptn{b}_{st}")
        t.op("dve", (lambda e, st=st: e.tensor_tensor(
            mha[:], tnh[:], hA01[:, st * 16:(st + 1) * 16], ALU.mult)),
             reads=[K_T, K_HA], writes=[K_M], tag=f"mpm{b}_{st}")
        t.op("pe", (lambda e, st=st: e.matmul(
            pb[0:16, 16:144], mha[:], C["ident"][:], start=True, stop=True)),
             reads=[K_M, "c_ident"], writes=[pb_key], tag=f"mptr{b}_{st}")
        t.op("dve", (lambda e, st=st: e.tensor_copy(
            modT_sb[0:16, st * 128:(st + 1) * 128], pb[0:16, 16:144])),
             reads=[pb_key], writes=["modT"], tag=f"mptre{b}_{st}")


def emit_l1ffn(nc, t, es, b, xnT, xnT_key, attnT, W, C, inter_d,
               out_tile_cb, psum, psc, psc_keys, pmisc, pkey, do_woa):
    h1T_key = f"h1T{b}"
    es1 = ExitStack()
    h1T = es1.enter_context(nc.sbuf_tensor(f"h1T{b}", [128, 16 * 2048], BF16))
    es_rn = ExitStack()
    emit_rmsnorm_pair(nc, t, es_rn, b, xnT, xnT_key, C["ln1"], C["ln2"], h1T,
                      h1T_key, psc, psc_keys, pmisc["rope"], pkey("rope"), C)
    if do_woa:
        emit_woa(nc, t, es_rn, W["Wo"], W["A"], C["woa"], C["a_sb"],
                 pmisc["proj"], pkey("proj"))
    t.barrier(f"l1f_rn{b}")
    es_rn.close()
    attA_acc = es1.enter_context(nc.sbuf_tensor(f"attA{b}", [128, 256], F32))
    at_cache = {}
    for g0 in (0, 8):
        emit_attention(nc, t, es1, f"af{b}", b, h1T, h1T_key, 16, 16, False,
                       W["WqT"], W["WkT"], W["WvT"], attnT, f"afT{b}", C, psc,
                       psc_keys, pmisc, pkey, heads=list(range(g0, g0 + 8)),
                       cache=at_cache)
        emit_attnA_partial(nc, t, es1, b, attnT, f"afT{b}",
                           C["woa"][:, g0 * 16:(g0 + 8) * 16], attA_acc,
                           8, g0 == 0, pmisc["proj"], pkey("proj"))
    modT = C["modT"]
    emit_mod_finish(nc, t, es1, b, xnT, xnT_key, attA_acc, C["a_sb"], modT, C,
                    pmisc["proj"], pkey("proj"))
    t.barrier(f"l1f_at{b}")
    es1.close()

    es2 = ExitStack()
    bsl = es2.enter_context(nc.sbuf_tensor(f"bsl{b}", [16, 512], BF16))

    def extra_gate_mm(ic, it, sc, pg, pgkey):
        if it == 0 and sc == 0:
            ap = bass.AP(W["B"].tensor, ic * 512, [[8192, 16], [1, 512]])
            t.dma("pool", (lambda e, ap=ap: e.dma_start(bsl[:], ap)),
                  writes=["c_bsl"], tag=f"f1bld{ic}")
        t.op("pe", (lambda e, it=it, sc=sc, pg=pg: e.matmul(
            pg[:], bsl[0:16, it * 128:(it + 1) * 128],
            modT[0:16, sc * 512:(sc + 1) * 512], start=False, stop=True)),
             reads=["c_bsl", "modT"], writes=[pgkey],
             tag=f"f1bg{ic}_{it}_{sc}")
    wsrc = wsrc_direct(t, W["W_inT"], W["W_upT"], W["W_downT"])
    emit_glu(nc, t, es2, f"f1_{b}", b, xnT, xnT_key, wsrc, inter_d,
             out_tile_cb, psum, extra_gate_mm=extra_gate_mm, db_w=False)
    t.barrier(f"l1f_glu{b}")
    es2.close()


def emit_mamba(nc, t, es, b, xnT, xnT_key, W, C, S, out_tile_cb, psum8,
               dbg_cb=None, phase="both"):
    CH = 256
    if phase == "m1":
        _emit_mamba_m1(nc, t, es, b, xnT, xnT_key, W, C, S, psum8)
        return
    if phase == "rest":
        _emit_mamba_rest(nc, t, es, b, W, C, S, out_tile_cb, psum8, dbg_cb)
        return
    _emit_mamba_m1(nc, t, es, b, xnT, xnT_key, W, C, S, psum8)
    _emit_mamba_rest(nc, t, es, b, W, C, S, out_tile_cb, psum8, dbg_cb)


def _emit_mamba_m1(nc, t, es, b, xnT, xnT_key, W, C, S, psum8):
    es1 = ExitStack()
    wbuf = [es1.enter_context(nc.sbuf_tensor(f"ma_w{b}_{i}", [128, 2048], BF16))
            for i in range(2)]
    evb = [es1.enter_context(nc.sbuf_tensor(f"ma_ev{b}_{i}", [128, 512], BF16))
           for i in range(2)]
    for ct in range(35):
        ncols = 128 if ct < 34 else 64
        wb = wbuf[ct % 2]
        ap = bass.AP(W["W_inT"].tensor, 4096 + ct * 128,
                     [[8512, 128], [128 * 8512, 16], [1, ncols]])
        t.dma("pool", (lambda e, wb=wb, ap=ap, ncols=ncols:
                       e.dma_start(wb[:, 0:16 * ncols], ap)),
              writes=[("ma_w", b, ct % 2)], tag=f"ma1w{b}_{ct}")
        for sc in range(4):
            ps, psk = psum8[sc % 2]
            for hc in range(16):
                t.op("pe", (lambda e, wb=wb, hc=hc, sc=sc, ps=ps, ncols=ncols,
                            s0=(hc == 0), s1=(hc == 15):
                            e.matmul(ps[0:ncols, :],
                                     wb[:, hc * ncols:(hc + 1) * ncols],
                                     xnT[:, hc * 2048 + sc * 512:
                                         hc * 2048 + (sc + 1) * 512],
                                     start=s0, stop=s1)),
                     reads=[("ma_w", b, ct % 2), xnT_key], writes=[psk],
                     tag=f"ma1m{b}_{ct}_{sc}_{hc}")
            ev = evb[sc % 2]
            t.op("act", (lambda e, ev=ev, ps=ps, ncols=ncols: e.activation(
                ev[0:ncols, :], ps[0:ncols, :], AF.Copy, bias=0.0, scale=1.0)),
                 reads=[psk], writes=[("ma_ev", b, sc % 2)],
                 tag=f"ma1e{b}_{ct}_{sc}")
            dst = bass.AP(S["hbc"].tensor, ct * 128 * 2048 + sc * 512,
                          [[2048, ncols], [1, 512]])
            t.dma("pool", (lambda e, dst=dst, ev=ev, ncols=ncols:
                           e.dma_start(dst, ev[0:ncols, :])),
                  reads=[("ma_ev", b, sc % 2)], writes=[("hbc_raw", ct)],
                  tag=f"ma1s{b}_{ct}_{sc}")
    gwb = [es1.enter_context(nc.sbuf_tensor(f"ma_gw{b}_{i}", [128, 8192], BF16))
           for i in range(2)]
    for gc in range(8):
        wb = gwb[gc % 2]
        ap = bass.AP(W["W_inT"].tensor, gc * 512,
                     [[8512, 128], [128 * 8512, 16], [1, 512]])
        t.dma("pool", (lambda e, wb=wb, ap=ap: e.dma_start(wb[:], ap)),
              writes=[("ma_w", b, gc % 2)], tag=f"ma1gw{b}_{gc}")
        for st in range(16):
            ps, psk = psum8[st % 2]
            for hc in range(16):
                t.op("pe", (lambda e, wb=wb, hc=hc, st=st, ps=ps,
                            s0=(hc == 0), s1=(hc == 15):
                            e.matmul(ps[:],
                                     xnT[:, hc * 2048 + st * 128:
                                         hc * 2048 + (st + 1) * 128],
                                     wb[:, hc * 512:(hc + 1) * 512],
                                     start=s0, stop=s1)),
                     reads=[("ma_w", b, gc % 2), xnT_key], writes=[psk],
                     tag=f"ma1gm{b}_{st}_{gc}_{hc}")
            ev = evb[st % 2]
            t.op("act", (lambda e, ev=ev, ps=ps: e.activation(
                ev[:], ps[:], AF.Silu, bias=0.0, scale=1.0)),
                 reads=[psk], writes=[("ma_ev", b, st % 2)],
                 tag=f"ma1ge{b}_{st}_{gc}")
            dst = bass.AP(S["gate"].tensor, st * 128 * 4096 + gc * 512,
                          [[4096, 128], [1, 512]])
            t.dma("pool", (lambda e, dst=dst, ev=ev: e.dma_start(dst, ev[:])),
                  reads=[("ma_ev", b, st % 2)], writes=[("gate_s", st, gc)],
                  tag=f"ma1gs{b}_{st}_{gc}")
    t.barrier(f"ma1{b}")
    es1.close()


def _emit_mamba_rest(nc, t, es, b, W, C, S, out_tile_cb, psum8, dbg_cb):
    CH = 256
    acol = es.enter_context(nc.sbuf_tensor(f"ma_acol{b}", [64, 1], F32))
    t.op("act", lambda e: e.activation(acol[:], C["alog"][0:64, :], AF.Exp,
                                       bias=0.0, scale=1.0),
         reads=["c_alog"], writes=["ma_acol"], tag=f"ma_a1{b}")
    t.op("dve", lambda e: e.tensor_scalar(acol[:], acol[:], -1.0, None,
                                          ALU.mult),
         reads=["ma_acol"], writes=["ma_acol"], tag=f"ma_a2{b}")
    es2 = ExitStack()
    ub = [es2.enter_context(nc.sbuf_tensor(f"ma_u{b}_{i}", [128, 2048], BF16))
          for i in range(2)]
    ac = [es2.enter_context(nc.sbuf_tensor(f"ma_ac{b}_{i}", [128, 2048], F32))
          for i in range(2)]
    cv = [es2.enter_context(nc.sbuf_tensor(f"ma_cv{b}_{i}", [128, 2048], BF16))
          for i in range(2)]
    for ct in range(34):
        k2 = ct % 2
        u, a, c_ = ub[k2], ac[k2], cv[k2]
        src = bass.AP(S["hbc"].tensor, ct * 128 * 2048, [[2048, 128], [1, 2048]])
        t.dma("pool", (lambda e, u=u, src=src: e.dma_start(u[:], src)),
              reads=[("hbc_raw", ct)], writes=[("ma_u", k2)], tag=f"ma2l{b}_{ct}")
        t.op("dve", (lambda e, u=u, a=a, ct=ct: e.tensor_scalar(
            a[:], u[:], C["wcv"][:, ct * 4 + 3:ct * 4 + 4], None, ALU.mult)),
             reads=[("ma_u", k2), "c_wcv"], writes=[("ma_ac", k2)],
             tag=f"ma2k3{b}_{ct}")
        for k in range(3):
            d = 3 - k
            t.op("dve", (lambda e, u=u, a=a, ct=ct, k=k, d=d:
                         e.scalar_tensor_tensor(
                             a[:, d:2048], u[:, 0:2048 - d],
                             C["wcv"][:, ct * 4 + k:ct * 4 + k + 1],
                             a[:, d:2048], ALU.mult, ALU.add)),
                 reads=[("ma_u", k2), "c_wcv", ("ma_ac", k2)],
                 writes=[("ma_ac", k2)], tag=f"ma2k{k}{b}_{ct}")
        t.op("act", (lambda e, a=a, c_=c_, ct=ct: e.activation(
            c_[:], a[:], AF.Silu, bias=C["cbc"][:, ct:ct + 1], scale=1.0)),
             reads=[("ma_ac", k2), "c_cbc"], writes=[("ma_cv", k2)],
             tag=f"ma2s{b}_{ct}")
        dst = bass.AP(S["hbc"].tensor, ct * 128 * 2048, [[2048, 128], [1, 2048]])
        t.dma("pool", (lambda e, dst=dst, c_=c_: e.dma_start(dst, c_[:])),
              reads=[("ma_cv", k2)], writes=[("hbc_cv", ct), ("hbc_raw", ct)],
              tag=f"ma2st{b}_{ct}")
    t.barrier(f"ma2{b}")
    es2.close()

    es3 = ExitStack()
    sb_ = lambda n, sh, dt: es3.enter_context(nc.sbuf_tensor(n + str(b), sh, dt))
    xs_cm = sb_("ma_xs", [128, 8192], BF16)
    x_tm = sb_("ma_xtm", [128, 8192], BF16)
    xdt = sb_("ma_xdt", [128, 8192], BF16)
    xdte = xs_cm
    ybuf = sb_("ma_y", [128, 8192], BF16)
    gsb = None
    bc_sb = sb_("ma_bc", [128, 512], BF16)
    b_tm = sb_("ma_btm", [128, 256], BF16)
    cbt = sb_("ma_cbt", [128, 512], BF16)
    mbuf = [sb_(f"ma_mT{i}", [128, 512], BF16) for i in range(2)]
    argb2 = [sb_(f"ma_arg{i}", [128, 256], F32) for i in range(2)]
    dtraw = sb_("ma_dtr", [64, 256], BF16)
    f64a = sb_("ma_f64a", [64, 256], F32)
    f64b = sb_("ma_f64b", [64, 256], F32)
    dt_cm = sb_("ma_dtc", [64, 256], F32)
    acum = sb_("ma_acu", [64, 256], F32)
    edec_cm = sb_("ma_edc", [64, 256], F32)
    acum_tm = sb_("ma_atm", [128, 128], F32)
    negacum_tm = sb_("ma_natm", [128, 128], F32)
    eacum_tm = sb_("ma_eatm", [128, 128], F32)
    dt_tm = sb_("ma_dttm", [128, 128], F32)
    edec_tm = sb_("ma_edtm", [128, 128], F32)
    cdb = sb_("ma_cdb", [128, 64], F32)
    cdrow = sb_("ma_cdr", [1, 64], F32)
    carry = sb_("ma_carry", [128, 4096], BF16)
    ytmp = sb_("ma_yt1", [128, 512], F32)
    ytmp2 = sb_("ma_yt2", [128, 512], F32)
    trev = sb_("ma_trev", [128, 512], BF16)
    ycol = sb_("ma_ycol", [128, 2], F32)
    KX, KXT, KDT, KDE, KY, KG = [("ma", b, i) for i in range(6)]
    KBC, KBT, KCBT, KAC, KTM, KCD, KCAR = [("ma", b, 6 + i) for i in range(7)]
    KF = ("ma", b, 13)
    pbc, pbc_k = psum8[0]
    pY = [psum8[1], psum8[2], psum8[3], psum8[4]]
    pst, pst_k = psum8[5]
    pcb, pcb_k = psum8[6]
    ptr, ptr_k = psum8[7]
    NCH = SEQ // CH
    for c8 in range(NCH):
        ap = bass.AP(S["hbc"].tensor, c8 * 256,
                     [[2048, 128], [128 * 2048, 32], [1, 256]])
        t.dma("pool", (lambda e, ap=ap: e.dma_start(xs_cm[:], ap)),
              reads=[("hbc_cv", ct) for ct in range(32)], writes=[KX, KDE],
              tag=f"m3lx{b}_{c8}")
        for i, r0 in ((0, 4096), (1, 4224)):
            ap = bass.AP(S["hbc"].tensor, r0 * 2048 + c8 * 256,
                         [[2048, 128], [1, 256]])
            t.dma("pool", (lambda e, ap=ap, i=i: e.dma_start(
                bc_sb[:, i * 256:(i + 1) * 256], ap)),
                  reads=[("hbc_cv", 32 + i)], writes=[KBC], tag=f"m3lbc{b}_{c8}_{i}")
        ap = bass.AP(S["hbc"].tensor, 4352 * 2048 + c8 * 256,
                     [[2048, 64], [1, 256]])
        t.dma("pool", (lambda e, ap=ap: e.dma_start(dtraw[:], ap)),
              reads=[("hbc_raw", 34)], writes=[("ma", b, "dtr")], tag=f"m3ldt{b}_{c8}")
        t.op("dve", lambda e: e.tensor_scalar(f64a[:], dtraw[:], C["dtb"][0:64, :],
                                              None, ALU.add),
             reads=[("ma", b, "dtr"), "c_dtb"], writes=[KF], tag=f"m3z{b}_{c8}")
        t.op("act", lambda e: e.activation(f64b[:], f64a[:], AF.Abs, bias=0.0,
                                           scale=1.0),
             reads=[KF], writes=[("ma", b, 14)], tag=f"m3ab{b}_{c8}")
        t.op("act", lambda e: e.activation(f64b[:], f64b[:], AF.Exp, bias=0.0,
                                           scale=-1.0),
             reads=[("ma", b, 14)], writes=[("ma", b, 14)], tag=f"m3ex{b}_{c8}")
        t.op("act", lambda e: e.activation(f64b[:], f64b[:], AF.Ln,
                                           bias=C["fc"][0:64, 2:3], scale=1.0),
             reads=[("ma", b, 14), "c_fc"], writes=[("ma", b, 14)], tag=f"m3ln{b}_{c8}")
        t.op("act", lambda e: e.activation(f64a[:], f64a[:], AF.Relu, bias=0.0,
                                           scale=1.0),
             reads=[KF], writes=[KF], tag=f"m3rl{b}_{c8}")
        t.op("dve", lambda e: e.tensor_tensor(dt_cm[:], f64a[:], f64b[:], ALU.add),
             reads=[KF, ("ma", b, 14)], writes=[("ma", b, "dtc")], tag=f"m3dt{b}_{c8}")
        t.op("dve", lambda e: e.tensor_scalar(f64a[:], dt_cm[:], acol[0:64, :],
                                              None, ALU.mult),
             reads=[("ma", b, "dtc"), "ma_acol"], writes=[KF], tag=f"m3da{b}_{c8}")
        t.op("dve", lambda e: e.tensor_tensor_scan(acum[:], C["ones256"][0:64, :],
                                                   f64a[:], 0.0, ALU.mult, ALU.add),
             reads=[KF, "c_ones256"], writes=[KAC], tag=f"m3sc{b}_{c8}",
             weak=True, weak_read=True)
        t.op("act", lambda e: e.activation(edec_cm[:], acum[:], AF.Exp,
                                           bias=acum[:, 255:256], scale=-1.0),
             reads=[KAC], writes=[("ma", b, "edc")], tag=f"m3ed{b}_{c8}")
        t.op("act", lambda e: e.activation(f64b[0:64, 0:1], acum[:, 255:256],
                                           AF.Exp, bias=0.0, scale=1.0),
             reads=[KAC], writes=[("ma", b, 14)], tag=f"m3cd{b}_{c8}")
        t.op("pe", lambda e: e.matmul(ptr[0:1, 0:64], f64b[0:64, 0:1],
                                      C["identf"][0:64, 0:64], start=True, stop=True),
             reads=[("ma", b, 14), "c_identf"], writes=[ptr_k], tag=f"m3ct{b}_{c8}")
        t.op("dve", lambda e: e.tensor_copy(cdrow[:], ptr[0:1, 0:64]),
             reads=[ptr_k], writes=[("ma", b, "cdr")], tag=f"m3ce{b}_{c8}")
        t.op("pe", lambda e: e.matmul(ptr[:, 64:128], C["ones1f"][:],
                                      cdrow[:], start=True, stop=True),
             reads=[("ma", b, "cdr"), "c_ones1f"], writes=[ptr_k], tag=f"m3cb{b}_{c8}")
        t.op("dve", lambda e: e.tensor_copy(cdb[:], ptr[:, 64:128]),
             reads=[ptr_k], writes=[KCD], tag=f"m3cbe{b}_{c8}")
        for st in range(2):
            t.op("pe", (lambda e, st=st: e.matmul(
                ptr[:, 128:192], acum[0:64, st * 128:(st + 1) * 128],
                C["identf"][0:64, 0:64], start=True, stop=True)),
                 reads=[KAC, "c_identf"], writes=[ptr_k], tag=f"m3ta{b}_{c8}_{st}")
            t.op("act", (lambda e, st=st: e.activation(
                acum_tm[:, st * 64:(st + 1) * 64], ptr[:, 128:192], AF.Copy,
                bias=0.0, scale=1.0)),
                 reads=[ptr_k], writes=[KTM], tag=f"m3tae{b}_{c8}_{st}")
            t.op("act", (lambda e, st=st: e.activation(
                negacum_tm[:, st * 64:(st + 1) * 64], ptr[:, 128:192], AF.Copy,
                bias=0.0, scale=-1.0)),
                 reads=[ptr_k], writes=[KTM], tag=f"m3tne{b}_{c8}_{st}")
            t.op("act", (lambda e, st=st: e.activation(
                eacum_tm[:, st * 64:(st + 1) * 64], ptr[:, 128:192], AF.Exp,
                bias=0.0, scale=1.0)),
                 reads=[ptr_k], writes=[KTM], tag=f"m3tee{b}_{c8}_{st}")
            t.op("pe", (lambda e, st=st: e.matmul(
                ptr[:, 192:256], dt_cm[0:64, st * 128:(st + 1) * 128],
                C["identf"][0:64, 0:64], start=True, stop=True)),
                 reads=[("ma", b, "dtc"), "c_identf"], writes=[ptr_k],
                 tag=f"m3td{b}_{c8}_{st}")
            t.op("act", (lambda e, st=st: e.activation(
                dt_tm[:, st * 64:(st + 1) * 64], ptr[:, 192:256], AF.Copy,
                bias=0.0, scale=1.0)),
                 reads=[ptr_k], writes=[KTM], tag=f"m3tde{b}_{c8}_{st}")
            t.op("pe", (lambda e, st=st: e.matmul(
                ptr[:, 256:320], edec_cm[0:64, st * 128:(st + 1) * 128],
                C["identf"][0:64, 0:64], start=True, stop=True)),
                 reads=[("ma", b, "edc"), "c_identf"], writes=[ptr_k],
                 tag=f"m3te{b}_{c8}_{st}")
            t.op("act", (lambda e, st=st: e.activation(
                edec_tm[:, st * 64:(st + 1) * 64], ptr[:, 256:320], AF.Copy,
                bias=0.0, scale=1.0)),
                 reads=[ptr_k], writes=[KTM], tag=f"m3tede{b}_{c8}_{st}")
        for st in range(2):
            t.op("pe", (lambda e, st=st: e.matmul(
                pcb[:, 0:256], bc_sb[:, st * 128:(st + 1) * 128],
                bc_sb[:, 256:512], start=True, stop=True)),
                 reads=[KBC], writes=[pcb_k], tag=f"m3cbt{b}_{c8}_{st}")
            t.op("dve", (lambda e, st=st: e.tensor_tensor(
                cbt[:, st * 256:(st + 1) * 256], pcb[:, 0:256],
                C["ssdm"][:, st * 256:(st + 1) * 256], ALU.mult)),
                 reads=[pcb_k, "c_ssdm"], writes=[KCBT], tag=f"m3cbm{b}_{c8}_{st}")
            t.op("pe", (lambda e, st=st: e.matmul(
                pcb[:, 256:384], bc_sb[:, st * 128:(st + 1) * 128],
                C["ident"][:], start=True, stop=True)),
                 reads=[KBC, "c_ident"], writes=[pcb_k], tag=f"m3bt{b}_{c8}_{st}")
            t.op("dve", (lambda e, st=st: e.tensor_copy(
                b_tm[:, st * 128:(st + 1) * 128], pcb[:, 256:384])),
                 reads=[pcb_k], writes=[KBT], tag=f"m3bte{b}_{c8}_{st}")
        for tt4 in range(16):
            for j in range(4):
                tt = tt4 * 4 + j
                ctl, st = tt // 2, tt % 2
                t.op("pe", (lambda e, ctl=ctl, st=st, j=j: e.matmul(
                    ptr[:, 128 * j:128 * (j + 1)],
                    xs_cm[:, ctl * 256 + st * 128: ctl * 256 + st * 128 + 128],
                    C["ident"][:], start=True, stop=True)),
                     reads=[KX, "c_ident"], writes=[ptr_k],
                     tag=f"m3xt{b}_{c8}_{tt}")
            t.op("dve", (lambda e: e.tensor_copy(trev[:], ptr[:, 0:512])),
                 reads=[ptr_k], writes=[("ma", b, "trev")],
                 tag=f"m3xte{b}_{c8}_{tt4}")
            for j in range(4):
                tt = tt4 * 4 + j
                ctl, st = tt // 2, tt % 2
                t.op("dve", (lambda e, st=st, ctl=ctl, j=j: e.tensor_copy(
                    x_tm[:, st * 4096 + ctl * 128: st * 4096 + (ctl + 1) * 128],
                    trev[:, j * 128:(j + 1) * 128])),
                     reads=[("ma", b, "trev")], writes=[KXT],
                     tag=f"m3xtc{b}_{c8}_{tt}")
        for st in range(2):
            dtexp = bass.AP(dt_tm, st * 64, [[128, 128], [1, 64], [0, 64]])
            t.op("dve", (lambda e, st=st, dtexp=dtexp: e.scalar_tensor_tensor(
                xdt[:, st * 4096:(st + 1) * 4096],
                x_tm[:, st * 4096:(st + 1) * 4096], 1.0, dtexp,
                ALU.mult, ALU.mult)),
                 reads=[KXT, KTM], writes=[KDT], tag=f"m3xd{b}_{c8}_{st}")
            edexp = bass.AP(edec_tm, st * 64, [[128, 128], [1, 64], [0, 64]])
            t.op("dve", (lambda e, st=st, edexp=edexp: e.scalar_tensor_tensor(
                xdte[:, st * 4096:(st + 1) * 4096],
                xdt[:, st * 4096:(st + 1) * 4096], 1.0, edexp,
                ALU.mult, ALU.mult)),
                 reads=[KDT, KTM], writes=[KDE, KX], tag=f"m3xe{b}_{c8}_{st}")
        for g in range(8):
            for hl in range(8):
                h = g * 8 + hl
                mb = mbuf[hl % 2]
                argb = argb2[hl % 2]
                pbank = pbc if hl % 2 == 0 else pcb
                pbank_k = pbc_k if hl % 2 == 0 else pcb_k
                t.op("pe", (lambda e, h=h, pbank=pbank: e.matmul(
                    pbank[:, 0:256], C["selb"][0:64, h * 128:(h + 1) * 128],
                    acum[:], start=True, stop=True)),
                     reads=["c_selb", KAC], writes=[pbank_k],
                     tag=f"m3bc{b}_{c8}_{h}")
                for st in range(2):
                    t.op("dve", (lambda e, st=st, h=h, argb=argb, pbank=pbank:
                                 e.tensor_scalar(
                        argb[:], pbank[:, 0:256],
                        negacum_tm[:, st * 64 + h:st * 64 + h + 1], 0.0,
                        ALU.add, ALU.min)),
                         reads=[pbank_k, KTM], writes=[("ma", b, "arg", hl % 2)],
                         tag=f"m3ma{b}_{c8}_{h}_{st}")
                    t.op("act", (lambda e, st=st, mb=mb, argb=argb: e.activation(
                        mb[:, st * 256:(st + 1) * 256], argb[:], AF.Exp,
                        bias=0.0, scale=1.0)),
                         reads=[("ma", b, "arg", hl % 2)],
                         writes=[("ma", b, "mT", hl % 2)],
                         tag=f"m3me{b}_{c8}_{h}_{st}")
                    t.op("dve", (lambda e, st=st, mb=mb: e.tensor_tensor(
                        mb[:, st * 256:(st + 1) * 256],
                        mb[:, st * 256:(st + 1) * 256],
                        cbt[:, st * 256:(st + 1) * 256], ALU.mult)),
                         reads=[("ma", b, "mT", hl % 2), KCBT],
                         writes=[("ma", b, "mT", hl % 2)],
                         tag=f"m3mm{b}_{c8}_{h}_{st}")
                for st in range(2):
                    pY1, pY1k = pY[st * 2]
                    for s2 in range(st + 1):
                        t.op("pe", (lambda e, st=st, s2=s2, hl=hl, mb=mb, h=h,
                                    pY1=pY1, s0=(s2 == 0), s1=(s2 == st):
                                    e.matmul(
                            pY1[:, hl * 64:(hl + 1) * 64],
                            mb[:, s2 * 256 + st * 128: s2 * 256 + st * 128 + 128],
                            xdt[:, s2 * 4096 + h * 64: s2 * 4096 + (h + 1) * 64],
                            start=s0, stop=s1)),
                             reads=[("ma", b, "mT", hl % 2), KDT],
                             writes=[pY1k], tag=f"m3y1{b}_{c8}_{h}_{st}_{s2}")
                    if c8 > 0:
                        pY2, pY2k = pY[st * 2 + 1]
                        t.op("pe", (lambda e, st=st, hl=hl, h=h, pY2=pY2:
                                    e.matmul(
                            pY2[:, hl * 64:(hl + 1) * 64],
                            bc_sb[:, 256 + st * 128:256 + st * 128 + 128],
                            carry[:, h * 64:(h + 1) * 64],
                            start=True, stop=True)),
                             reads=[KBC, ("carry", g)], writes=[pY2k],
                             tag=f"m3y2{b}_{c8}_{h}_{st}")
                for st in range(2):
                    t.op("pe", (lambda e, st=st, hl=hl, h=h,
                                s0=(st == 0), s1=(st == 1): e.matmul(
                        pst[:, hl * 64:(hl + 1) * 64],
                        b_tm[:, st * 128:(st + 1) * 128],
                        xdte[:, st * 4096 + h * 64: st * 4096 + (h + 1) * 64],
                        start=s0, stop=s1)),
                         reads=[KBT, KDE], writes=[pst_k],
                         tag=f"m3st{b}_{c8}_{h}_{st}")
            for st in range(2):
                pY1, pY1k = pY[st * 2]
                pY2, pY2k = pY[st * 2 + 1]
                if c8 > 0:
                    eaexp = bass.AP(eacum_tm, st * 64 + g * 8,
                                    [[128, 128], [1, 8], [0, 64]])
                    t.op("dve", (lambda e, st=st, eaexp=eaexp, pY2=pY2:
                                 e.scalar_tensor_tensor(
                                     ytmp[:], pY2[:, 0:512], 1.0, eaexp,
                                     ALU.mult, ALU.mult)),
                         reads=[pY2k, KTM], writes=[("ma", b, "yt1")],
                         tag=f"m3e1{b}_{c8}_{g}_{st}")
                    t.op("dve", (lambda e, st=st, pY1=pY1: e.tensor_tensor(
                        ytmp[:], ytmp[:], pY1[:, 0:512], ALU.add)),
                         reads=[("ma", b, "yt1"), pY1k], writes=[("ma", b, "yt1")],
                         tag=f"m3e2{b}_{c8}_{g}_{st}")
                else:
                    t.op("dve", (lambda e, st=st, pY1=pY1: e.tensor_copy(
                        ytmp[:], pY1[:, 0:512])),
                         reads=[pY1k], writes=[("ma", b, "yt1")],
                         tag=f"m3e2z{b}_{c8}_{g}_{st}")
                dexp = bass.AP(C["Db"], g * 8, [[64, 128], [1, 8], [0, 64]])
                t.op("dve", (lambda e, st=st, g=g, dexp=dexp:
                             e.scalar_tensor_tensor(
                                 ytmp2[:], x_tm[:, st * 4096 + g * 512:
                                                st * 4096 + (g + 1) * 512],
                                 1.0, dexp, ALU.mult, ALU.mult)),
                     reads=[KXT, "c_Db"], writes=[("ma", b, "yt2")],
                     tag=f"m3e3{b}_{c8}_{g}_{st}")
                t.op("dve", (lambda e, st=st, g=g: e.tensor_tensor(
                    ybuf[:, st * 4096 + g * 512: st * 4096 + (g + 1) * 512],
                    ytmp[:], ytmp2[:], ALU.add)),
                     reads=[("ma", b, "yt1"), ("ma", b, "yt2")], writes=[KY],
                     tag=f"m3e4{b}_{c8}_{g}_{st}")
            if c8 == 0:
                t.op("dve", (lambda e, g=g: e.tensor_copy(
                    carry[:, g * 512:(g + 1) * 512], pst[:, 0:512])),
                     reads=[pst_k], writes=[("carry", g)], tag=f"m3cu0{b}_{g}")
            else:
                cdexp = bass.AP(cdb, g * 8, [[64, 128], [1, 8], [0, 64]])
                t.op("dve", (lambda e, g=g, cdexp=cdexp:
                             e.scalar_tensor_tensor(
                                 ytmp[:], carry[:, g * 512:(g + 1) * 512],
                                 1.0, cdexp, ALU.mult, ALU.mult)),
                     reads=[("carry", g), KCD], writes=[("ma", b, "yt1")],
                     tag=f"m3cu1{b}_{c8}_{g}")
                t.op("dve", (lambda e, g=g: e.tensor_tensor(
                    carry[:, g * 512:(g + 1) * 512], ytmp[:], pst[:, 0:512],
                    ALU.add)),
                     reads=[("ma", b, "yt1"), pst_k], writes=[("carry", g)],
                     tag=f"m3cu2{b}_{c8}_{g}")
        if dbg_cb is not None:
            dbg_cb(c8, dict(acum=acum, dt_cm=dt_cm, cbt=cbt, x_tm=x_tm,
                            xdt=xdt, ybuf=ybuf, carry=carry, b_tm=b_tm,
                            eacum_tm=eacum_tm, negacum_tm=negacum_tm,
                            edec_tm=edec_tm, dt_tm=dt_tm, cdb=cdb, mbuf=mbuf,
                            keys=dict(KY=KY, KAC=KAC, KTM=KTM, KCBT=KCBT,
                                      KXT=KXT, KDT=KDT, KCD=KCD)))
        gsb = xdt
        ap = bass.AP(S["gate"].tensor, c8 * 256 * 4096,
                     [[4096, 128], [128 * 4096, 2], [1, 4096]])
        t.dma("pool", (lambda e, ap=ap: e.dma_start(gsb[:], ap)),
              reads=[("gate_s", st2, gc) for st2 in (c8 * 2, c8 * 2 + 1)
                     for gc in range(8)],
              writes=[KG, KDT], tag=f"m3lg{b}_{c8}")
        for st in range(2):
            t.op("dve", (lambda e, st=st: e.tensor_tensor(
                ybuf[:, st * 4096:(st + 1) * 4096],
                ybuf[:, st * 4096:(st + 1) * 4096],
                gsb[:, st * 4096:(st + 1) * 4096], ALU.mult)),
                 reads=[KY, KG], writes=[KY], tag=f"m3gt{b}_{c8}_{st}")
            t.op("act", (lambda e, st=st: e.activation(
                gsb[:, st * 4096:(st + 1) * 4096],
                ybuf[:, st * 4096:(st + 1) * 4096], AF.Square,
                bias=0.0, scale=1.0, accum_out=ycol[:, 0:1])),
                 reads=[KY], writes=[KG, ("ma", b, "yc")],
                 tag=f"m3sq{b}_{c8}_{st}")
            t.op("act", (lambda e: e.activation(
                ycol[:, 1:2], ycol[:, 0:1], AF.Ln, bias=C["fc"][:, 1:2],
                scale=1.0 / 4096.0)),
                 reads=[("ma", b, "yc"), "c_fc"], writes=[("ma", b, "yc")],
                 tag=f"m3l{b}_{c8}_{st}")
            t.op("act", (lambda e: e.activation(
                ycol[:, 1:2], ycol[:, 1:2], AF.Exp, bias=0.0, scale=-0.5)),
                 reads=[("ma", b, "yc")], writes=[("ma", b, "yc")],
                 tag=f"m3r{b}_{c8}_{st}")
            t.op("dve", (lambda e, st=st: e.scalar_tensor_tensor(
                ybuf[:, st * 4096:(st + 1) * 4096],
                ybuf[:, st * 4096:(st + 1) * 4096], ycol[:, 1:2],
                C["nrmw"][:], ALU.mult, ALU.mult)),
                 reads=[KY, ("ma", b, "yc"), "c_nrmw"], writes=[KY],
                 tag=f"m3nw{b}_{c8}_{st}")
            for ct4 in range(8):
                for j in range(4):
                    ct = ct4 * 4 + j
                    t.op("pe", (lambda e, st=st, ct=ct, j=j: e.matmul(
                        ptr[:, 128 * j:128 * (j + 1)],
                        ybuf[:, st * 4096 + ct * 128: st * 4096 + (ct + 1) * 128],
                        C["ident"][:], start=True, stop=True)),
                         reads=[KY, "c_ident"], writes=[ptr_k],
                         tag=f"m3yt{b}_{c8}_{st}_{ct}")
                t.op("dve", (lambda e: e.tensor_copy(trev[:], ptr[:, 0:512])),
                     reads=[ptr_k], writes=[("ma", b, "trev")],
                     tag=f"m3yte{b}_{c8}_{st}_{ct4}")
                dst = bass.AP(S["yT"].tensor,
                              (ct4 * 4) * 128 * 2048 + c8 * 256 + st * 128,
                              [[2048, 128], [128 * 2048, 4], [1, 128]])
                t.dma("pool", (lambda e, dst=dst: e.dma_start(dst, trev[:])),
                      reads=[("ma", b, "trev")], writes=[("yT", c8, st, ct4)],
                      tag=f"m3ys{b}_{c8}_{st}_{ct4}")
    t.barrier(f"ma3{b}")
    es3.close()

    es4 = ExitStack()
    wod2 = [es4.enter_context(nc.sbuf_tensor(f"ma_wod{b}_{i}", [128, 32 * 512], BF16))
            for i in range(2)]
    itb = [es4.enter_context(nc.sbuf_tensor(f"ma_itb{b}_{i}", [128, 32 * 128], BF16))
           for i in range(2)]
    osb = [es4.enter_context(nc.sbuf_tensor(f"ma_osb{b}_{i}", [128, 512], F32))
           for i in range(2)]
    for dc in range(4):
        wod = wod2[dc % 2]
        for it in range(32):
            ap = bass.AP(W["W_outT"].tensor, it * 128 * 2048 + dc * 512,
                         [[2048, 128], [1, 512]])
            t.dma("pool", (lambda e, it=it, ap=ap, wod=wod: e.dma_start(
                wod[:, it * 512:(it + 1) * 512], ap)),
                  writes=[("ma_wod", dc % 2, it)], tag=f"m4w{b}_{dc}_{it}")
        for st in range(16):
            kb = st % 2
            ib = itb[kb]
            lap = bass.AP(S["yT"].tensor, st * 128,
                          [[2048, 128], [128 * 2048, 32], [1, 128]])
            t.dma("pool", (lambda e, ib=ib, lap=lap: e.dma_start(ib[:], lap)),
                  reads=[("yT", st // 2, st % 2, c4) for c4 in range(8)],
                  writes=[("ma_itb", kb)], tag=f"m4il{b}_{dc}_{st}")
            ps, psk = psum8[kb]
            for it in range(32):
                t.op("pe", (lambda e, ib=ib, it=it, ps=ps, wod=wod,
                            s0=(it == 0), s1=(it == 31): e.matmul(
                    ps[:], ib[:, it * 128:(it + 1) * 128],
                    wod[:, it * 512:(it + 1) * 512], start=s0, stop=s1)),
                     reads=[("ma_itb", kb), ("ma_wod", dc % 2, it)], writes=[psk],
                     tag=f"m4m{b}_{dc}_{st}_{it}")
            ob = osb[kb]
            t.op("dve", (lambda e, ob=ob, ps=ps: e.tensor_copy(ob[:], ps[:])),
                 reads=[psk], writes=[("ma_osb", kb)], tag=f"m4e{b}_{dc}_{st}")
            out_tile_cb(st, dc, ob, [("ma_osb", kb)])
    t.barrier(f"ma4{b}")
    es4.close()


def act_fence(nc, t, col_ap, key, tag):
    t.op("act", (lambda e: e.activation(col_ap, col_ap, AF.Copy, bias=0.0,
                                        scale=1.0)),
         reads=[key], writes=[key], tag=tag)


def emit_ln(nc, t, es, b, layer, x_d, x_rkey, store_d, acc_key, acc_xn,
            xnT, xnT_key, C, ptrp):
    ptr, ptr_k = ptrp
    xt = es.enter_context(nc.sbuf_tensor(f"ln_x{layer}{b}", [128, 2048], F32))
    xc = es.enter_context(nc.sbuf_tensor(f"ln_c{layer}{b}", [128, 2048], F32))
    xnb = es.enter_context(nc.sbuf_tensor(f"ln_n{layer}{b}", [128, 2048], BF16))
    sj = es.enter_context(nc.sbuf_tensor(f"ln_j{layer}{b}", [128, 2048], BF16))
    cols = es.enter_context(nc.sbuf_tensor(f"ln_co{layer}{b}", [128, 4], F32))
    KXT_, KC, KN, KJ, KCO = [(f"ln{layer}{b}", i) for i in range(5)]
    for st in range(16):
        src = bass.AP(x_d.tensor, (b * 2048 + st * 128) * 2048,
                      [[2048, 128], [1, 2048]])
        t.dma("pool", (lambda e, src=src: e.dma_start(xt[:], src)),
              reads=[x_rkey(st)], writes=[KXT_], tag=f"lnl{layer}{b}_{st}")
        dstx = bass.AP(store_d.tensor, (b * 2048 + st * 128) * 2048,
                       [[2048, 128], [1, 2048]])
        t.dma("pool", (lambda e, dstx=dstx: e.dma_start(dstx, xt[:])),
              reads=[KXT_], writes=[acc_key(st)], tag=f"lnsx{layer}{b}_{st}")
        t.op("act", (lambda e: e.activation(sj[:], xt[:], AF.Copy, bias=0.0,
                                            scale=1.0, accum_out=cols[:, 0:1])),
             reads=[KXT_], writes=[KJ, KCO], tag=f"lnsm{layer}{b}_{st}")
        t.op("act", (lambda e: e.activation(cols[:, 1:2], cols[:, 0:1],
                                            AF.Copy, bias=0.0,
                                            scale=-1.0 / 2048.0)),
             reads=[KCO], writes=[KCO], tag=f"lnmc{layer}{b}_{st}")
        t.op("dve", (lambda e: e.tensor_scalar(xc[:], xt[:], cols[:, 1:2],
                                               None, ALU.add)),
             reads=[KXT_, KCO], writes=[KC], tag=f"lncc{layer}{b}_{st}")
        t.op("act", (lambda e: e.activation(sj[:], xc[:], AF.Square, bias=0.0,
                                            scale=1.0, accum_out=cols[:, 2:3])),
             reads=[KC], writes=[KJ, KCO], tag=f"lnsq{layer}{b}_{st}")
        t.op("act", (lambda e: e.activation(cols[:, 3:4], cols[:, 2:3], AF.Ln,
                                            bias=C["fc"][:, 1:2],
                                            scale=1.0 / 2048.0)),
             reads=[KCO, "c_fc"], writes=[KCO], tag=f"lnln{layer}{b}_{st}")
        t.op("act", (lambda e: e.activation(cols[:, 3:4], cols[:, 3:4], AF.Exp,
                                            bias=0.0, scale=-0.5)),
             reads=[KCO], writes=[KCO], tag=f"lnex{layer}{b}_{st}")
        t.op("dve", (lambda e: e.tensor_scalar(xnb[:], xc[:], cols[:, 3:4],
                                               None, ALU.mult)),
             reads=[KC, KCO], writes=[KN], tag=f"lnxn{layer}{b}_{st}")
        if acc_xn:
            t.op("dve", (lambda e: e.tensor_scalar(xc[:], xc[:], cols[:, 3:4],
                                                   None, ALU.mult)),
                 reads=[KC, KCO], writes=[KC], tag=f"lnxf{layer}{b}_{st}")
            t.dma("pool", (lambda e, dstx=dstx: e.dma_start(
                dstx, xc[:], accum_op=ALU.add)),
                  reads=[KC, acc_key(st)], writes=[acc_key(st)],
                  tag=f"lnax{layer}{b}_{st}")
        for hc4 in range(4):
            for j in range(4):
                hc = hc4 * 4 + j
                t.op("pe", (lambda e, hc=hc, j=j: e.matmul(
                    ptr[:, j * 128:(j + 1) * 128],
                    xnb[:, hc * 128:(hc + 1) * 128], C["ident"][:],
                    start=True, stop=True)),
                     reads=[KN, "c_ident"], writes=[ptr_k],
                     tag=f"lnt{layer}{b}_{st}_{hc}")
            for j in range(4):
                hc = hc4 * 4 + j
                t.op("dve", (lambda e, hc=hc, st=st, j=j: e.tensor_copy(
                    xnT[:, hc * 2048 + st * 128:hc * 2048 + (st + 1) * 128],
                    ptr[:, j * 128:(j + 1) * 128])),
                     reads=[ptr_k], writes=[xnT_key],
                     tag=f"lnte{layer}{b}_{st}_{hc}")


CSPEC_G = [("ident", [128, 128], BF16), ("tril01", [128, 128], BF16),
           ("ones_col", [128, 1], BF16), ("ones1f", [1, 128], F32),
           ("fc", [128, 4], F32), ("ln1", [128, 16], F32),
           ("ln2", [128, 16], F32)]
CSPEC_A1 = [("pneg", [128, 128], BF16), ("cosT", [128, 2048], F32),
            ("sinT", [128, 2048], F32)]
CSPEC_MA = [("identf", [128, 128], F32), ("selb", [64, 8192], F32),
            ("ssdm", [128, 512], BF16), ("Db", [128, 64], F32),
            ("dtb", [64, 1], F32), ("alog", [64, 1], F32),
            ("wcv", [128, 136], F32), ("cbc", [128, 34], F32),
            ("ones256", [128, 256], F32), ("nrmw", [128, 4096], BF16)]

WSPEC = [("L0_W_inT", [2048, 8512]), ("L0_W_outT", [4096, 2048]),
         ("A1_WqT", [2048, 2048]), ("A1_WkT", [2048, 512]),
         ("A1_WvT", [2048, 512]), ("A1_WoT", [2048, 2048]),
         ("F1_WqT", [2048, 2048]), ("F1_WkT", [2048, 2048]),
         ("F1_WvT", [2048, 2048]), ("F1_Wo", [2048, 2048]),
         ("F1_A", [2048, 16]), ("F1_B", [16, 8192]),
         ("F1_W_inT", [2048, 8192]), ("F1_W_upT", [2048, 8192]),
         ("F1_W_downT", [8192, 2048])]


_LC_SEQ = [0]


def _load_consts(nc, t, es, cd, spec):
    C = {}
    for n, sh, dt in spec:
        _LC_SEQ[0] += 1
        C[n] = es.enter_context(
            nc.sbuf_tensor(f"c_{n}_{_LC_SEQ[0]}", sh, dt))
        t.dma("pool", (lambda e, n=n, C=C: e.dma_start(C[n][:], cd[n])),
              writes=["c_" + n], tag="ldc_" + n)
    return C


def build_full(n_b=NB, dump_acc0=False):
    nc = bass.Bass("TRN2", target_bir_lowering=False)
    t = Tin(nc)
    x_d = nc.dram_tensor("x", [n_b * 2048 * 2048], F32, kind="ExternalInput").ap()
    out_d = nc.dram_tensor("out", [n_b * 2048 * 2048], F32, kind="ExternalOutput").ap()
    gamma_d = nc.dram_tensor("gamma", [N_QT * 128 * 5 * 128], F32, kind="ExternalInput").ap()
    cpw_d = nc.dram_tensor("cpw", [N_QT * 128 * 15], F32, kind="ExternalInput").ap()
    cd = {}
    for n, sh, dt in CSPEC_G + CSPEC_A1 + CSPEC_MA:
        cd[n] = nc.dram_tensor("i_" + n, sh, dt, kind="ExternalInput").ap()
    Wd = {}
    for n, sh in WSPEC:
        Wd[n] = nc.dram_tensor(n, sh, F32, kind="ExternalInput").ap()
    S = {"hbc": nc.dram_tensor("s_hbc", [4416 * 2048], BF16, kind="Internal").ap(),
         "gate": nc.dram_tensor("s_gate", [2048 * 4096], BF16, kind="Internal").ap(),
         "yT": nc.dram_tensor("s_yT", [4096 * 2048], BF16, kind="Internal").ap(),
         "w3": nc.dram_tensor("s_w3", [3 * N_SAMP], BF16, kind="Internal").ap(),
         "int": nc.dram_tensor("s_int", [8192 * 2048], BF16, kind="Internal").ap(),
         "acc0": nc.dram_tensor("s_acc0", [n_b * 2048 * 2048], F32,
                                kind="ExternalOutput" if dump_acc0 else "Internal").ap()}

    es = ExitStack()
    psc = es.enter_context(nc.psum_tensor("psc", [128, 2048], F32))
    pm = [es.enter_context(nc.psum_tensor(f"pm{i}", [128, 512], F32)) for i in range(4)]
    psc_keys = [("psb", i) for i in range(4)]
    pmisc = {"proj": pm[0], "rope": pm[1], "pt": pm[2], "pv": pm[3],
             ("op", 0): pm[0], ("op", 1): pm[1]}
    pknames = {"proj": 0, "rope": 1, "pt": 2, "pv": 3, ("op", 0): 0, ("op", 1): 1}
    pkey = lambda n: ("ps", pknames[n])
    psum6 = [(psc[:, i * 512:(i + 1) * 512], ("psb", i)) for i in range(4)] +             [(pm[2], ("ps", 2)), (pm[3], ("ps", 3))]
    psum8 = [(psc[:, i * 512:(i + 1) * 512], ("psb", i)) for i in range(4)] +             [(pm[i], ("ps", i)) for i in range(4)]
    C = _load_consts(nc, t, es, cd, CSPEC_G)

    es_rc = ExitStack()
    mk_w3key = lambda w, g: ("rc_w3", w, g)
    emit_recon(nc, t, es_rc, gamma_d, cpw_d, S["w3"])
    wsrc = wsrc_spline(t, S["w3"], mk_w3key)
    W0 = {"W_inT": Wd["L0_W_inT"], "W_outT": Wd["L0_W_outT"]}

    for b in range(n_b):
        es_b = ExitStack()
        xnT = es_b.enter_context(nc.sbuf_tensor(f"xnT0_{b}", [128, 16 * 2048], BF16))
        es_ln = ExitStack()
        emit_ln(nc, t, es_ln, b, 0, x_d, lambda st: "x_in", S["acc0"],
                (lambda st, b=b: ("acc0", b, st)), False, xnT, "xnT", C,
                (pm[3], ("ps", 3)))
        t.barrier(f"ln0_{b}")
        es_ln.close()
        emit_mamba(nc, t, es_b, b, xnT, "xnT", W0, None, S, None, psum8,
                   phase="m1")
        emit_glu(nc, t, es_b, "f0", b, xnT, "xnT", wsrc, S["int"], None,
                 psum6, phase="gu")
        t.barrier(f"l0gu_{b}")
        es_b.close()
        es_r = ExitStack()
        Cm = _load_consts(nc, t, es_r, cd, CSPEC_MA)
        Cm["fc"] = C["fc"]
        Cm["ident"] = C["ident"]
        Cm["ones1f"] = C["ones1f"]

        def cb_mamba(st, dc, ob, reads, b=b):
            dst = bass.AP(S["acc0"].tensor, (b * 2048 + st * 128) * 2048 + dc * 512,
                          [[2048, 128], [1, 512]])
            t.dma("pool", (lambda e, dst=dst, ob=ob: e.dma_start(
                dst, ob[:], accum_op=ALU.add)),
                  reads=reads + [("acc0", b, st)], writes=[("acc0", b, st)],
                  tag=f"accm{b}_{st}_{dc}")
        emit_mamba(nc, t, es_r, b, None, None, W0, Cm, S, cb_mamba, psum8,
                   phase="rest")

        def cb_ffn0(st, dc, ob, reads, b=b):
            dst = bass.AP(S["acc0"].tensor, (b * 2048 + st * 128) * 2048 + dc * 256,
                          [[2048, 128], [1, 256]])
            t.dma("pool", (lambda e, dst=dst, ob=ob: e.dma_start(
                dst, ob[:], accum_op=ALU.add)),
                  reads=reads + [("acc0", b, st)], writes=[("acc0", b, st)],
                  tag=f"accf0{b}_{st}_{dc}")
        emit_glu(nc, t, es_r, "f0", b, None, None, wsrc, S["int"], cb_ffn0,
                 psum6, phase="down")
        t.barrier(f"l0_{b}")
        es_r.close()
    es_rc.close()

    for b in range(n_b):
        es_b = ExitStack()
        xnT = es_b.enter_context(nc.sbuf_tensor(f"xnT1_{b}", [128, 16 * 2048], BF16))
        attnT = es_b.enter_context(nc.sbuf_tensor(f"attnT_{b}", [128, 8 * 2048], BF16))
        es_ln = ExitStack()
        emit_ln(nc, t, es_ln, b, 1, S["acc0"], (lambda st, b=b: ("acc0", b, st)),
                out_d, (lambda st, b=b: ("out", b, st)), True, xnT, "xnT", C,
                (pm[3], ("ps", 3)))
        t.barrier(f"ln1_{b}")
        es_ln.close()
        es_a = ExitStack()
        Ca = _load_consts(nc, t, es_a, cd, CSPEC_A1)
        Ca.update(C)
        at_cache = {}
        opb = None
        for g0 in (0, 8):
            emit_attention(nc, t, es_a, f"a1_{b}", b, xnT, "xnT", 16, 4, True,
                           Wd["A1_WqT"], Wd["A1_WkT"], Wd["A1_WvT"], attnT,
                           "attnT", Ca, psc, psc_keys, pmisc, pkey,
                           heads=list(range(g0, g0 + 8)), cache=at_cache)

            def cb_at(st, dc, ob, reads, first, b=b):
                dst = bass.AP(out_d.tensor, (b * 2048 + st * 128) * 2048 + dc * 512,
                              [[2048, 128], [1, 512]])
                t.dma("pool", (lambda e, dst=dst, ob=ob: e.dma_start(
                    dst, ob[:], accum_op=ALU.add)),
                      reads=reads + [("out", b, st)], writes=[("out", b, st)],
                      tag=f"acca1{b}_{st}_{dc}_{first}")
            opb = emit_outproj(nc, t, es_a, f"a1_{b}", attnT, "attnT",
                               Wd["A1_WoT"], cb_at, pmisc, pkey,
                               heads=list(range(g0, g0 + 8)), bufs=opb)
        t.barrier(f"l1a_{b}")
        es_a.close()

        def cb_ffn1(st, dc, ob, reads, b=b):
            dst = bass.AP(out_d.tensor, (b * 2048 + st * 128) * 2048 + dc * 256,
                          [[2048, 128], [1, 256]])
            t.dma("pool", (lambda e, dst=dst, ob=ob: e.dma_start(
                dst, ob[:], accum_op=ALU.add)),
                  reads=reads + [("out", b, st)], writes=[("out", b, st)],
                  tag=f"accf1{b}_{st}_{dc}")
        W1 = {"WqT": Wd["F1_WqT"], "WkT": Wd["F1_WkT"], "WvT": Wd["F1_WvT"],
              "Wo": Wd["F1_Wo"], "A": Wd["F1_A"], "B": Wd["F1_B"],
              "W_inT": Wd["F1_W_inT"], "W_upT": Wd["F1_W_upT"],
              "W_downT": Wd["F1_W_downT"]}
        es_f = ExitStack()
        Cf = dict(C)
        Cf["woa"] = es_b.enter_context(nc.sbuf_tensor(f"c_woa{b}", [128, 256], BF16))
        Cf["a_sb"] = es_b.enter_context(nc.sbuf_tensor(f"c_a_sb{b}", [128, 256], BF16))
        Cf["modT"] = es_b.enter_context(nc.sbuf_tensor(f"c_modT{b}", [16, 2048], BF16))
        emit_l1ffn(nc, t, es_f, b, xnT, "xnT", attnT, W1, Cf, S["int"], cb_ffn1,
                   psum6, psc, psc_keys, pmisc, pkey, do_woa=True)
        t.barrier(f"l1f_{b}")
        es_f.close()
        es_b.close()

    t.finalize_on("sp")
    t.emit(verbose=True)
    es.close()
    return nc


_SPLINE_CACHE = {}


def host_prep(inputs):
    import ml_dtypes
    np_ = np
    if "gamma" not in _SPLINE_CACHE:
        g, bq = spline_tables()
        _SPLINE_CACHE["gamma"] = np_.ascontiguousarray(g.reshape(-1))
        _SPLINE_CACHE["bq"] = bq
    bq = _SPLINE_CACHE["bq"]
    cpw = np_.stack([spline_cpw(np_.asarray(inputs[k]), bq) for k in
                     ("L0_ffn_gate_cp", "L0_ffn_up_cp", "L0_ffn_down_cp")],
                    axis=-1).transpose(0, 2, 1).reshape(-1)
    TT = lambda a: np_.ascontiguousarray(np_.asarray(a).T)
    ins = {"gamma": _SPLINE_CACHE["gamma"], "cpw": np_.ascontiguousarray(cpw)}
    ins["L0_W_inT"] = TT(inputs["L0_attn_W_in"])
    ins["L0_W_outT"] = TT(inputs["L0_attn_W_out"])
    ins["A1_WqT"] = TT(inputs["L1_attn_Wq"])
    ins["A1_WkT"] = TT(inputs["L1_attn_Wk"])
    ins["A1_WvT"] = TT(inputs["L1_attn_Wv"])
    ins["A1_WoT"] = TT(inputs["L1_attn_Wo"])
    ins["F1_WqT"] = TT(inputs["L1_ffn_Wq"])
    ins["F1_WkT"] = TT(inputs["L1_ffn_Wk"])
    ins["F1_WvT"] = TT(inputs["L1_ffn_Wv"])
    ins["F1_Wo"] = np_.ascontiguousarray(np_.asarray(inputs["L1_ffn_Wo"]))
    ins["F1_A"] = np_.ascontiguousarray(np_.asarray(inputs["L1_ffn_A"]))
    ins["F1_B"] = np_.ascontiguousarray(np_.asarray(inputs["L1_ffn_B"]))
    ins["F1_W_inT"] = TT(inputs["L1_ffn_W_in_base"])
    ins["F1_W_upT"] = TT(inputs["L1_ffn_W_up_base"])
    ins["F1_W_downT"] = TT(inputs["L1_ffn_W_down"])
    bf = ml_dtypes.bfloat16
    ins["i_ident"] = np_.eye(128).astype(bf)
    ins["i_identf"] = np_.eye(128).astype(np_.float32)
    ins["i_tril01"] = np_.tril(np_.ones((128, 128))).astype(bf)
    pneg = np_.zeros((128, 128), np_.float32)
    for dp in range(128):
        if dp % 2 == 0:
            pneg[dp + 1, dp] = -1.0
        else:
            pneg[dp - 1, dp] = 1.0
    ins["i_pneg"] = pneg.astype(bf)
    freqs = 1.0 / (10000.0 ** (np_.arange(0, 128, 2, dtype=np_.float64) / 128))
    ang = np_.arange(2048, dtype=np_.float64)[:, None] * freqs[None, :]
    ins["i_cosT"] = np_.cos(ang).T.repeat(2, axis=0).astype(np_.float32)
    ins["i_sinT"] = np_.sin(ang).T.repeat(2, axis=0).astype(np_.float32)
    ins["i_ones_col"] = np_.ones((128, 1), bf)
    ins["i_ones1f"] = np_.ones((1, 128), np_.float32)
    ins["i_fc"] = np_.broadcast_to(
        np_.array([1e-6, 1e-5, 1.0, 0.0], np_.float32), (128, 4)).copy()
    selb = np_.zeros((64, 64, 128), np_.float32)
    for h in range(64):
        selb[h, h, :] = 1.0
    ins["i_selb"] = np_.ascontiguousarray(selb.transpose(1, 0, 2).reshape(64, 8192))
    m0 = np_.zeros((128, 256), np_.float32)
    m1 = np_.zeros((128, 256), np_.float32)
    for s in range(128):
        m0[s, s:] = 1.0
        m1[s, 128 + s:] = 1.0
    ins["i_ssdm"] = np_.concatenate([m0, m1], 1).astype(bf)
    ins["i_Db"] = np_.broadcast_to(np_.asarray(inputs["L0_attn_D"]), (128, 64)).copy()
    ins["i_dtb"] = np_.asarray(inputs["L0_attn_dt_bias"]).reshape(64, 1).copy()
    ins["i_alog"] = np_.asarray(inputs["L0_attn_A_log"]).reshape(64, 1).copy()
    cw = np_.asarray(inputs["L0_attn_conv_w"])
    ins["i_wcv"] = np_.ascontiguousarray(
        cw.reshape(34, 128, 4).transpose(1, 0, 2).reshape(128, 136))
    ins["i_cbc"] = np_.ascontiguousarray(
        np_.asarray(inputs["L0_attn_conv_b"]).reshape(34, 128).T)
    ins["i_ones256"] = np_.ones((128, 256), np_.float32)
    ins["i_nrmw"] = np_.broadcast_to(np_.asarray(inputs["L0_attn_norm_w"]),
                                     (128, 4096)).astype(bf)
    ins["i_ln1"] = np_.ascontiguousarray(
        np_.asarray(inputs["L1_ffn_ln1_w"]).reshape(16, 128).T)
    ins["i_ln2"] = np_.ascontiguousarray(
        np_.asarray(inputs["L1_ffn_ln2_w"]).reshape(16, 128).T)
    return ins


from concourse.bass_utils import run_bass_kernel_spmd

_NC_CACHE = {}


def kernel(**inputs):
    inputs = {k: np.asarray(v) for k, v in inputs.items()}
    common = host_prep(inputs)
    x = np.ascontiguousarray(np.asarray(inputs["L0_attn_x"], np.float32))
    n_cores = 8
    per = x.shape[0] // n_cores
    if "nc" not in _NC_CACHE:
        _NC_CACHE["nc"] = build_full(per)
    nc = _NC_CACHE["nc"]
    in_maps = []
    for c in range(n_cores):
        m = dict(common)
        m["x"] = np.ascontiguousarray(x[c * per:(c + 1) * per].reshape(-1))
        in_maps.append(m)
    res = run_bass_kernel_spmd(nc, in_maps, core_ids=list(range(n_cores)))
    out = np.concatenate(
        [np.asarray(r["out"], np.float32).reshape(per, 2048, 2048)
         for r in res.results], axis=0)
    return out
